# version 12
# baseline (speedup 1.0000x reference)
import contextlib
import math

import numpy as np
import ml_dtypes

import concourse.bass as bass
import concourse.mybir as mybir
from concourse.bass_utils import run_bass_kernel_spmd

F32 = mybir.dt.float32
BF16 = mybir.dt.bfloat16
AF = mybir.ActivationFunctionType
ALU = mybir.AluOpType
AX = mybir.AxisListType

BF = ml_dtypes.bfloat16

T = 4096
DM = 2048
NFT = DM // 128
TC = 512
NCH = T // TC
LN_EPS = 1e-5


class Prog:

    ENGINES = ("sync", "scalar", "vector", "tensor", "gpsimd")

    def __init__(self, nc, ctx):
        self.nc = nc
        self.ctx = ctx
        self.ops = {e: [] for e in self.ENGINES}
        self.sems = {}
        self.ctr = {}
        self.alias = {}
        self.dram_map = {}
        self.io = {}

    def sem(self, name):
        if name in self.sems:
            return self.sems[name]
        self.sems[name] = self.ctx.enter_context(self.nc.semaphore(name))
        self.ctr[name] = 0
        return self.sems[name]

    def sb(self, name, shape, dtype):
        if name in self.alias:
            return self.alias[name]
        if not hasattr(self, "_sbufs"):
            self._sbufs = {}
        if name not in self._sbufs:
            self._sbufs[name] = self.ctx.enter_context(self.nc.sbuf_tensor(name, shape, dtype))
        return self._sbufs[name]

    def ps(self, name, shape, dtype=F32):
        if name in self.alias:
            return self.alias[name]
        if not hasattr(self, "_psums"):
            self._psums = {}
        if name not in self._psums:
            self._psums[name] = self.ctx.enter_context(self.nc.psum_tensor(name, shape, dtype))
        return self._psums[name]

    def dram(self, name, shape, dtype, kind):
        if name in self.dram_map:
            return self.dram_map[name]
        ap = self.nc.dram_tensor(name, shape, dtype, kind=kind).ap()
        self.dram_map[name] = ap
        return ap

    def barrier(self, tag=""):
        self.sem("bar")
        for eng in self.ENGINES:
            self.step(eng, lambda e: e.nop(), inc="bar")
        tgt = self.ctr["bar"]
        for eng in self.ENGINES:
            sh = self.sems["bar"]
            self.ops[eng].append(lambda e, sh=sh, v=tgt: e.wait_ge(sh, v))

    def step(self, eng, fns, waits=(), inc=None, inc_dma=False):
        if not isinstance(fns, (list, tuple)):
            fns = [fns]
        for s, v in waits:
            sh = self.sems[s]
            self.ops[eng].append(lambda e, sh=sh, v=v: e.wait_ge(sh, v))
        tgt = None
        if inc is not None:
            sh = self.sems[inc]
            self.ctr[inc] += 16 if inc_dma else 1
            tgt = self.ctr[inc]
        for i, f in enumerate(fns):
            if inc is not None and i == len(fns) - 1:
                self.ops[eng].append(
                    lambda e, f=f, sh=sh, n=(16 if inc_dma else 1): f(e).then_inc(sh, n))
            else:
                self.ops[eng].append(f)
        return tgt

    def dma(self, eng, out, in_, sem="ld", waits=()):
        return self.step(eng, lambda e, o=out, i=in_: e.dma_start(out=o, in_=i),
                         waits=waits, inc=sem, inc_dma=True)

    def run(self):
        block = self.ctx.enter_context(self.nc.Block())
        for eng_name in self.ENGINES:
            opslist = self.ops[eng_name]
            if not opslist:
                continue

            def body(e, opslist=opslist):
                for fn in opslist:
                    fn(e)

            getattr(block, eng_name)(body)
        self.ctx.close()


def tile_img(WT):
    C, R = WT.shape
    a = C // 128
    rb = R // 128
    return np.ascontiguousarray(
        WT.reshape(a, 128, rb, 128).transpose(2, 1, 0, 3).reshape(rb, 128, C))


def host_consts():
    c = {}
    c["ones_st"] = np.ones((1, 128), np.float32)
    c["ones_st_bf"] = np.ones((1, 128), BF)
    c["ones_col"] = np.ones((128, 1), np.float32)
    c["ones_col_bf"] = np.ones((128, 1), BF)
    dh = 128
    inv = 1.0 / (10000.0 ** (np.arange(0, dh, 2, dtype=np.float64) / dh))
    freqs = np.arange(T, dtype=np.float64)[:, None] * inv[None, :]
    c["rope_cos"] = np.ascontiguousarray(np.repeat(np.cos(freqs), 2, axis=1).T).astype(np.float32)
    c["rope_sin"] = np.ascontiguousarray(np.repeat(np.sin(freqs), 2, axis=1).T).astype(np.float32)
    S = np.zeros((128, 128), np.float32)
    for i in range(64):
        S[2 * i + 1, 2 * i] = -1.0
        S[2 * i, 2 * i + 1] = 1.0
    c["rot_st"] = S
    NEG = -1e9
    ka = np.arange(128)[:, None]
    ro = np.arange(128)[None, :]
    c["m_low"] = np.where(ka <= ro, 0.0, NEG).astype(BF)
    c["m_up"] = np.where(ka > ro, 0.0, NEG).astype(BF)
    c["ident_bf"] = np.eye(128).astype(BF)
    sel = np.zeros((16, 16 * 128), np.float32)
    for si in range(16):
        sel[si, si * 128:(si + 1) * 128] = 1.0
    c["sel16"] = sel.astype(BF)
    c["fconst"] = np.array([[1.1920929e-07, 1.0 / 128, -0.5, 1e-5, -1.0, 1.0, 1.0 / 2048, 0.0]],
                           np.float32)
    return c


DEBUG_DUMP = False
HQ, HKV, DH, WIN = 16, 4, 128, 512
RMS_EPS = 1.1920929e-07


def emit_ln_chunk(p, xf, src_rows_ps, ps_bc0, ps_bc1, rowbuf, tmp1, tmp2,
                  out_cb, onesc, ones1, fconst, wt_after=(), tag=""):
    mms = []
    for ft in range(NFT):
        mms.append(lambda e, ft=ft: e.matmul(ps_bc0[0:1, :], onesc[:],
                                             xf[:, ft * TC:(ft + 1) * TC],
                                             start=(ft == 0), stop=(ft == NFT - 1)))
    t_mean = p.step("tensor", mms, waits=wt_after, inc="wt")
    sq_done = [None, None]
    for ft in range(NFT):
        buf = tmp1 if ft % 2 == 0 else tmp2
        w = list(wt_after) if ft == 0 else []
        if ft == 0 and p.ctr["st"] > 0:
            w.append(("st", p.ctr["st"]))
        if sq_done[ft % 2] is not None:
            w.append(("wt", sq_done[ft % 2]))
        s_tgt = p.step("scalar", lambda e, ft=ft, buf=buf: e.activation(
            out=buf[:], in_=xf[:, ft * TC:(ft + 1) * TC], func=AF.Square),
            waits=w, inc="ws")
        sq_done[ft % 2] = p.step("tensor", lambda e, buf=buf, ft=ft: e.matmul(
            ps_bc1[0:1, :], onesc[:], buf[:],
            start=(ft == 0), stop=(ft == NFT - 1)),
            waits=[("ws", s_tgt)], inc="wt")
    t_sq = sq_done[(NFT - 1) % 2]
    R = TC
    mean_r = rowbuf[0:1, 0:R]
    var_r = rowbuf[0:1, R:2 * R]
    rstd_r = rowbuf[0:1, 2 * R:3 * R]
    v1 = p.step("vector", [
        lambda e: e.tensor_scalar(out=mean_r, in0=ps_bc0[0:1, :], scalar1=1.0 / DM,
                                  scalar2=None, op0=ALU.mult),
        lambda e: e.tensor_tensor(out=var_r, in0=mean_r, in1=mean_r, op=ALU.mult),
        lambda e: e.scalar_tensor_tensor(out=var_r, in0=ps_bc1[0:1, :], scalar=1.0 / DM,
                                         in1=var_r, op0=ALU.mult, op1=ALU.subtract),
    ], waits=[("wt", t_mean), ("wt", t_sq)], inc="wv")
    s1 = p.step("scalar", [
        lambda e: e.activation(out=rstd_r, in_=var_r, func=AF.Ln, bias=fconst[0:1, 3:4]),
        lambda e: e.activation(out=rstd_r, in_=rstd_r, func=AF.Exp, scale=fconst[0:1, 2:3]),
    ], waits=[("wv", v1)], inc="ws")
    t_bc = p.step("tensor", [
        lambda e: e.matmul(ps_bc0[:], ones1[:], mean_r, start=True, stop=True),
        lambda e: e.matmul(ps_bc1[:], ones1[:], rstd_r, start=True, stop=True),
    ], waits=[("ws", s1), ("wv", v1)], inc="wt")
    last_v = None
    for ft in range(NFT):
        buf = tmp1 if ft % 2 == 0 else tmp2
        w = [("wt", t_bc)] if ft < 2 else []
        w2 = out_cb(ft, None, None, probe_wait=True)
        if w2:
            w += w2
        last_v = p.step("vector", [
            lambda e, ft=ft, buf=buf: e.tensor_tensor(
                out=buf[:], in0=xf[:, ft * TC:(ft + 1) * TC], in1=ps_bc0[:], op=ALU.subtract),
            lambda e, buf=buf: e.tensor_tensor(out=buf[:], in0=buf[:], in1=ps_bc1[:], op=ALU.mult),
        ], waits=w, inc="wv")
        out_cb(ft, buf, last_v, probe_wait=False)
    return last_v


def emit_phase1(p):
    nc = p.nc
    xT = p.io["x_in"]
    x1 = p.io["x_out"]
    wqkv_img = p.dram("wqkv_img", [24, 128, DM], F32, "ExternalInput")
    wproj_img = p.dram("wproj_img", [NFT, 128, DM], F32, "ExternalInput")
    rope_cos = p.dram("rope_cos", [128, T], F32, "ExternalInput")
    rope_sin = p.dram("rope_sin", [128, T], F32, "ExternalInput")
    rot_st = p.dram("rot_st", [128, 128], F32, "ExternalInput")
    m_low_d = p.dram("m_low", [128, 128], BF16, "ExternalInput")
    m_up_d = p.dram("m_up", [128, 128], BF16, "ExternalInput")
    ident_d = p.dram("ident_bf", [128, 128], BF16, "ExternalInput")
    ones_st = p.dram("ones_st", [1, 128], F32, "ExternalInput")
    ones_col = p.dram("ones_col", [128, 1], F32, "ExternalInput")
    ones_col_bf = p.dram("ones_col_bf", [128, 1], BF16, "ExternalInput")
    fconst_d = p.dram("fconst", [1, 8], F32, "ExternalInput")

    for s in ("ld", "lw", "wv", "ws", "wt", "st"):
        p.sem(s)

    cosT = p.sb("cosT", [128, T], F32)
    sinT = p.sb("sinT", [128, T], F32)
    rotS = p.sb("rotS", [128, 128], F32)
    mlow = p.sb("mlow", [128, 128], BF16)
    mup = p.sb("mup", [128, 128], BF16)
    ident = p.sb("ident", [128, 128], BF16)
    ones1 = p.sb("ones1", [1, 128], F32)
    onesc = p.sb("onesc", [128, 1], F32)
    onescb = p.sb("onescb", [128, 1], BF16)
    fconst = p.sb("fconst_sb", [1, 8], F32)

    xf = p.sb("xf", [128, NFT * TC], F32)
    xb = p.sb("xb", [128, NFT * TC], BF16)
    kf = p.sb("kf", [128, HKV * TC], F32)
    vbf = p.sb("vbf", [128, HKV * TC], BF16)
    qn = p.sb("qn", [128, HQ * TC], BF16)
    kn_ring = p.sb("kn_ring", [128, 2 * HKV * TC], BF16)
    vt_ring = p.sb("vt_ring", [128, 2 * HKV * 4 * 128], BF16)
    Ebuf = p.sb("Ebuf", [128, 2 * TC], BF16)
    ybuf = p.sb("ybuf", [128, HQ * TC], BF16)
    qtile = p.sb("qtile", [128, TC], F32)
    tmp1 = p.sb("tmp1", [128, TC], F32)
    tmp2 = p.sb("tmp2", [128, TC], F32)
    ytmp = p.sb("ytmp", [128, TC], F32)
    rows = p.sb("rows", [1, 3 * TC], F32)
    wgrp = [p.sb(f"wgrp{i}", [128, NFT * 128], BF16) for i in range(2)]

    ps_mm = [p.ps(f"ps_mm{i}", [128, TC]) for i in range(2)]
    ps_sc = [p.ps(f"ps_sc{i}", [128, TC]) for i in range(2)]
    ps_y = p.ps("ps_y", [128, TC])
    ps_bc = [p.ps(f"ps_bc{i}", [128, TC]) for i in range(2)]
    ps_tr = p.ps("ps_tr", [128, 128], BF16)

    for dst, src in [(cosT[:], rope_cos), (sinT[:], rope_sin), (rotS[:], rot_st),
                     (mlow[:], m_low_d), (mup[:], m_up_d), (ident[:], ident_d),
                     (ones1[:], ones_st), (onesc[:], ones_col), (onescb[:], ones_col_bf),
                     (fconst[:], fconst_d)]:
        p.dma("sync", dst, src, "ld")
    const_ld = p.ctr["ld"]

    SCL = 1.0 / math.sqrt(DH)

    wbuf_free = {0: None, 1: None}
    bank_free = {0: None, 1: None}
    sc_free = {0: None, 1: None}
    ebuf_free = {0: None, 1: None}
    bc1_free = None
    out_dma = {0: None, 1: None}
    xf_free = None
    ring_ok = []
    prev_attn_done = None

    for c in range(NCH):
        t0 = c * TC
        slot = c % 2

        if c == 0:
            w = [("ld", const_ld)]
            if p.ctr["st"] > 0:
                w.append(("st", p.ctr["st"]))
        else:
            w = [("wv", xf_free)]
        ldx = None
        for ft in range(NFT):
            ldx = p.dma("sync", xf[:, ft * TC:(ft + 1) * TC],
                        xT[ft * 128:(ft + 1) * 128, t0:t0 + TC], "ld",
                        waits=(w if ft == 0 else ()))
        cast_done = None
        for ft in range(NFT):
            cast_done = p.step("vector", lambda e, ft=ft: e.tensor_copy(
                out=xb[:, ft * TC:(ft + 1) * TC], in_=xf[:, ft * TC:(ft + 1) * TC]),
                waits=([("ld", ldx)] if ft == 0 else ()), inc="wv")

        for rt in range(24):
            buf = rt % 2
            w = [] if wbuf_free[buf] is None else [("wt", wbuf_free[buf])]
            lw_tgt = p.dma("gpsimd", wgrp[buf][:], wqkv_img[rt], "lw", waits=w)
            bank = rt % 2
            w = [("lw", lw_tgt), ("wv", cast_done)]
            if bank_free[bank] is not None:
                w.append(("wv", bank_free[bank]))
            mms = [lambda e, ci=ci, buf=buf, bank=bank: e.matmul(
                ps_mm[bank][:], wgrp[buf][:, ci * 128:(ci + 1) * 128],
                xb[:, ci * TC:(ci + 1) * TC], start=(ci == 0), stop=(ci == NFT - 1))
                for ci in range(NFT)]
            mm_tgt = p.step("tensor", mms, waits=w, inc="wt")
            wbuf_free[buf] = mm_tgt
            if rt < 20:
                if rt < 16:
                    ev = p.step("vector", lambda e, bank=bank: e.tensor_copy(
                        out=qtile[:], in_=ps_mm[bank][:]), waits=[("wt", mm_tgt)], inc="wv")
                    src = qtile
                    srcoff = 0
                else:
                    ev = p.step("vector", lambda e, bank=bank, rt=rt: e.tensor_copy(
                        out=kf[:, (rt - 16) * TC:(rt - 15) * TC], in_=ps_mm[bank][:]),
                        waits=[("wt", mm_tgt)], inc="wv")
                    src = kf
                    srcoff = (rt - 16) * TC
                bank_free[bank] = ev
                t_rot = p.step("tensor", lambda e, src=src, srcoff=srcoff: e.matmul(
                    ps_bc[0][:], rotS[:], src[:, srcoff:srcoff + TC], start=True, stop=True),
                    waits=[("wv", ev)], inc="wt")
                v_rope = p.step("vector", [
                    lambda e, src=src, srcoff=srcoff, t0=t0: e.tensor_tensor(
                        out=tmp1[:], in0=src[:, srcoff:srcoff + TC],
                        in1=cosT[:, t0:t0 + TC], op=ALU.mult),
                    lambda e, t0=t0: e.tensor_tensor(out=tmp2[:], in0=ps_bc[0][:],
                                                     in1=sinT[:, t0:t0 + TC], op=ALU.mult),
                    lambda e: e.tensor_tensor(out=tmp1[:], in0=tmp1[:], in1=tmp2[:], op=ALU.add),
                ], waits=[("wt", t_rot)], inc="wv")
                s_sq = p.step("scalar", lambda e: e.activation(out=tmp2[:], in_=tmp1[:], func=AF.Square),
                              waits=[("wv", v_rope)], inc="ws")
                t_ss = p.step("tensor", lambda e: e.matmul(ps_bc[0][0:1, :], onesc[:], tmp2[:],
                                                           start=True, stop=True),
                              waits=[("ws", s_sq)], inc="wt")
                s_rstd = p.step("scalar", [
                    lambda e: e.activation(out=rows[0:1, 0:TC], in_=ps_bc[0][0:1, :],
                                           func=AF.Ln, scale=fconst[0:1, 1:2], bias=fconst[0:1, 0:1]),
                    lambda e: e.activation(out=rows[0:1, 0:TC], in_=rows[0:1, 0:TC],
                                           func=AF.Exp, scale=fconst[0:1, 2:3]),
                ], waits=[("wt", t_ss)], inc="ws")
                t_bc = p.step("tensor", lambda e: e.matmul(ps_bc[0][:], ones1[:], rows[0:1, 0:TC],
                                                           start=True, stop=True),
                              waits=[("ws", s_rstd)], inc="wt")
                if rt < 16:
                    dst = qn[:, rt * TC:(rt + 1) * TC]
                    scl = SCL
                else:
                    dst = kn_ring[:, (slot * HKV + (rt - 16)) * TC:(slot * HKV + (rt - 16) + 1) * TC]
                    scl = 1.0
                v_qn = p.step("vector", lambda e, dst=dst, scl=scl: e.scalar_tensor_tensor(
                    out=dst, in0=tmp1[:], scalar=scl, in1=ps_bc[0][:], op0=ALU.mult, op1=ALU.mult),
                    waits=[("wt", t_bc)], inc="wv")
            else:
                ev = p.step("vector", lambda e, bank=bank, rt=rt: e.tensor_copy(
                    out=vbf[:, (rt - 20) * TC:(rt - 19) * TC], in_=ps_mm[bank][:]),
                    waits=[("wt", mm_tgt)], inc="wv")
                bank_free[bank] = ev

        v_stageQ = p.ctr["wv"]

        tr_evac = None
        for g in range(HKV):
            for kb in range(4):
                w = [("wv", v_stageQ)] if (g == 0 and kb == 0) else [("wv", tr_evac)]
                t_tr = p.step("tensor", lambda e, g=g, kb=kb: e.transpose(
                    ps_tr[:], vbf[:, g * TC + kb * 128:g * TC + (kb + 1) * 128], ident[:]),
                    waits=w, inc="wt")
                tr_evac = p.step("vector", lambda e, g=g, kb=kb, slot=slot: e.tensor_copy(
                    out=vt_ring[:, ((slot * HKV + g) * 4 + kb) * 128:((slot * HKV + g) * 4 + kb + 1) * 128],
                    in_=ps_tr[:]), waits=[("wt", t_tr)], inc="wv")
        ring_ok.append(p.ctr["wv"])

        kb_list = [kb for kb in range(4 * c - 4, 4 * c + 4) if kb >= 0]

        def kb_addr(kb):
            cc = kb // 4
            sl = cc % 2
            return sl, kb % 4

        for g in range(HKV):
            for qh in range(4 * g, 4 * g + 4):
                av_done = None
                first = True
                for j, kb in enumerate(kb_list):
                    sl, kbl = kb_addr(kb)
                    scb = j % 2
                    eb = j % 2
                    w = [("wv", ring_ok[c])]
                    if sc_free[scb] is not None:
                        w.append(("ws", sc_free[scb]))
                    kslice = kn_ring[:, (sl * HKV + g) * TC + kbl * 128:(sl * HKV + g) * TC + (kbl + 1) * 128]
                    t_sc = p.step("tensor", lambda e, scb=scb, kslice=kslice, qh=qh: e.matmul(
                        ps_sc[scb][:], kslice, qn[:, qh * TC:(qh + 1) * TC], start=True, stop=True),
                        waits=w, inc="wt")
                    mask_fns = []
                    inval = []
                    for qb in range(4):
                        wdist = 4 * c + qb - kb
                        if wdist == 0:
                            mask_fns.append(lambda e, scb=scb, qb=qb: e.tensor_tensor(
                                out=ps_sc[scb][:, qb * 128:(qb + 1) * 128],
                                in0=ps_sc[scb][:, qb * 128:(qb + 1) * 128], in1=mlow[:], op=ALU.add))
                        elif wdist == 4:
                            mask_fns.append(lambda e, scb=scb, qb=qb: e.tensor_tensor(
                                out=ps_sc[scb][:, qb * 128:(qb + 1) * 128],
                                in0=ps_sc[scb][:, qb * 128:(qb + 1) * 128], in1=mup[:], op=ALU.add))
                        elif wdist < 0 or wdist > 4:
                            inval.append(qb)
                    v_mask = None
                    if mask_fns:
                        v_mask = p.step("vector", mask_fns, waits=[("wt", t_sc)], inc="wv")
                    w = [("wv", v_mask)] if v_mask is not None else [("wt", t_sc)]
                    if ebuf_free[eb] is not None:
                        w.append(("wt", ebuf_free[eb]))
                    s_exp = p.step("scalar", lambda e, scb=scb, eb=eb: e.activation(
                        out=Ebuf[:, eb * TC:(eb + 1) * TC], in_=ps_sc[scb][:], func=AF.Exp),
                        waits=w, inc="ws")
                    sc_free[scb] = s_exp
                    v_zero = None
                    if inval:
                        v_zero = p.step("vector", [
                            lambda e, eb=eb, qb=qb: e.memset(
                                Ebuf[:, eb * TC + qb * 128:eb * TC + (qb + 1) * 128], 0.0)
                            for qb in inval], waits=[("ws", s_exp)], inc="wv")
                    w = [("wv", v_zero)] if v_zero is not None else [("ws", s_exp)]
                    if first and bc1_free is not None:
                        w.append(("wv", bc1_free))
                    vslice = vt_ring[:, ((sl * HKV + g) * 4 + kbl) * 128:((sl * HKV + g) * 4 + kbl + 1) * 128]
                    last = (j == len(kb_list) - 1)
                    av_done = p.step("tensor", [
                        lambda e, eb=eb, first=first, last=last: e.matmul(
                            ps_bc[1][0:1, :], onescb[:], Ebuf[:, eb * TC:(eb + 1) * TC],
                            start=first, stop=last),
                        lambda e, eb=eb, vslice=vslice, first=first, last=last: e.matmul(
                            ps_y[:], vslice, Ebuf[:, eb * TC:(eb + 1) * TC],
                            start=first, stop=last),
                    ], waits=w, inc="wt")
                    ebuf_free[eb] = av_done
                    first = False
                v_rec = p.step("vector", [
                    lambda e: e.reciprocal(out=rows[0:1, TC:2 * TC], in_=ps_bc[1][0:1, :]),
                    lambda e: e.tensor_copy(out=ytmp[:], in_=ps_y[:]),
                ], waits=[("wt", av_done)], inc="wv")
                t_dbc = p.step("tensor", lambda e: e.matmul(
                    ps_bc[1][:], ones1[:], rows[0:1, TC:2 * TC], start=True, stop=True),
                    waits=[("wv", v_rec)], inc="wt")
                bc1_free = p.step("vector", lambda e, qh=qh: e.tensor_tensor(
                    out=ybuf[:, qh * TC:(qh + 1) * TC], in0=ytmp[:], in1=ps_bc[1][:], op=ALU.mult),
                    waits=[("wt", t_dbc)], inc="wv")

        v_attn = p.ctr["wv"]

        res_done = None
        for ot in range(NFT):
            buf = ot % 2
            w = [("wt", wbuf_free[buf])] if wbuf_free[buf] is not None else []
            lw_tgt = p.dma("gpsimd", wgrp[buf][:], wproj_img[ot], "lw", waits=w)
            bank = ot % 2
            w = [("lw", lw_tgt), ("wv", v_attn)]
            if bank_free[bank] is not None:
                w.append(("wv", bank_free[bank]))
            mms = [lambda e, dt=dt, buf=buf, bank=bank: e.matmul(
                ps_mm[bank][:], wgrp[buf][:, dt * 128:(dt + 1) * 128],
                ybuf[:, dt * TC:(dt + 1) * TC], start=(dt == 0), stop=(dt == NFT - 1))
                for dt in range(NFT)]
            mm_tgt = p.step("tensor", mms, waits=w, inc="wt")
            wbuf_free[buf] = mm_tgt
            res_done = p.step("vector", lambda e, ot=ot, bank=bank: e.tensor_tensor(
                out=xf[:, ot * TC:(ot + 1) * TC], in0=xf[:, ot * TC:(ot + 1) * TC],
                in1=ps_mm[bank][:], op=ALU.add), waits=[("wt", mm_tgt)], inc="wv")
            bank_free[bank] = res_done

        def out_cb(ft, buf, v_tgt, probe_wait):
            if probe_wait:
                d = out_dma[ft % 2]
                return [("st", d)] if d is not None else []
            out_dma[ft % 2] = p.dma("sync", x1[ft * 128:(ft + 1) * 128, t0:t0 + TC],
                                    buf[:], "st", waits=[("wv", v_tgt)])
            return None

        xf_free = emit_ln_chunk(p, xf, None, ps_bc[0], ps_bc[1], rows, tmp1, tmp2,
                                out_cb, onesc, ones1, fconst,
                                wt_after=[("wv", res_done)], tag=f"ln{c}")

    return


DH_FFN = 8192
NHT = DH_FFN // 128


def emit_phase2(p):
    nc = p.nc
    xT = p.io["x_in"]
    x2 = p.io["x_out"]
    w1_img = p.dram("w1_img", [NHT, 128, DM], F32, "ExternalInput")
    w3_img = p.dram("w3_img", [NHT, 128, DM], F32, "ExternalInput")
    w2_img = p.dram("w2_img", [NFT, 128, DH_FFN], F32, "ExternalInput")
    h128_d = p.dram("h128_bf", [128, 128], BF16, "ExternalInput")
    ones_st = p.dram("ones_st", [1, 128], F32, "ExternalInput")
    ones_col = p.dram("ones_col", [128, 1], F32, "ExternalInput")
    fconst_d = p.dram("fconst", [1, 8], F32, "ExternalInput")
    w2h = p.dram("w2h", [NFT, 128, DH_FFN], BF16, "Internal")

    for s in ("ld", "lw", "wv", "ws", "wt", "st"):
        p.sem(s)

    h128 = p.sb("h128", [128, 128], BF16)
    ones1 = p.sb("ones1", [1, 128], F32)
    onesc = p.sb("onesc", [128, 1], F32)
    fconst = p.sb("fconst_sb", [1, 8], F32)

    xf = p.sb("xf", [128, NFT * TC], F32)
    xb = p.sb("xb", [128, NFT * TC], BF16)
    hidH = p.sb("hidH", [128, NHT * TC], BF16)
    hidt = p.sb("hidt", [128, TC], BF16)
    tmpS = p.sb("tmpS", [128, TC], F32)
    tmp1 = p.sb("tmp1", [128, TC], F32)
    tmp2 = p.sb("tmp2", [128, TC], F32)
    rows = p.sb("rows", [1, 4 * TC], F32)
    w1g = [p.sb(f"w1g{i}", [128, NFT * 128], BF16) for i in range(2)]
    w3g = [p.sb(f"w3g{i}", [128, NFT * 128], BF16) for i in range(2)]
    w2g = p.sb("w2g", [128, NHT * 128], BF16)
    wbig = [p.sb(f"wbig{i}", [128, NHT * 128], BF16) for i in range(2)]

    ps_g = [p.ps(f"ps_g{i}", [128, TC]) for i in range(2)]
    ps_l = [p.ps(f"ps_l{i}", [128, TC]) for i in range(2)]
    ps_h = p.ps("ps_h", [128, TC])
    ps_bc = [p.ps(f"ps_bc{i}", [128, TC]) for i in range(2)]

    for dst, src in [(h128[:], h128_d), (ones1[:], ones_st), (onesc[:], ones_col),
                     (fconst[:], fconst_d)]:
        p.dma("sync", dst, src, "ld")
    const_ld = p.ctr["ld"]

    IS = 1.0 / math.sqrt(DH_FFN)
    st_prev = None
    v_last_bfly = None
    for ob in range(NFT):
        w = [] if v_last_bfly is None else [("wv", v_last_bfly)]
        if st_prev is not None:
            w.append(("st", st_prev))
        lw = p.dma("gpsimd", wbig[0][:], w2_img[ob], "lw", waits=w)
        for s in range(6):
            src = wbig[s % 2]
            dstb = wbig[(s + 1) % 2]
            d = 1 << s
            fns = []
            for i in range(NHT):
                j = i ^ d
                lo = min(i, j)
                op = ALU.add if i == lo else ALU.subtract
                a = lo * 128
                b = (lo ^ d) * 128
                fns.append(lambda e, i=i, a=a, b=b, op=op, src=src, dstb=dstb: e.tensor_tensor(
                    out=dstb[:, i * 128:(i + 1) * 128],
                    in0=src[:, a:a + 128], in1=src[:, b:b + 128], op=op))
            w = [("lw", lw)] if s == 0 else []
            v_last_bfly = p.step("vector", fns, waits=w, inc="wv")
        st_prev = p.dma("sync", w2h[ob], wbig[0][:], "st", waits=[("wv", v_last_bfly)])
    w2h_ready = st_prev

    w1buf_free = {0: None, 1: None}
    w3buf_free = {0: None, 1: None}
    w2buf_free = None
    g_free = {0: None, 1: None}
    l_free = {0: None, 1: None}
    h_free = None
    hidt_free = None
    tmpS_free = None
    out_dma = {0: None, 1: None}
    xf_free = None

    for c in range(NCH):
        t0 = c * TC

        if c == 0:
            w = [("ld", const_ld)]
            if p.ctr["st"] > 0:
                w.append(("st", p.ctr["st"]))
        else:
            w = [("wv", xf_free)]
        ldx = None
        for ft in range(NFT):
            ldx = p.dma("sync", xf[:, ft * TC:(ft + 1) * TC],
                        xT[ft * 128:(ft + 1) * 128, t0:t0 + TC], "ld",
                        waits=(w if ft == 0 else ()))
        cast_done = None
        for ft in range(NFT):
            cast_done = p.step("vector", lambda e, ft=ft: e.tensor_copy(
                out=xb[:, ft * TC:(ft + 1) * TC], in_=xf[:, ft * TC:(ft + 1) * TC]),
                waits=([("ld", ldx)] if ft == 0 else ()), inc="wv")

        ev_hid = None
        for ht in range(NHT):
            buf = ht % 2
            bank = ht % 2
            w = [] if w1buf_free[buf] is None else [("wt", w1buf_free[buf])]
            lw1 = p.dma("gpsimd", w1g[buf][:], w1_img[ht], "lw", waits=w)
            w = [] if w3buf_free[buf] is None else [("wt", w3buf_free[buf])]
            lw3 = p.dma("gpsimd", w3g[buf][:], w3_img[ht], "lw", waits=w)
            w = [("lw", lw1), ("wv", cast_done)]
            if g_free[bank] is not None:
                w.append(("ws", g_free[bank]))
            mmG = p.step("tensor", [lambda e, ci=ci, buf=buf, bank=bank: e.matmul(
                ps_g[bank][:], w1g[buf][:, ci * 128:(ci + 1) * 128],
                xb[:, ci * TC:(ci + 1) * TC], start=(ci == 0), stop=(ci == NFT - 1))
                for ci in range(NFT)], waits=w, inc="wt")
            w1buf_free[buf] = mmG
            w = [("lw", lw3)]
            if l_free[bank] is not None:
                w.append(("wv", l_free[bank]))
            mmL = p.step("tensor", [lambda e, ci=ci, buf=buf, bank=bank: e.matmul(
                ps_l[bank][:], w3g[buf][:, ci * 128:(ci + 1) * 128],
                xb[:, ci * TC:(ci + 1) * TC], start=(ci == 0), stop=(ci == NFT - 1))
                for ci in range(NFT)], waits=w, inc="wt")
            w3buf_free[buf] = mmL
            w = [("wt", mmG)]
            if tmpS_free is not None:
                w.append(("wv", tmpS_free))
            s_silu = p.step("scalar", lambda e, bank=bank: e.activation(
                out=tmpS[:], in_=ps_g[bank][:], func=AF.Silu), waits=w, inc="ws")
            g_free[bank] = s_silu
            w = [("ws", s_silu), ("wt", mmL)]
            if hidt_free is not None:
                w.append(("wt", hidt_free))
            v_mul = p.step("vector", lambda e, bank=bank: e.tensor_tensor(
                out=hidt[:], in0=tmpS[:], in1=ps_l[bank][:], op=ALU.mult),
                waits=w, inc="wv")
            tmpS_free = v_mul
            l_free[bank] = v_mul
            w = [("wv", v_mul)]
            if h_free is not None:
                w.append(("wv", h_free))
            mmH = p.step("tensor", lambda e: e.matmul(
                ps_h[:], h128[:], hidt[:], start=True, stop=True), waits=w, inc="wt")
            hidt_free = mmH
            ev_hid = p.step("vector", lambda e, ht=ht: e.tensor_scalar(
                out=hidH[:, ht * TC:(ht + 1) * TC], in0=ps_h[:], scalar1=IS,
                scalar2=None, op0=ALU.mult), waits=[("wt", mmH)], inc="wv")
            h_free = ev_hid

        res_done = None
        for ot in range(NFT):
            w = [("st", w2h_ready)] if (c == 0 and ot == 0) else []
            if w2buf_free is not None:
                w.append(("wt", w2buf_free))
            lw2a = p.dma("gpsimd", w2g[:, 0:4096], w2h[ot][:, 0:4096], "lw", waits=w)
            w = [("wt", w2buf_free)] if w2buf_free is not None else []
            lw2b = p.dma("gpsimd", w2g[:, 4096:8192], w2h[ot][:, 4096:8192], "lw", waits=w)
            bank = ot % 2
            w = [("lw", lw2a), ("wv", ev_hid)]
            if g_free[bank] is not None:
                w.append(("ws", g_free[bank]))
            if l_free[bank] is not None:
                w.append(("wv", l_free[bank]))
            mm_a = p.step("tensor", [lambda e, kt=kt, bank=bank: e.matmul(
                ps_g[bank][:], w2g[:, kt * 128:(kt + 1) * 128],
                hidH[:, kt * TC:(kt + 1) * TC], start=(kt == 0), stop=False)
                for kt in range(NHT // 2)], waits=w, inc="wt")
            mm = p.step("tensor", [lambda e, kt=kt, bank=bank: e.matmul(
                ps_g[bank][:], w2g[:, kt * 128:(kt + 1) * 128],
                hidH[:, kt * TC:(kt + 1) * TC], start=False, stop=(kt == NHT - 1))
                for kt in range(NHT // 2, NHT)], waits=[("lw", lw2b)], inc="wt")
            w2buf_free = mm
            res_done = p.step("vector", lambda e, ot=ot, bank=bank: e.tensor_tensor(
                out=xf[:, ot * TC:(ot + 1) * TC], in0=xf[:, ot * TC:(ot + 1) * TC],
                in1=ps_g[bank][:], op=ALU.add), waits=[("wt", mm)], inc="wv")
            g_free[bank] = None
            l_free[bank] = None
            l_free[bank] = res_done

        if DEBUG_DUMP and c == 0:
            dbg_xf = nc.dram_tensor("dbg_xf", [128, NFT * TC], F32, kind="ExternalOutput").ap()
            p.dma("sync", dbg_xf, xf[:], "st", waits=[("wv", res_done)])

        def out_cb(ft, buf, v_tgt, probe_wait):
            if probe_wait:
                d = out_dma[ft % 2]
                return [("st", d)] if d is not None else []
            out_dma[ft % 2] = p.dma("sync", x2[ft * 128:(ft + 1) * 128, t0:t0 + TC],
                                    buf[:], "st", waits=[("wv", v_tgt)])
            return None

        xf_free = emit_ln_chunk(p, xf, None, ps_bc[0], ps_bc[1], rows, tmp1, tmp2,
                                out_cb, onesc, ones1, fconst,
                                wt_after=[("wv", res_done)], tag=f"l2ln{c}")

    return


def run_launch2(x1T_list, inputs, n_cores=8):
    c = host_consts()
    h = np.zeros((128, 128), np.float64)
    for i in range(128):
        for j in range(128):
            h[i, j] = 1.0 if bin(i & j).count("1") % 2 == 0 else -1.0
    base = {
        "w1_img": tile_img(inputs["L0_ffn_W1"].T.astype(np.float32)),
        "w3_img": tile_img(inputs["L0_ffn_W3"].T.astype(np.float32)),
        "w2_img": tile_img(inputs["L0_ffn_W2"].T.astype(np.float32)),
        "h128_bf": h.astype(BF),
        "ones_st": c["ones_st"], "ones_col": c["ones_col"], "fconst": c["fconst"],
    }
    nc = build_single(2, "x2")
    in_maps = []
    for xT in x1T_list:
        m = dict(base)
        m["xT"] = xT
        in_maps.append(m)
    res = run_bass_kernel_spmd(nc, in_maps, core_ids=list(range(n_cores)))
    if DEBUG_DUMP:
        return [np.asarray(r["x2"], dtype=np.float32) for r in res.results], res.results
    return [np.asarray(r["x2"], dtype=np.float32) for r in res.results]


DIN = 4096
NDT = DIN // 128
DST = 16


def emit_phase3(p):
    nc = p.nc
    xT = p.io["x_in"]
    x3 = p.io["x_out"]
    win_img = p.dram("win_img", [64, 128, DM], F32, "ExternalInput")
    outw_img = p.dram("outw_img", [NFT, 128, DIN], F32, "ExternalInput")
    conv_cols_d = p.dram("conv_cols", [128, NDT * 4], F32, "ExternalInput")
    convb_d = p.dram("convb_cols", [128, NDT], F32, "ExternalInput")
    wxb_d = p.dram("wxb_cols", [128, NDT * DST], F32, "ExternalInput")
    wxc_d = p.dram("wxc_cols", [128, NDT * DST], F32, "ExternalInput")
    wxdt_d = p.dram("wxdt_cols", [128, NDT], F32, "ExternalInput")
    a2_d = p.dram("a2_cols", [128, NDT * DST], F32, "ExternalInput")
    dtb_d = p.dram("dtb_cols", [128, NDT], F32, "ExternalInput")
    dcols_d = p.dram("d_cols", [128, NDT], F32, "ExternalInput")
    ones_st = p.dram("ones_st", [1, 128], F32, "ExternalInput")
    ones_stb_d = p.dram("ones_st_bf", [1, 128], BF16, "ExternalInput")
    ones_col = p.dram("ones_col", [128, 1], F32, "ExternalInput")
    fconst_d = p.dram("fconst", [1, 8], F32, "ExternalInput")
    sel16_d = p.dram("sel16", [DST, DST * 128], BF16, "ExternalInput")
    zscr = p.dram("z_scr", [DIN, T], BF16, "Internal")

    for s in ("ld", "lw", "wv", "ws", "wt", "st"):
        p.sem(s)

    ones1 = p.sb("ones1", [1, 128], F32)
    ones1b = p.sb("ones1b", [1, 128], BF16)
    sel16 = p.sb("sel16_sb", [DST, DST * 128], BF16)
    onesc = p.sb("onesc", [128, 1], F32)
    fconst = p.sb("fconst_sb", [1, 8], F32)
    conv_cols = p.sb("conv_cols_sb", [128, NDT * 4], F32)
    convb = p.sb("convb_sb", [128, NDT], F32)
    wxb = p.sb("wxb", [128, NDT * DST], BF16)
    wxc = p.sb("wxc", [128, NDT * DST], BF16)
    wxdt = p.sb("wxdt", [128, NDT], BF16)
    a2 = p.sb("a2", [128, NDT * DST], F32)
    dtb = p.sb("dtb", [128, NDT], F32)
    dcols = p.sb("dcols", [128, NDT], F32)
    mb = p.sb("mb", [1, 1], F32)

    xf = p.sb("xf", [128, NFT * TC], F32)
    xb = p.sb("xb", [128, NFT * TC], BF16)
    xcb = p.sb("xcb", [128, NDT * TC], BF16)
    dtBrep = p.sb("dtBrep", [128, DST * TC], BF16)
    crepA = p.sb("crepA", [128, DST * TC], BF16)
    zst = [p.sb(f"zst{i}", [128, TC], BF16) for i in range(2)]
    hist = p.sb("hist", [128, NDT * 3], F32)
    xs_tmp = p.sb("xs_tmp", [128, TC + 3], F32)
    convacc = p.sb("convacc", [128, TC], F32)
    B_sb = p.sb("B_sb", [DST, TC], BF16)
    C_sb = p.sb("C_sb", [DST, TC], BF16)
    dtB_sb = p.sb("dtB_sb", [DST, TC], BF16)
    rowm = p.sb("rowm", [1, 3 * TC], F32)
    carry = p.sb("carry", [128, NDT * DST], F32)
    abar = [p.sb(f"abar{i}", [128, TC], F32) for i in range(2)]
    u_sb = [p.sb(f"u_sb{i}", [128, TC], BF16) for i in range(2)]
    hbf = [p.sb(f"hbf{i}", [128, TC], BF16) for i in range(2)]
    wbf = [p.sb(f"wbf{i}", [128, TC], BF16) for i in range(2)]
    h_tmp = p.sb("h_tmp", [128, TC], F32)
    w_tmp = p.sb("w_tmp", [128, TC], F32)
    yacc = p.sb("yacc", [128, TC], F32)
    ytmp2 = p.sb("ytmp2", [128, TC], F32)
    wgi = [p.sb("wgi0", [128, DM], BF16)] * 2
    wgo = [p.sb("wgo0", [128, DIN], BF16)] * 2

    ps_mm = [p.ps(f"ps_mm{i}", [128, TC]) for i in range(2)]
    ps_B = p.ps("ps_B", [DST, TC])
    ps_C = p.ps("ps_C", [DST, TC])
    ps_dt = p.ps("ps_dt", [1, TC])
    ps_bc = [p.ps(f"ps_bc{i}", [128, TC]) for i in range(3)]

    for dst, src in [(ones1[:], ones_st), (ones1b[:], ones_stb_d), (onesc[:], ones_col),
                     (fconst[:], fconst_d),
                     (conv_cols[:], conv_cols_d), (convb[:], convb_d),
                     (a2[:], a2_d), (dtb[:], dtb_d), (dcols[:], dcols_d),
                     (sel16[:], sel16_d)]:
        p.dma("sync", dst, src, "ld")
    for dst, src in [(wxb[:], wxb_d), (wxc[:], wxc_d), (wxdt[:], wxdt_d)]:
        p.dma("gpsimd", dst, src, "ld")
    const_ld = p.ctr["ld"]

    v0 = p.step("vector", [
        lambda e: e.memset(carry[:], 0.0),
        lambda e: e.memset(hist[:], 0.0),
        lambda e: e.reduce_sum(out=h_tmp[:, 0:1], in_=dtb[:], axis=AX.X),
    ], waits=[("ld", const_ld)], inc="wv")
    s0 = p.step("scalar", lambda e: e.activation(out=a2[:], in_=a2[:], func=AF.Exp),
                waits=[("ld", const_ld)], inc="ws")
    v0b = p.step("vector", lambda e: e.tensor_scalar(out=a2[:], in0=a2[:], scalar1=-1.0,
                                                     scalar2=None, op0=ALU.mult),
                 waits=[("ws", s0)], inc="wv")
    t0_ = p.step("tensor", lambda e: e.matmul(ps_dt[0:1, 0:1], onesc[:], h_tmp[:, 0:1],
                                              start=True, stop=True),
                 waits=[("wv", v0)], inc="wt")
    v0c = p.step("vector", lambda e: e.tensor_scalar(out=mb[:], in0=ps_dt[0:1, 0:1],
                                                     scalar1=1.0 / DIN, scalar2=None,
                                                     op0=ALU.mult),
                 waits=[("wt", t0_)], inc="wv")

    wibuf_free = {0: None, 1: None}
    wobuf_free = {0: None, 1: None}
    zst_free = {0: None, 1: None}
    zld_free = {0: None, 1: None}
    bank_busy = {0: None, 1: None}
    convacc_free = None
    bc0_sfree = None
    bc1_free = None
    bc2_free = None
    out_dma = {0: None, 1: None}
    xf_free = None

    for c in range(NCH):
        t0 = c * TC
        if c == 0:
            w = [("wv", v0c), ("ld", const_ld)]
            if p.ctr["st"] > 0:
                w.append(("st", p.ctr["st"]))
        else:
            w = [("wv", xf_free)]
        ldx = None
        for ft in range(NFT):
            ldx = p.dma("sync", xf[:, ft * TC:(ft + 1) * TC],
                        xT[ft * 128:(ft + 1) * 128, t0:t0 + TC], "ld",
                        waits=(w if ft == 0 else ()))
        cast_done = None
        for ft in range(NFT):
            cast_done = p.step("vector", lambda e, ft=ft: e.tensor_copy(
                out=xb[:, ft * TC:(ft + 1) * TC], in_=xf[:, ft * TC:(ft + 1) * TC]),
                waits=([("ld", ldx)] if ft == 0 else ()), inc="wv")

        silu_last = None
        for rt in range(64):
            buf = 0
            bank = rt % 2
            w = [] if wibuf_free[buf] is None else [("wt", wibuf_free[buf])]
            lw_t = p.dma("gpsimd", wgi[buf][:], win_img[rt], "lw", waits=w)
            w = [("lw", lw_t), ("wv", cast_done)]
            if bank_busy[bank] is not None:
                w.append(bank_busy[bank])
            mms = [lambda e, ci=ci, buf=buf, bank=bank: e.matmul(
                ps_mm[bank][:], wgi[buf][:, ci * 128:(ci + 1) * 128],
                xb[:, ci * TC:(ci + 1) * TC], start=(ci == 0), stop=(ci == NFT - 1))
                for ci in range(NFT)]
            mm = p.step("tensor", mms, waits=w, inc="wt")
            wibuf_free[buf] = mm
            if rt < NDT:
                dh = rt
                w = [("wt", mm)]
                if convacc_free is not None:
                    w.append(("ws", convacc_free))
                v_conv = p.step("vector", [
                    lambda e, dh=dh: e.tensor_copy(out=xs_tmp[:, 0:3], in_=hist[:, dh * 3:dh * 3 + 3]),
                    lambda e, bank=bank: e.tensor_copy(out=xs_tmp[:, 3:TC + 3], in_=ps_mm[bank][:]),
                    lambda e, dh=dh, bank=bank: e.tensor_copy(out=hist[:, dh * 3:dh * 3 + 3],
                                                              in_=ps_mm[bank][:, TC - 3:TC]),
                    lambda e, dh=dh: e.tensor_scalar(
                        out=convacc[:], in0=xs_tmp[:, 0:TC],
                        scalar1=conv_cols[:, dh * 4:dh * 4 + 1],
                        scalar2=convb[:, dh:dh + 1], op0=ALU.mult, op1=ALU.add),
                    lambda e, dh=dh: e.scalar_tensor_tensor(
                        out=convacc[:], in0=xs_tmp[:, 1:TC + 1],
                        scalar=conv_cols[:, dh * 4 + 1:dh * 4 + 2], in1=convacc[:],
                        op0=ALU.mult, op1=ALU.add),
                    lambda e, dh=dh: e.scalar_tensor_tensor(
                        out=convacc[:], in0=xs_tmp[:, 2:TC + 2],
                        scalar=conv_cols[:, dh * 4 + 2:dh * 4 + 3], in1=convacc[:],
                        op0=ALU.mult, op1=ALU.add),
                    lambda e, dh=dh: e.scalar_tensor_tensor(
                        out=convacc[:], in0=xs_tmp[:, 3:TC + 3],
                        scalar=conv_cols[:, dh * 4 + 3:dh * 4 + 4], in1=convacc[:],
                        op0=ALU.mult, op1=ALU.add),
                ], waits=w, inc="wv")
                bank_busy[bank] = ("wv", v_conv)
                silu_last = p.step("scalar", lambda e, dh=dh: e.activation(
                    out=xcb[:, dh * TC:(dh + 1) * TC], in_=convacc[:], func=AF.Silu),
                    waits=[("wv", v_conv)], inc="ws")
                convacc_free = silu_last
            else:
                dh = rt - NDT
                zb = dh % 2
                w = [("wt", mm)]
                if zst_free[zb] is not None:
                    w.append(("st", zst_free[zb]))
                silu_last = p.step("scalar", lambda e, zb=zb, bank=bank: e.activation(
                    out=zst[zb][:], in_=ps_mm[bank][:], func=AF.Silu),
                    waits=w, inc="ws")
                bank_busy[bank] = ("ws", silu_last)
                zst_free[zb] = p.dma("sync", zscr[dh * 128:(dh + 1) * 128, t0:t0 + TC],
                                     zst[zb][:], "st", waits=[("ws", silu_last)])

        w = [("ws", silu_last)]
        mmB = p.step("tensor", [lambda e, d=d: e.matmul(
            ps_B[:], wxb[:, d * DST:(d + 1) * DST], xcb[:, d * TC:(d + 1) * TC],
            start=(d == 0), stop=(d == NDT - 1)) for d in range(NDT)],
            waits=w, inc="wt")
        mmC = p.step("tensor", [lambda e, d=d: e.matmul(
            ps_C[:], wxc[:, d * DST:(d + 1) * DST], xcb[:, d * TC:(d + 1) * TC],
            start=(d == 0), stop=(d == NDT - 1)) for d in range(NDT)],
            inc="wt")
        mmD = p.step("tensor", [lambda e, d=d: e.matmul(
            ps_dt[:], wxdt[:, d:d + 1], xcb[:, d * TC:(d + 1) * TC],
            start=(d == 0), stop=(d == NDT - 1)) for d in range(NDT)],
            inc="wt")
        v_bc = p.step("vector", [
            lambda e: e.tensor_copy(out=B_sb[:], in_=ps_B[:]),
            lambda e: e.tensor_copy(out=C_sb[:], in_=ps_C[:]),
        ], waits=[("wt", mmB), ("wt", mmC), ("wt", mmD)], inc="wv")
        abs_r = rowm[0:1, 0:TC]
        ln_r = rowm[0:1, TC:2 * TC]
        relu_r = rowm[0:1, 0:TC]
        dt_row = rowm[0:1, 2 * TC:3 * TC]
        s_dt = p.step("scalar", [
            lambda e: e.activation(out=abs_r, in_=ps_dt[0:1, :], func=AF.Abs, bias=mb[0:1, 0:1]),
            lambda e: e.activation(out=abs_r, in_=abs_r, func=AF.Exp, scale=fconst[0:1, 4:5]),
            lambda e: e.activation(out=ln_r, in_=abs_r, func=AF.Ln, bias=1.0),
            lambda e: e.activation(out=relu_r, in_=ps_dt[0:1, :], func=AF.Relu, bias=mb[0:1, 0:1]),
        ], waits=[("wt", mmD)], inc="ws")
        v_dt = p.step("vector", lambda e: e.tensor_tensor(out=dt_row, in0=relu_r, in1=ln_r,
                                                          op=ALU.add),
                      waits=[("ws", s_dt)], inc="wv")
        w = [("wv", v_dt)]
        if bc1_free is not None:
            w.append(("wv", bc1_free))
        t_d16 = p.step("tensor", lambda e: e.matmul(ps_bc[1][0:DST, :], ones1[0:1, 0:DST],
                                                    dt_row, start=True, stop=True),
                       waits=w, inc="wt")
        v_dtb = p.step("vector", lambda e: e.tensor_tensor(
            out=dtB_sb[:], in0=B_sb[:], in1=ps_bc[1][0:DST, :], op=ALU.mult),
            waits=[("wt", t_d16), ("wv", v_bc)], inc="wv")
        bc1_free = v_dtb
        w = [("wv", v_dt)]
        if bc0_sfree is not None:
            w.append(("ws", bc0_sfree))
        t_dtrep = p.step("tensor", lambda e: e.matmul(ps_bc[0][:], ones1[:], dt_row,
                                                      start=True, stop=True),
                         waits=w, inc="wt")
        cache_done = None
        for si in range(DST):
            if si == 0:
                w = [("wv", v_dtb)]
                if bc1_free is not None:
                    w.append(("wv", bc1_free))
            else:
                w = [("wv", cache_done)]
            tb1 = p.step("tensor", lambda e, si=si: e.matmul(
                ps_bc[1][:], sel16[:, si * 128:(si + 1) * 128], dtB_sb[:],
                start=True, stop=True), waits=w, inc="wt")
            w = [("wv", bc2_free)] if (bc2_free is not None and si == 0) else []
            tb2 = p.step("tensor", lambda e, si=si: e.matmul(
                ps_bc[2][:], sel16[:, si * 128:(si + 1) * 128], C_sb[:],
                start=True, stop=True), waits=w, inc="wt")
            cache_done = p.step("vector", [
                lambda e, si=si: e.tensor_copy(out=dtBrep[:, si * TC:(si + 1) * TC],
                                               in_=ps_bc[1][:]),
                lambda e, si=si: e.tensor_copy(out=crepA[:, si * TC:(si + 1) * TC],
                                               in_=ps_bc[2][:]),
            ], waits=[("wt", tb1), ("wt", tb2)], inc="wv")
            bc1_free = cache_done
            bc2_free = cache_done

        p.sem("gp")
        y2_last = None
        abar_free = [None, None]
        u_free = [None, None]
        h_free = [None, None]
        wbf_free = {0: None, 1: None}
        pi = 0
        for dh in range(NDT):
            zb = dh % 2
            w = []
            if zld_free[zb] is not None:
                w.append(("wv", zld_free[zb]))
            if dh < 2:
                w.append(("st", zst_free[0] if zst_free[0] else 0) if False else ("st", p.ctr["st"]))
            zld = p.dma("sync", zst[zb][:], zscr[dh * 128:(dh + 1) * 128, t0:t0 + TC],
                        "lw", waits=w)
            for si in range(DST):
                pi = (dh * DST + si) % 2
                idx = dh * DST + si
                w = [("wt", t_dtrep)]
                if abar_free[pi] is not None:
                    w.append(("wv", abar_free[pi]))
                s_ab = p.step("scalar", lambda e, dh=dh, si=si, pi=pi: e.activation(
                    out=abar[pi][:], in_=ps_bc[0][:], func=AF.Exp,
                    scale=a2[:, dh * DST + si:dh * DST + si + 1]), waits=w, inc="ws")
                bc0_sfree = s_ab
                w = [("wv", cache_done)] if idx == 0 else []
                if idx == 0 and p.ctr["st"] > 0:
                    w.append(("st", p.ctr["st"]))
                if u_free[pi] is not None:
                    w.append(("wv", u_free[pi]))
                v_u = p.step("vector", lambda e, dh=dh, si=si, pi=pi: e.tensor_tensor(
                    out=u_sb[pi][:], in0=dtBrep[:, si * TC:(si + 1) * TC],
                    in1=xcb[:, dh * TC:(dh + 1) * TC], op=ALU.mult), waits=w, inc="wv")
                w = [("ws", s_ab)]
                if h_free[pi] is not None:
                    w.append(h_free[pi])
                v_scan = p.step("vector", lambda e, idx=idx, pi=pi: e.tensor_tensor_scan(
                    out=hbf[pi][:], data0=abar[pi][:], data1=u_sb[pi][:],
                    initial=carry[:, idx:idx + 1], op0=ALU.mult, op1=ALU.add),
                    waits=w, inc="wv")
                abar_free[pi] = v_scan
                u_free[pi] = v_scan
                s_cy = p.step("scalar", lambda e, idx=idx, pi=pi: e.copy(
                    out=carry[:, idx:idx + 1], in_=hbf[pi][:, TC - 1:TC]),
                    waits=[("wv", v_scan)], inc="ws")
                wb_i = si % 2
                w = []
                if wbf_free[wb_i] is not None:
                    w.append(("gp", wbf_free[wb_i]))
                v_w = p.step("vector", lambda e, si=si, pi=pi, wb_i=wb_i: e.tensor_tensor(
                    out=wbf[wb_i][:], in0=hbf[pi][:], in1=crepA[:, si * TC:(si + 1) * TC],
                    op=ALU.mult), waits=w, inc="wv")
                if si == 0:
                    g_acc = p.step("gpsimd", lambda e, wb_i=wb_i: e.tensor_copy(
                        out=yacc[:], in_=wbf[wb_i][:]), waits=[("wv", v_w)], inc="gp")
                else:
                    g_acc = p.step("gpsimd", lambda e, wb_i=wb_i: e.tensor_tensor(
                        out=yacc[:], in0=yacc[:], in1=wbf[wb_i][:], op=ALU.add),
                        waits=[("wv", v_w)], inc="gp")
                wbf_free[wb_i] = g_acc
                h_free[pi] = ("ws", s_cy)
            y2_last = p.step("vector", [
                lambda e, dh=dh: e.scalar_tensor_tensor(
                    out=ytmp2[:], in0=xcb[:, dh * TC:(dh + 1) * TC],
                    scalar=dcols[:, dh:dh + 1], in1=yacc[:], op0=ALU.mult, op1=ALU.add),
                lambda e, dh=dh, zb=zb: e.tensor_tensor(
                    out=xcb[:, dh * TC:(dh + 1) * TC], in0=ytmp2[:],
                    in1=zst[zb][:], op=ALU.mult),
            ], waits=[("lw", zld), ("gp", g_acc)], inc="wv")
            zld_free[zb] = y2_last

        res_done = None
        for ot in range(NFT):
            buf = 0
            bank = ot % 2
            w = [] if wobuf_free[buf] is None else [("wt", wobuf_free[buf])]
            lw_t = p.dma("gpsimd", wgo[buf][:], outw_img[ot], "lw", waits=w)
            w = [("lw", lw_t), ("wv", y2_last)]
            if bank_busy[bank] is not None:
                w.append(bank_busy[bank])
            mms = [lambda e, d=d, buf=buf, bank=bank: e.matmul(
                ps_mm[bank][:], wgo[buf][:, d * 128:(d + 1) * 128],
                xcb[:, d * TC:(d + 1) * TC], start=(d == 0), stop=(d == NDT - 1))
                for d in range(NDT)]
            mm = p.step("tensor", mms, waits=w, inc="wt")
            wobuf_free[buf] = mm
            res_done = p.step("vector", lambda e, ot=ot, bank=bank: e.tensor_tensor(
                out=xf[:, ot * TC:(ot + 1) * TC], in0=xf[:, ot * TC:(ot + 1) * TC],
                in1=ps_mm[bank][:], op=ALU.add), waits=[("wt", mm)], inc="wv")
            bank_busy[bank] = ("wv", res_done)

        def out_cb(ft, buf, v_tgt, probe_wait):
            if probe_wait:
                d = out_dma[ft % 2]
                return [("st", d)] if d is not None else []
            out_dma[ft % 2] = p.dma("sync", x3[ft * 128:(ft + 1) * 128, t0:t0 + TC],
                                    buf[:], "st", waits=[("wv", v_tgt)])
            return None

        xf_free = emit_ln_chunk(p, xf, None, ps_bc[1], ps_bc[2], rowm, h_tmp, w_tmp,
                                out_cb, onesc, ones1, fconst,
                                wt_after=[("wv", res_done)], tag=f"l3ln{c}")
        bc1_free = xf_free
        bc2_free = xf_free

    return


def run_launch3(x2T_list, inputs, n_cores=8):
    c = host_consts()
    conv_w = inputs["L1_attn_conv_w"].astype(np.float32)
    cc = conv_w[:, 0, :].reshape(NDT, 128, 4).transpose(1, 0, 2).reshape(128, NDT * 4)
    cb = inputs["L1_attn_conv_b"].astype(np.float32).reshape(NDT, 128).T
    W_x = inputs["L1_attn_W_x"].astype(np.float32)
    wxb = W_x[0:DST].T.reshape(NDT, 128, DST).transpose(1, 0, 2).reshape(128, NDT * DST)
    wxc = W_x[DST:2 * DST].T.reshape(NDT, 128, DST).transpose(1, 0, 2).reshape(128, NDT * DST)
    wxdt = W_x[2 * DST].reshape(NDT, 128).T
    a2 = inputs["L1_attn_A_log"].astype(np.float32).reshape(NDT, 128, DST).transpose(1, 0, 2).reshape(128, NDT * DST)
    dtb = inputs["L1_attn_dt_bias"].astype(np.float32).reshape(NDT, 128).T
    dc = inputs["L1_attn_D"].astype(np.float32).reshape(NDT, 128).T
    base = {
        "win_img": tile_img(inputs["L1_attn_W_in"].T.astype(np.float32)),
        "outw_img": tile_img(inputs["L1_attn_out_W"].T.astype(np.float32)),
        "conv_cols": np.ascontiguousarray(cc),
        "convb_cols": np.ascontiguousarray(cb),
        "wxb_cols": np.ascontiguousarray(wxb),
        "wxc_cols": np.ascontiguousarray(wxc),
        "wxdt_cols": np.ascontiguousarray(wxdt),
        "a2_cols": np.ascontiguousarray(a2),
        "dtb_cols": np.ascontiguousarray(dtb),
        "d_cols": np.ascontiguousarray(dc),
        "ones_st": c["ones_st"], "ones_st_bf": c["ones_st_bf"], "ones_col": c["ones_col"],
        "fconst": c["fconst"], "sel16": c["sel16"],
    }
    nc = build_single(3, "x3")
    in_maps = []
    for xT in x2T_list:
        m = dict(base)
        m["xT"] = xT
        in_maps.append(m)
    res = run_bass_kernel_spmd(nc, in_maps, core_ids=list(range(n_cores)))
    if DEBUG_DUMP:
        return [np.asarray(r["x3"], dtype=np.float32) for r in res.results], res.results
    return [np.asarray(r["x3"], dtype=np.float32) for r in res.results]


NE = 8
TQ = 512
NQ = T // TQ
NSEG = TQ // TC


def emit_phase4(p):
    nc = p.nc
    st0 = p.ctr.get("st", 0)
    xT = p.io["x_in"]
    x4 = p.io["x_out"]
    rw_img = p.dram("rw_img", [NE, NFT, 128, DM], F32, "ExternalInput")
    ew_img = p.dram("ew_img", [NE, NFT, 128, DM], F32, "ExternalInput")
    rb_cols_d = p.dram("rb_cols", [128, NE * NFT], F32, "ExternalInput")
    ebT_d = p.dram("ebT", [NE, DM], F32, "ExternalInput")
    ones_st = p.dram("ones_st", [1, 128], F32, "ExternalInput")
    ones_stb_d = p.dram("ones_st_bf", [1, 128], BF16, "ExternalInput")
    ones_col = p.dram("ones_col", [128, 1], F32, "ExternalInput")
    ones_colb_d = p.dram("ones_col_bf", [128, 1], BF16, "ExternalInput")
    fconst_d = p.dram("fconst", [1, 8], F32, "ExternalInput")
    cbounce = p.dram("c_bounce", [1, NE * TQ], BF16, "Internal")

    for s in ("ld", "lw", "wv", "ws", "wt", "st"):
        p.sem(s)

    ones1 = p.sb("ones1", [1, 128], F32)
    ones1b = p.sb("ones1b", [1, 128], BF16)
    onesc = p.sb("onesc", [128, 1], F32)
    onescb = p.sb("onescb", [128, 1], BF16)
    fconst = p.sb("fconst_sb", [1, 8], F32)
    rb_cols = p.sb("rb_cols_sb", [128, NE * NFT], F32)
    ebT = p.sb("ebT_sb", [NE, DM], BF16)

    x3b = p.sb("x3b", [128, NFT * TQ], BF16)
    finalq = p.sb("finalq", [128, NFT * TQ], F32)
    planes = p.sb("planes", [128, NE * NSEG * TC], BF16)
    crep = p.sb("crep", [128, NE * NSEG * TC], BF16)
    c_sb = p.sb("c_sb", [1, NE * TQ], BF16)
    c_tile = p.sb("c_tile", [NE, TQ], BF16)
    xf = p.sb("xf", [128, NFT * TC], F32)
    m1 = p.sb("m1", [128, TC], F32)
    m2 = p.sb("m2", [128, TC], F32)
    s12 = p.sb("s12", [128, TC], F32)
    tmpd = p.sb("tmpd", [128, TC], F32)
    tmps = p.sb("tmps", [128, TC], F32)
    tmp1 = p.sb("tmp1", [128, TC], F32)
    tmp2 = p.sb("tmp2", [128, TC], F32)
    rows = p.sb("rows", [1, 3 * TC], F32)
    wg = [p.sb(f"wg{i}", [128, DM], BF16) for i in range(2)]

    ps_mm = [p.ps(f"ps_mm{i}", [128, TC]) for i in range(2)]
    ps_row = p.ps("ps_row", [1, TC])
    ps_bc = [p.ps(f"ps_bc{i}", [128, TC]) for i in range(2)]

    for dst, src in [(ones1[:], ones_st), (ones1b[:], ones_stb_d), (onesc[:], ones_col),
                     (onescb[:], ones_colb_d),
                     (fconst[:], fconst_d), (rb_cols[:], rb_cols_d)]:
        p.dma("sync", dst, src, "ld")
    p.dma("gpsimd", ebT[:], ebT_d, "ld")
    const_ld = p.ctr["ld"]

    wbuf_free = {0: None, 1: None}
    bank_free = {0: None, 1: None}
    row_free = None
    x3b_free = None
    out_dma = {0: None, 1: None}
    xf_free = None
    bc_free = [None, None]

    for q in range(NQ):
        q0 = q * TQ
        if q == 0:
            w = [("ld", const_ld)]
            if p.ctr["st"] > 0:
                w.append(("st", p.ctr["st"]))
        else:
            w = [("wt", x3b_free)]
        ldq = None
        for ft in range(NFT):
            ldq = p.dma("gpsimd", x3b[:, ft * TQ:(ft + 1) * TQ],
                        xT[ft * 128:(ft + 1) * 128, q0:q0 + TQ], "lw",
                        waits=(w if ft == 0 else ()))
        c_add = None
        gv_last = None
        for ot in range(NFT):
            ev_tgt = {}
            for ei in range(NE):
                buf = (ot * NE + ei) % 2
                w = [] if wbuf_free[buf] is None else [("wt", wbuf_free[buf])]
                lw_t = p.dma("gpsimd", wg[buf][:], rw_img[ei, ot], "lw", waits=w)
                for ts_ in range(NSEG):
                    bank = ts_ % 2
                    w = [("lw", max(lw_t, ldq))]
                    if bank_free[bank] is not None:
                        w.append(bank_free[bank])
                    mm = p.step("tensor", [lambda e, ci=ci, buf=buf, bank=bank, ts_=ts_: e.matmul(
                        ps_mm[bank][:], wg[buf][:, ci * 128:(ci + 1) * 128],
                        x3b[:, ci * TQ + ts_ * TC:ci * TQ + (ts_ + 1) * TC],
                        start=(ci == 0), stop=(ci == NFT - 1))
                        for ci in range(NFT)], waits=w, inc="wt")
                    if ts_ == NSEG - 1:
                        wbuf_free[buf] = mm
                    ev = p.step("vector", lambda e, ei=ei, ts_=ts_, ot=ot, bank=bank: e.tensor_scalar(
                        out=planes[:, (ei * NSEG + ts_) * TC:(ei * NSEG + ts_ + 1) * TC],
                        in0=ps_mm[bank][:], scalar1=rb_cols[:, ei * NFT + ot:ei * NFT + ot + 1],
                        scalar2=None, op0=ALU.add), waits=[("wt", mm)], inc="wv")
                    bank_free[bank] = ("wv", ev)
            for ts_ in range(NSEG):
                def pl(ei, ts_=ts_):
                    return planes[:, (ei * NSEG + ts_) * TC:(ei * NSEG + ts_ + 1) * TC]
                fns = [
                    lambda e, pl=pl: e.tensor_tensor(out=m1[:], in0=pl(0), in1=pl(1), op=ALU.max),
                    lambda e, pl=pl: e.tensor_tensor(out=tmpd[:], in0=pl(2), in1=pl(3), op=ALU.max),
                    lambda e: e.tensor_tensor(out=m1[:], in0=m1[:], in1=tmpd[:], op=ALU.max),
                    lambda e, pl=pl: e.tensor_tensor(out=tmpd[:], in0=pl(4), in1=pl(5), op=ALU.max),
                    lambda e: e.tensor_tensor(out=m1[:], in0=m1[:], in1=tmpd[:], op=ALU.max),
                    lambda e, pl=pl: e.tensor_tensor(out=tmpd[:], in0=pl(6), in1=pl(7), op=ALU.max),
                    lambda e: e.tensor_tensor(out=m1[:], in0=m1[:], in1=tmpd[:], op=ALU.max),
                ]
                for ei in range(NE):
                    fns.append(lambda e, ei=ei, pl=pl: e.tensor_tensor(
                        out=tmpd[:], in0=pl(ei), in1=m1[:], op=ALU.is_ge))
                    fns.append(lambda e, ei=ei, pl=pl: e.scalar_tensor_tensor(
                        out=tmpd[:], in0=tmpd[:], scalar=-1e30, in1=pl(ei),
                        op0=ALU.mult, op1=ALU.add))
                    if ei == 0:
                        fns.append(lambda e: e.tensor_copy(out=m2[:], in_=tmpd[:]))
                    else:
                        fns.append(lambda e: e.tensor_tensor(out=m2[:], in0=m2[:],
                                                             in1=tmpd[:], op=ALU.max))
                fns.append(lambda e: e.tensor_tensor(out=s12[:], in0=m1[:], in1=m2[:], op=ALU.add))
                v_m = p.step("vector", fns, inc="wv")
                for ei in range(NE):
                    v_d = p.step("vector", lambda e, ei=ei, pl=pl: e.scalar_tensor_tensor(
                        out=tmpd[:], in0=pl(ei), scalar=2.0, in1=s12[:],
                        op0=ALU.mult, op1=ALU.subtract),
                        waits=([] if ei > 0 else [("wv", v_m)]), inc="wv")
                    s_g = p.step("scalar", lambda e: e.activation(
                        out=tmps[:], in_=tmpd[:], func=AF.Sigmoid),
                        waits=[("wv", v_d)], inc="ws")
                    gfns = [
                        lambda e, ei=ei, pl=pl: e.tensor_tensor(out=tmpd[:], in0=pl(ei), in1=m2[:],
                                                                op=ALU.is_ge),
                        lambda e, ei=ei, pl=pl: e.tensor_tensor(out=pl(ei), in0=tmps[:], in1=tmpd[:],
                                                                op=ALU.mult),
                    ]
                    gslice = crep[:, (ei * NSEG + ts_) * TC:(ei * NSEG + ts_ + 1) * TC]
                    if ot == 0:
                        w = [bc_free[0]] if (bc_free[0] is not None and ei == 0 and ts_ == 0) else []
                        gfns.append(lambda e, ei=ei, pl=pl, gslice=gslice: e.tensor_copy(
                            out=gslice, in_=pl(ei)))
                    else:
                        w = []
                        gfns.append(lambda e, ei=ei, pl=pl, gslice=gslice: e.tensor_tensor(
                            out=gslice, in0=gslice, in1=pl(ei), op=ALU.add))
                    gv = p.step("vector", gfns, waits=[("ws", s_g)] + w, inc="wv")
                    gv_last = gv

        c_add = None
        for ei in range(NE):
            for ts_ in range(NSEG):
                w = [("wv", gv_last)] if (ei == 0 and ts_ == 0) else []
                if row_free is not None:
                    w.append(("wv", row_free))
                t_cs = p.step("tensor", lambda e, ei=ei, ts_=ts_: e.matmul(
                    ps_row[:], onescb[:],
                    crep[:, (ei * NSEG + ts_) * TC:(ei * NSEG + ts_ + 1) * TC],
                    start=True, stop=True), waits=w, inc="wt")
                c_add = p.step("vector", lambda e, ei=ei, ts_=ts_: e.tensor_scalar(
                    out=c_sb[0:1, ei * TQ + ts_ * TC:ei * TQ + (ts_ + 1) * TC],
                    in0=ps_row[:], scalar1=1.0, scalar2=None, op0=ALU.mult),
                    waits=[("wt", t_cs)], inc="wv")
                row_free = c_add
        st_c = p.dma("sync", cbounce, c_sb[0:1, :], "lw", waits=[("wv", c_add)])
        ld_ct = p.dma("gpsimd", c_tile[:], cbounce.rearrange("c (a b) -> (c a) b", b=TQ),
                      "lw", waits=[("lw", st_c)])
        crep_done = None
        for ei in range(NE):
            for ts_ in range(NSEG):
                w = [("wv", c_add), ("wt", p.ctr["wt"])] if (ei == 0 and ts_ == 0) else []
                if bc_free[0] is not None:
                    w.append(bc_free[0])
                t_b = p.step("tensor", lambda e, ei=ei, ts_=ts_: e.matmul(
                    ps_bc[0][:], ones1b[:],
                    c_sb[0:1, ei * TQ + ts_ * TC:ei * TQ + (ts_ + 1) * TC],
                    start=True, stop=True), waits=w, inc="wt")
                crep_done = p.step("vector", lambda e, ei=ei, ts_=ts_: e.tensor_copy(
                    out=crep[:, (ei * NSEG + ts_) * TC:(ei * NSEG + ts_ + 1) * TC],
                    in_=ps_bc[0][:]), waits=[("wt", t_b)], inc="wv")
                bc_free[0] = ("wv", crep_done)

        ex_last_mm = None
        fin_last = None
        for ot in range(NFT):
            for ei in range(NE):
                buf = (ot * NE + ei) % 2
                w = [] if wbuf_free[buf] is None else [("wt", wbuf_free[buf])]
                lw_t = p.dma("gpsimd", wg[buf][:], ew_img[ei, ot], "lw", waits=w)
                for ts_ in range(NSEG):
                    bank = ts_ % 2
                    w = [("lw", lw_t), ("wv", crep_done), ("wv", gv_last)]
                    if bank_free[bank] is not None:
                        w.append(bank_free[bank])
                    mm = p.step("tensor", [lambda e, ci=ci, buf=buf, bank=bank, ts_=ts_: e.matmul(
                        ps_mm[bank][:], wg[buf][:, ci * 128:(ci + 1) * 128],
                        x3b[:, ci * TQ + ts_ * TC:ci * TQ + (ts_ + 1) * TC],
                        start=(ci == 0), stop=(ci == NFT - 1))
                        for ci in range(NFT)], waits=w, inc="wt")
                    if ts_ == NSEG - 1:
                        wbuf_free[buf] = mm
                    ex_last_mm = mm
                    fslice = finalq[:, ot * TQ + ts_ * TC:ot * TQ + (ts_ + 1) * TC]
                    cslice = crep[:, (ei * NSEG + ts_) * TC:(ei * NSEG + ts_ + 1) * TC]
                    if ei == 0:
                        fin = p.step("vector", lambda e, fslice=fslice, cslice=cslice, bank=bank:
                                     e.tensor_tensor(out=fslice, in0=ps_mm[bank][:], in1=cslice,
                                                     op=ALU.mult), waits=[("wt", mm)], inc="wv")
                    else:
                        fin = p.step("vector", [
                            lambda e, cslice=cslice, bank=bank: e.tensor_tensor(
                                out=tmp2[:], in0=ps_mm[bank][:], in1=cslice, op=ALU.mult),
                            lambda e, fslice=fslice: e.tensor_tensor(
                                out=fslice, in0=fslice, in1=tmp2[:], op=ALU.add),
                        ], waits=[("wt", mm)], inc="wv")
                    bank_free[bank] = ("wv", fin)
                    fin_last = fin
            for ts_ in range(NSEG):
                w = [("lw", ld_ct)]
                if bc_free[1] is not None:
                    w.append(bc_free[1])
                t_bias = p.step("tensor", lambda e, ot=ot, ts_=ts_: e.matmul(
                    ps_bc[1][:], ebT[:, ot * 128:(ot + 1) * 128],
                    c_tile[:, ts_ * TC:(ts_ + 1) * TC], start=True, stop=True),
                    waits=w, inc="wt")
                fin_last = p.step("vector", lambda e, ot=ot, ts_=ts_: e.tensor_tensor(
                    out=finalq[:, ot * TQ + ts_ * TC:ot * TQ + (ts_ + 1) * TC],
                    in0=finalq[:, ot * TQ + ts_ * TC:ot * TQ + (ts_ + 1) * TC],
                    in1=ps_bc[1][:], op=ALU.add), waits=[("wt", t_bias)], inc="wv")
                bc_free[1] = ("wv", fin_last)
        x3b_free = ex_last_mm

        for ts_ in range(NSEG):
            t0 = q0 + ts_ * TC
            if xf_free is not None:
                w = [("wv", xf_free)]
            else:
                w = [("st", st0)] if st0 > 0 else []
            ldx = None
            for ft in range(NFT):
                ldx = p.dma("sync", xf[:, ft * TC:(ft + 1) * TC],
                            xT[ft * 128:(ft + 1) * 128, t0:t0 + TC], "ld",
                            waits=(w if ft == 0 else ()))
            res_done = None
            for ft in range(NFT):
                res_done = p.step("vector", lambda e, ft=ft, ts_=ts_: e.tensor_tensor(
                    out=xf[:, ft * TC:(ft + 1) * TC], in0=xf[:, ft * TC:(ft + 1) * TC],
                    in1=finalq[:, ft * TQ + ts_ * TC:ft * TQ + (ts_ + 1) * TC], op=ALU.add),
                    waits=([("ld", ldx), ("wv", fin_last)] if ft == 0 else ()), inc="wv")

            def out_cb(ft, buf, v_tgt, probe_wait, t0=t0):
                if probe_wait:
                    d = out_dma[ft % 2]
                    return [("st", d)] if d is not None else []
                out_dma[ft % 2] = p.dma("sync", x4[ft * 128:(ft + 1) * 128, t0:t0 + TC],
                                        buf[:], "st", waits=[("wv", v_tgt)])
                return None

            xf_free = emit_ln_chunk(p, xf, None, ps_bc[0], ps_bc[1], rows, tmp1, tmp2,
                                    out_cb, onesc, ones1, fconst,
                                    wt_after=[("wv", res_done)], tag=f"l4ln{q}_{ts_}")
            bc_free = [("wv", xf_free), ("wv", xf_free)]

    return


def run_launch4(x3T_list, inputs, n_cores=8):
    c = host_consts()
    rw = inputs["L1_ffn_router_w"].astype(np.float32)
    rw_em = rw.reshape(DM, NE, DM).transpose(1, 0, 2)
    rw_img = np.stack([tile_img(np.ascontiguousarray(rw_em[e].T)) for e in range(NE)])
    ew = inputs["L1_ffn_expert_w"].astype(np.float32)
    ew_img = np.stack([tile_img(np.ascontiguousarray(ew[e].T)) for e in range(NE)])
    rb = inputs["L1_ffn_router_b"].astype(np.float32)
    rb_cols = rb.reshape(NFT, 128, NE).transpose(1, 2, 0).reshape(128, NE * NFT)
    base = {
        "rw_img": rw_img, "ew_img": ew_img,
        "rb_cols": np.ascontiguousarray(rb_cols),
        "ebT": inputs["L1_ffn_expert_b"].astype(np.float32),
        "ones_st": c["ones_st"], "ones_st_bf": c["ones_st_bf"], "ones_col": c["ones_col"],
        "ones_col_bf": c["ones_col_bf"], "fconst": c["fconst"],
    }
    nc = build_single(4, "x4")
    in_maps = []
    for xT in x3T_list:
        m = dict(base)
        m["xT"] = xT
        in_maps.append(m)
    res = run_bass_kernel_spmd(nc, in_maps, core_ids=list(range(n_cores)))
    return [np.asarray(r["x4"], dtype=np.float32) for r in res.results]


def _prep_l1(inputs, consts):
    base = dict(consts_sub(consts, ["rope_cos", "rope_sin", "rot_st", "m_low", "m_up",
                                    "ident_bf", "ones_st", "ones_col", "ones_col_bf", "fconst"]))
    base["wqkv_img"] = tile_img(inputs["L0_attn_w_qkv"].T.astype(np.float32))
    base["wproj_img"] = tile_img(inputs["L0_attn_w_proj"].T.astype(np.float32))
    return base


def consts_sub(c, names):
    m = {"ident_bf": "ident_bf", "m_low": "m_low", "m_up": "m_up"}
    out = {}
    for n in names:
        out[n] = c[n]
    return out


PHASE_EMITTERS = {1: None, 2: None, 3: None, 4: None}


def _mk_prog():
    nc = bass.Bass("TRN2", target_bir_lowering=False)
    ctx = contextlib.ExitStack()
    return Prog(nc, ctx)


def _finish(p):
    p.step("gpsimd", lambda e: e.nop(), waits=[("st", p.ctr["st"])])
    p.run()
    return p.nc


def build_single(phase_idx, out_name):
    p = _mk_prog()
    p.io["x_in"] = p.dram("xT", [DM, T], F32, "ExternalInput")
    p.io["x_out"] = p.dram(out_name, [DM, T], F32, "ExternalOutput")
    PHASE_EMITTERS[phase_idx](p)
    return _finish(p)


def fused_aliases(p):
    a = p.alias
    xf = p.ctx.enter_context(p.nc.sbuf_tensor("A_xf", [128, 8192], F32))
    xb = p.ctx.enter_context(p.nc.sbuf_tensor("A_xb", [128, 8192], BF16))
    bfA = p.ctx.enter_context(p.nc.sbuf_tensor("A_bfA", [128, 32768], BF16))
    fC = p.ctx.enter_context(p.nc.sbuf_tensor("A_fC", [128, 4096], F32))
    bfB = p.ctx.enter_context(p.nc.sbuf_tensor("A_bfB", [128, 4096], BF16))
    f32big = p.ctx.enter_context(p.nc.sbuf_tensor("A_f32big", [128, 8192], F32))
    wstr = p.ctx.enter_context(p.nc.sbuf_tensor("A_wstr", [128, 16384], BF16))
    rows = p.ctx.enter_context(p.nc.sbuf_tensor("A_rows", [1, 1536], F32))
    K = [p.ctx.enter_context(p.nc.psum_tensor(f"K{i}", [128, 512], F32)) for i in range(7)]
    a["rows"] = rows[:]
    a["rowm"] = rows[:]
    a["xf"] = xf[:]
    a["xb"] = xb[:]
    a["cosT"] = f32big[:, 0:4096]
    a["sinT"] = f32big[:, 4096:8192]
    a["qn"] = bfA[:, 0:8192]
    a["ybuf"] = bfA[:, 8192:16384]
    a["kn_ring"] = bfA[:, 16384:20480]
    a["vt_ring"] = bfA[:, 20480:24576]
    a["Ebuf"] = bfA[:, 24576:25600]
    a["vbf"] = bfA[:, 25600:27648]
    a["kf"] = fC[:, 0:2048]
    a["qtile"] = fC[:, 2048:2560]
    a["tmp1"] = fC[:, 2560:3072]
    a["tmp2"] = fC[:, 3072:3584]
    a["ytmp"] = fC[:, 3584:4096]
    a["wgrp0"] = wstr[:, 0:2048]
    a["wgrp1"] = wstr[:, 2048:4096]
    a["ps_mm0"] = K[0][:]
    a["ps_mm1"] = K[1][:]
    a["ps_sc0"] = K[2][:]
    a["ps_sc1"] = K[3][:]
    a["ps_y"] = K[4][:]
    a["ps_bc0"] = K[5][:]
    a["ps_bc1"] = K[6][:]
    a["hidH"] = bfA[:, 0:32768]
    a["hidt"] = bfB[:, 0:512]
    a["tmpS"] = fC[:, 0:512]
    a["w1g0"] = wstr[:, 0:2048]
    a["w1g1"] = wstr[:, 2048:4096]
    a["w3g0"] = wstr[:, 4096:6144]
    a["w3g1"] = wstr[:, 6144:8192]
    a["w2g"] = wstr[:, 8192:16384]
    a["wbig0"] = bfA[:, 0:8192]
    a["wbig1"] = bfA[:, 8192:16384]
    a["ps_g0"] = K[0][:]
    a["ps_g1"] = K[1][:]
    a["ps_l0"] = K[2][:]
    a["ps_l1"] = K[3][:]
    a["ps_h"] = K[4][:]
    a["xcb"] = bfA[:, 0:16384]
    a["dtBrep"] = bfA[:, 16384:24576]
    a["crepA"] = bfA[:, 24576:32768]
    a["a2"] = f32big[:, 0:512]
    a["carry"] = f32big[:, 512:1024]
    a["conv_cols_sb"] = f32big[:, 1536:1664]
    a["convb_sb"] = f32big[:, 1664:1696]
    a["dtb"] = f32big[:, 1696:1728]
    a["dcols"] = f32big[:, 1728:1760]
    a["mb"] = f32big[0:1, 1760:1761]
    a["abar0"] = fC[:, 2048:2560]
    a["abar1"] = f32big[:, 1024:1536]
    a["u_sb0"] = bfB[:, 0:512]
    a["u_sb1"] = bfB[:, 512:1024]
    a["hbf0"] = bfB[:, 1024:1536]
    a["hbf1"] = bfB[:, 1536:2048]
    a["wbf0"] = bfB[:, 2048:2560]
    a["zst0"] = bfB[:, 2560:3072]
    a["zst1"] = bfB[:, 3072:3584]
    a["wbf1"] = bfB[:, 3584:4096]
    a["hist"] = fC[:, 3600:3696]
    a["wxb"] = wstr[:, 3584:4096]
    a["wxc"] = wstr[:, 10752:11264]
    a["wxdt"] = wstr[:, 11264:11296]
    a["ident"] = wstr[:, 11296:11424]
    a["ones1b"] = wstr[0:1, 11424:11552]
    a["onescb"] = wstr[0:128, 11552:11553]
    a["B_sb"] = wstr[0:16, 2048:2560]
    a["C_sb"] = wstr[0:16, 2560:3072]
    a["dtB_sb"] = wstr[0:16, 3072:3584]
    a["sel16_sb"] = wstr[0:16, 8192:10240]
    a["h_tmp"] = fC[:, 0:512]
    a["w_tmp"] = fC[:, 512:1024]
    a["yacc"] = fC[:, 1024:1536]
    a["ytmp2"] = fC[:, 1536:2048]
    a["convacc"] = fC[:, 2560:3072]
    a["xs_tmp"] = fC[:, 3072:3588]
    a["wgi0"] = wstr[:, 0:2048]
    a["wgo0"] = wstr[:, 4096:8192]
    a["ps_B"] = K[2][0:16, :]
    a["ps_C"] = K[2][32:48, :]
    a["ps_dt"] = K[2][64:65, :]
    a["ps_bc2"] = K[4][:]
    a["x3b"] = xb[:]
    a["finalq"] = f32big[:, 0:8192]
    a["planes"] = bfA[:, 0:4096]
    a["crep"] = bfA[:, 4096:8192]
    a["m1"] = fC[:, 0:512]
    a["m2"] = fC[:, 512:1024]
    a["s12"] = fC[:, 1024:1536]
    a["tmpd"] = fC[:, 1536:2048]
    a["tmps"] = fC[:, 2048:2560]
    a["wg0"] = wstr[:, 0:2048]
    a["wg1"] = wstr[:, 2048:4096]
    a["ebT_sb"] = wstr[0:8, 4096:6144]
    a["rb_cols_sb"] = fC[:, 3696:3824]
    a["c_sb"] = wstr[0:1, 6144:10240]
    a["c_tile"] = wstr[0:8, 10240:10752]
    a["ps_row"] = K[2][0:1, :]


def build_fused():
    p = _mk_prog()
    fused_aliases(p)
    xin = p.dram("xT", [DM, T], F32, "ExternalInput")
    x1s = p.dram("x1s", [DM, T], F32, "Internal")
    x2s = p.dram("x2s", [DM, T], F32, "Internal")
    x3s = p.dram("x3s", [DM, T], F32, "Internal")
    x4o = p.dram("x4", [DM, T], F32, "ExternalOutput")
    chain = [(xin, x1s), (x1s, x2s), (x2s, x3s), (x3s, x4o)]
    for i, (src, dst) in enumerate(chain):
        p.io["x_in"] = src
        p.io["x_out"] = dst
        PHASE_EMITTERS[i + 1](p)
        if i < 3:
            p.barrier()
    return _finish(p)


def run_launch1(x_full, inputs, n_cores=8):
    outs = run_launch1_fm(x_full, inputs, n_cores)
    if DEBUG_DUMP:
        outs, results = outs
        return np.stack([o.T for o in outs], axis=0), results
    return np.stack([o.T for o in outs], axis=0)


def run_launch1_fm(x_full, inputs, n_cores=8):
    c = host_consts()
    base = _prep_l1(inputs, c)
    nc = build_single(1, "x1")
    in_maps = []
    for b in range(x_full.shape[0]):
        m = dict(base)
        m["xT"] = np.ascontiguousarray(x_full[b].T).astype(np.float32)
        in_maps.append(m)
    res = run_bass_kernel_spmd(nc, in_maps, core_ids=list(range(n_cores)))
    outs = [np.asarray(r["x1"], dtype=np.float32) for r in res.results]
    if DEBUG_DUMP:
        return outs, res.results
    return outs


PHASE_EMITTERS[1] = emit_phase1
PHASE_EMITTERS[2] = emit_phase2
PHASE_EMITTERS[3] = emit_phase3
PHASE_EMITTERS[4] = emit_phase4


def host_all_inputs(inputs):
    c = host_consts()
    base = _prep_l1(inputs, c)
    h = np.zeros((128, 128), np.float64)
    for i in range(128):
        for j in range(128):
            h[i, j] = 1.0 if bin(i & j).count("1") % 2 == 0 else -1.0
    base.update({
        "w1_img": tile_img(inputs["L0_ffn_W1"].T.astype(np.float32)),
        "w3_img": tile_img(inputs["L0_ffn_W3"].T.astype(np.float32)),
        "w2_img": tile_img(inputs["L0_ffn_W2"].T.astype(np.float32)),
        "h128_bf": h.astype(BF),
    })
    conv_w = inputs["L1_attn_conv_w"].astype(np.float32)
    cc = conv_w[:, 0, :].reshape(NDT, 128, 4).transpose(1, 0, 2).reshape(128, NDT * 4)
    cb = inputs["L1_attn_conv_b"].astype(np.float32).reshape(NDT, 128).T
    W_x = inputs["L1_attn_W_x"].astype(np.float32)
    base.update({
        "win_img": tile_img(inputs["L1_attn_W_in"].T.astype(np.float32)),
        "outw_img": tile_img(inputs["L1_attn_out_W"].T.astype(np.float32)),
        "conv_cols": np.ascontiguousarray(cc),
        "convb_cols": np.ascontiguousarray(cb),
        "wxb_cols": np.ascontiguousarray(W_x[0:DST].T.reshape(NDT, 128, DST).transpose(1, 0, 2).reshape(128, NDT * DST)),
        "wxc_cols": np.ascontiguousarray(W_x[DST:2 * DST].T.reshape(NDT, 128, DST).transpose(1, 0, 2).reshape(128, NDT * DST)),
        "wxdt_cols": np.ascontiguousarray(W_x[2 * DST].reshape(NDT, 128).T),
        "a2_cols": np.ascontiguousarray(inputs["L1_attn_A_log"].astype(np.float32).reshape(NDT, 128, DST).transpose(1, 0, 2).reshape(128, NDT * DST)),
        "dtb_cols": np.ascontiguousarray(inputs["L1_attn_dt_bias"].astype(np.float32).reshape(NDT, 128).T),
        "d_cols": np.ascontiguousarray(inputs["L1_attn_D"].astype(np.float32).reshape(NDT, 128).T),
        "ones_st_bf": c["ones_st_bf"],
        "sel16": c["sel16"],
    })
    rw = inputs["L1_ffn_router_w"].astype(np.float32)
    rw_em = rw.reshape(DM, NE, DM).transpose(1, 0, 2)
    rb = inputs["L1_ffn_router_b"].astype(np.float32)
    base.update({
        "rw_img": np.stack([tile_img(np.ascontiguousarray(rw_em[e].T)) for e in range(NE)]),
        "ew_img": np.stack([tile_img(np.ascontiguousarray(inputs["L1_ffn_expert_w"].astype(np.float32)[e].T)) for e in range(NE)]),
        "rb_cols": np.ascontiguousarray(rb.reshape(NFT, 128, NE).transpose(1, 2, 0).reshape(128, NE * NFT)),
        "ebT": inputs["L1_ffn_expert_b"].astype(np.float32),
    })
    return base


def kernel(**inputs):
    x = np.asarray(inputs["L0_attn_x"], dtype=np.float32)
    n_cores = x.shape[0]
    base = host_all_inputs(inputs)
    nc = build_fused()
    in_maps = []
    for b in range(n_cores):
        m = dict(base)
        m["xT"] = np.ascontiguousarray(x[b].T).astype(np.float32)
        in_maps.append(m)
    res = run_bass_kernel_spmd(nc, in_maps, core_ids=list(range(n_cores)))
    return np.stack([np.asarray(r["x4"], dtype=np.float32).T for r in res.results], axis=0)


# version 14
# speedup vs baseline: 1.0211x; 1.0211x over previous
import contextlib
import math

import numpy as np
import ml_dtypes

import concourse.bass as bass
import concourse.mybir as mybir
from concourse.bass_utils import run_bass_kernel_spmd

F32 = mybir.dt.float32
BF16 = mybir.dt.bfloat16
AF = mybir.ActivationFunctionType
ALU = mybir.AluOpType
AX = mybir.AxisListType

BF = ml_dtypes.bfloat16

T = 4096
DM = 2048
NFT = DM // 128
TC = 512
NCH = T // TC
LN_EPS = 1e-5


class Prog:

    ENGINES = ("sync", "scalar", "vector", "tensor", "gpsimd")

    def __init__(self, nc, ctx):
        self.nc = nc
        self.ctx = ctx
        self.ops = {e: [] for e in self.ENGINES}
        self.sems = {}
        self.ctr = {}
        self.alias = {}
        self.dram_map = {}
        self.io = {}

    def sem(self, name):
        if name in self.sems:
            return self.sems[name]
        self.sems[name] = self.ctx.enter_context(self.nc.semaphore(name))
        self.ctr[name] = 0
        return self.sems[name]

    def sb(self, name, shape, dtype):
        if name in self.alias:
            return self.alias[name]
        if not hasattr(self, "_sbufs"):
            self._sbufs = {}
        if name not in self._sbufs:
            self._sbufs[name] = self.ctx.enter_context(self.nc.sbuf_tensor(name, shape, dtype))
        return self._sbufs[name]

    def ps(self, name, shape, dtype=F32):
        if name in self.alias:
            return self.alias[name]
        if not hasattr(self, "_psums"):
            self._psums = {}
        if name not in self._psums:
            self._psums[name] = self.ctx.enter_context(self.nc.psum_tensor(name, shape, dtype))
        return self._psums[name]

    def dram(self, name, shape, dtype, kind):
        if name in self.dram_map:
            return self.dram_map[name]
        ap = self.nc.dram_tensor(name, shape, dtype, kind=kind).ap()
        self.dram_map[name] = ap
        return ap

    def barrier(self, tag=""):
        self.sem("bar")
        for eng in self.ENGINES:
            self.step(eng, lambda e: e.nop(), inc="bar")
        tgt = self.ctr["bar"]
        for eng in self.ENGINES:
            sh = self.sems["bar"]
            self.ops[eng].append(lambda e, sh=sh, v=tgt: e.wait_ge(sh, v))

    def step(self, eng, fns, waits=(), inc=None, inc_dma=False):
        if not isinstance(fns, (list, tuple)):
            fns = [fns]
        for s, v in waits:
            sh = self.sems[s]
            self.ops[eng].append(lambda e, sh=sh, v=v: e.wait_ge(sh, v))
        tgt = None
        if inc is not None:
            sh = self.sems[inc]
            self.ctr[inc] += 16 if inc_dma else 1
            tgt = self.ctr[inc]
        for i, f in enumerate(fns):
            if inc is not None and i == len(fns) - 1:
                self.ops[eng].append(
                    lambda e, f=f, sh=sh, n=(16 if inc_dma else 1): f(e).then_inc(sh, n))
            else:
                self.ops[eng].append(f)
        return tgt

    def dma(self, eng, out, in_, sem="ld", waits=()):
        return self.step(eng, lambda e, o=out, i=in_: e.dma_start(out=o, in_=i),
                         waits=waits, inc=sem, inc_dma=True)

    def run(self):
        block = self.ctx.enter_context(self.nc.Block())
        for eng_name in self.ENGINES:
            opslist = self.ops[eng_name]
            if not opslist:
                continue

            def body(e, opslist=opslist):
                for fn in opslist:
                    fn(e)

            getattr(block, eng_name)(body)
        self.ctx.close()


def tile_img(WT):
    C, R = WT.shape
    a = C // 128
    rb = R // 128
    return np.ascontiguousarray(
        WT.reshape(a, 128, rb, 128).transpose(2, 1, 0, 3).reshape(rb, 128, C))


def host_consts():
    c = {}
    c["ones_st"] = np.ones((1, 128), np.float32)
    c["ones_st_bf"] = np.ones((1, 128), BF)
    c["ones_col"] = np.ones((128, 1), np.float32)
    c["ones_col_bf"] = np.ones((128, 1), BF)
    dh = 128
    inv = 1.0 / (10000.0 ** (np.arange(0, dh, 2, dtype=np.float64) / dh))
    freqs = np.arange(T, dtype=np.float64)[:, None] * inv[None, :]
    c["rope_cos"] = np.ascontiguousarray(np.repeat(np.cos(freqs), 2, axis=1).T).astype(np.float32)
    c["rope_sin"] = np.ascontiguousarray(np.repeat(np.sin(freqs), 2, axis=1).T).astype(np.float32)
    S = np.zeros((128, 128), np.float32)
    for i in range(64):
        S[2 * i + 1, 2 * i] = -1.0
        S[2 * i, 2 * i + 1] = 1.0
    c["rot_st"] = S
    NEG = -1e9
    ka = np.arange(128)[:, None]
    ro = np.arange(128)[None, :]
    c["m_low"] = np.where(ka <= ro, 0.0, NEG).astype(BF)
    c["m_up"] = np.where(ka > ro, 0.0, NEG).astype(BF)
    c["ident_bf"] = np.eye(128).astype(BF)
    sel = np.zeros((16, 16 * 128), np.float32)
    for si in range(16):
        sel[si, si * 128:(si + 1) * 128] = 1.0
    c["sel16"] = sel.astype(BF)
    c["fconst"] = np.array([[1.1920929e-07, 1.0 / 128, -0.5, 1e-5, -1.0, 1.0, 1.0 / 2048, 0.0]],
                           np.float32)
    return c


DEBUG_DUMP = False
HQ, HKV, DH, WIN = 16, 4, 128, 512
RMS_EPS = 1.1920929e-07


def emit_ln_chunk(p, xf, src_rows_ps, ps_bc0, ps_bc1, rowbuf, tmp1, tmp2,
                  out_cb, onesc, ones1, fconst, wt_after=(), tag=""):
    mms = []
    for ft in range(NFT):
        mms.append(lambda e, ft=ft: e.matmul(ps_bc0[0:1, :], onesc[:],
                                             xf[:, ft * TC:(ft + 1) * TC],
                                             start=(ft == 0), stop=(ft == NFT - 1)))
    t_mean = p.step("tensor", mms, waits=wt_after, inc="wt")
    sq_done = [None, None]
    for ft in range(NFT):
        buf = tmp1 if ft % 2 == 0 else tmp2
        w = list(wt_after) if ft == 0 else []
        if ft == 0 and p.ctr["st"] > 0:
            w.append(("st", p.ctr["st"]))
        if sq_done[ft % 2] is not None:
            w.append(("wt", sq_done[ft % 2]))
        s_tgt = p.step("scalar", lambda e, ft=ft, buf=buf: e.activation(
            out=buf[:], in_=xf[:, ft * TC:(ft + 1) * TC], func=AF.Square),
            waits=w, inc="ws")
        sq_done[ft % 2] = p.step("tensor", lambda e, buf=buf, ft=ft: e.matmul(
            ps_bc1[0:1, :], onesc[:], buf[:],
            start=(ft == 0), stop=(ft == NFT - 1)),
            waits=[("ws", s_tgt)], inc="wt")
    t_sq = sq_done[(NFT - 1) % 2]
    R = TC
    mean_r = rowbuf[0:1, 0:R]
    var_r = rowbuf[0:1, R:2 * R]
    rstd_r = rowbuf[0:1, 2 * R:3 * R]
    v1 = p.step("vector", [
        lambda e: e.tensor_scalar(out=mean_r, in0=ps_bc0[0:1, :], scalar1=1.0 / DM,
                                  scalar2=None, op0=ALU.mult),
        lambda e: e.tensor_tensor(out=var_r, in0=mean_r, in1=mean_r, op=ALU.mult),
        lambda e: e.scalar_tensor_tensor(out=var_r, in0=ps_bc1[0:1, :], scalar=1.0 / DM,
                                         in1=var_r, op0=ALU.mult, op1=ALU.subtract),
    ], waits=[("wt", t_mean), ("wt", t_sq)], inc="wv")
    s1 = p.step("scalar", [
        lambda e: e.activation(out=rstd_r, in_=var_r, func=AF.Ln, bias=fconst[0:1, 3:4]),
        lambda e: e.activation(out=rstd_r, in_=rstd_r, func=AF.Exp, scale=fconst[0:1, 2:3]),
    ], waits=[("wv", v1)], inc="ws")
    t_bc = p.step("tensor", [
        lambda e: e.matmul(ps_bc0[:], ones1[:], mean_r, start=True, stop=True),
        lambda e: e.matmul(ps_bc1[:], ones1[:], rstd_r, start=True, stop=True),
    ], waits=[("ws", s1), ("wv", v1)], inc="wt")
    last_v = None
    for ft in range(NFT):
        buf = tmp1 if ft % 2 == 0 else tmp2
        w = [("wt", t_bc)] if ft < 2 else []
        w2 = out_cb(ft, None, None, probe_wait=True)
        if w2:
            w += w2
        last_v = p.step("vector", [
            lambda e, ft=ft, buf=buf: e.tensor_tensor(
                out=buf[:], in0=xf[:, ft * TC:(ft + 1) * TC], in1=ps_bc0[:], op=ALU.subtract),
            lambda e, buf=buf: e.tensor_tensor(out=buf[:], in0=buf[:], in1=ps_bc1[:], op=ALU.mult),
        ], waits=w, inc="wv")
        out_cb(ft, buf, last_v, probe_wait=False)
    return last_v


def emit_phase1(p):
    nc = p.nc
    xT = p.io["x_in"]
    x1 = p.io["x_out"]
    wqkv_img = p.dram("wqkv_img", [24, 128, DM], F32, "ExternalInput")
    wproj_img = p.dram("wproj_img", [NFT, 128, DM], F32, "ExternalInput")
    rope_cos = p.dram("rope_cos", [128, T], F32, "ExternalInput")
    rope_sin = p.dram("rope_sin", [128, T], F32, "ExternalInput")
    rot_st = p.dram("rot_st", [128, 128], F32, "ExternalInput")
    m_low_d = p.dram("m_low", [128, 128], BF16, "ExternalInput")
    m_up_d = p.dram("m_up", [128, 128], BF16, "ExternalInput")
    ident_d = p.dram("ident_bf", [128, 128], BF16, "ExternalInput")
    ones_st = p.dram("ones_st", [1, 128], F32, "ExternalInput")
    ones_col = p.dram("ones_col", [128, 1], F32, "ExternalInput")
    ones_col_bf = p.dram("ones_col_bf", [128, 1], BF16, "ExternalInput")
    fconst_d = p.dram("fconst", [1, 8], F32, "ExternalInput")

    for s in ("ld", "lw", "wv", "ws", "wt", "st"):
        p.sem(s)

    cosT = p.sb("cosT", [128, T], F32)
    sinT = p.sb("sinT", [128, T], F32)
    rotS = p.sb("rotS", [128, 128], F32)
    mlow = p.sb("mlow", [128, 128], BF16)
    mup = p.sb("mup", [128, 128], BF16)
    ident = p.sb("ident", [128, 128], BF16)
    ones1 = p.sb("ones1", [1, 128], F32)
    onesc = p.sb("onesc", [128, 1], F32)
    onescb = p.sb("onescb", [128, 1], BF16)
    fconst = p.sb("fconst_sb", [1, 8], F32)

    xf = p.sb("xf", [128, NFT * TC], F32)
    xb = p.sb("xb", [128, NFT * TC], BF16)
    kf = p.sb("kf", [128, HKV * TC], F32)
    vbf = p.sb("vbf", [128, HKV * TC], BF16)
    qn = p.sb("qn", [128, HQ * TC], BF16)
    kn_ring = p.sb("kn_ring", [128, 2 * HKV * TC], BF16)
    vt_ring = p.sb("vt_ring", [128, 2 * HKV * 4 * 128], BF16)
    Ebuf = p.sb("Ebuf", [128, 2 * TC], BF16)
    ybuf = p.sb("ybuf", [128, HQ * TC], BF16)
    qtile = p.sb("qtile", [128, TC], F32)
    tmp1 = p.sb("tmp1", [128, TC], F32)
    tmp2 = p.sb("tmp2", [128, TC], F32)
    ytmp = p.sb("ytmp", [128, TC], F32)
    rows = p.sb("rows", [1, 3 * TC], F32)
    wgrp = [p.sb(f"wgrp{i}", [128, NFT * 128], BF16) for i in range(2)]

    ps_mm = [p.ps(f"ps_mm{i}", [128, TC]) for i in range(2)]
    ps_sc = [p.ps(f"ps_sc{i}", [128, TC]) for i in range(2)]
    ps_y = p.ps("ps_y", [128, TC])
    ps_bc = [p.ps(f"ps_bc{i}", [128, TC]) for i in range(2)]
    ps_tr = p.ps("ps_tr", [128, 128], BF16)

    for dst, src in [(cosT[:], rope_cos), (sinT[:], rope_sin), (rotS[:], rot_st),
                     (mlow[:], m_low_d), (mup[:], m_up_d), (ident[:], ident_d),
                     (ones1[:], ones_st), (onesc[:], ones_col), (onescb[:], ones_col_bf),
                     (fconst[:], fconst_d)]:
        p.dma("sync", dst, src, "ld")
    const_ld = p.ctr["ld"]

    SCL = 1.0 / math.sqrt(DH)

    wbuf_free = {0: None, 1: None}
    bank_free = {0: None, 1: None}
    sc_free = {0: None, 1: None}
    ebuf_free = {0: None, 1: None}
    bc1_free = None
    out_dma = {0: None, 1: None}
    xf_free = None
    ring_ok = []
    prev_attn_done = None

    for c in range(NCH):
        t0 = c * TC
        slot = c % 2

        if c == 0:
            w = [("ld", const_ld)]
            if p.ctr["st"] > 0:
                w.append(("st", p.ctr["st"]))
        else:
            w = [("wv", xf_free)]
        ldx = None
        for ft in range(NFT):
            ldx = p.dma("sync", xf[:, ft * TC:(ft + 1) * TC],
                        xT[ft * 128:(ft + 1) * 128, t0:t0 + TC], "ld",
                        waits=(w if ft == 0 else ()))
        cast_done = None
        for ft in range(NFT):
            cast_done = p.step("vector", lambda e, ft=ft: e.tensor_copy(
                out=xb[:, ft * TC:(ft + 1) * TC], in_=xf[:, ft * TC:(ft + 1) * TC]),
                waits=([("ld", ldx)] if ft == 0 else ()), inc="wv")

        for rt in range(24):
            buf = rt % 2
            w = [] if wbuf_free[buf] is None else [("wt", wbuf_free[buf])]
            lw_tgt = p.dma("gpsimd", wgrp[buf][:], wqkv_img[rt], "lw", waits=w)
            bank = rt % 2
            w = [("lw", lw_tgt), ("wv", cast_done)]
            if bank_free[bank] is not None:
                w.append(("wv", bank_free[bank]))
            mms = [lambda e, ci=ci, buf=buf, bank=bank: e.matmul(
                ps_mm[bank][:], wgrp[buf][:, ci * 128:(ci + 1) * 128],
                xb[:, ci * TC:(ci + 1) * TC], start=(ci == 0), stop=(ci == NFT - 1))
                for ci in range(NFT)]
            mm_tgt = p.step("tensor", mms, waits=w, inc="wt")
            wbuf_free[buf] = mm_tgt
            if rt < 20:
                if rt < 16:
                    ev = p.step("vector", lambda e, bank=bank: e.tensor_copy(
                        out=qtile[:], in_=ps_mm[bank][:]), waits=[("wt", mm_tgt)], inc="wv")
                    src = qtile
                    srcoff = 0
                else:
                    ev = p.step("vector", lambda e, bank=bank, rt=rt: e.tensor_copy(
                        out=kf[:, (rt - 16) * TC:(rt - 15) * TC], in_=ps_mm[bank][:]),
                        waits=[("wt", mm_tgt)], inc="wv")
                    src = kf
                    srcoff = (rt - 16) * TC
                bank_free[bank] = ev
                t_rot = p.step("tensor", lambda e, src=src, srcoff=srcoff: e.matmul(
                    ps_bc[0][:], rotS[:], src[:, srcoff:srcoff + TC], start=True, stop=True),
                    waits=[("wv", ev)], inc="wt")
                v_rope = p.step("vector", [
                    lambda e, src=src, srcoff=srcoff, t0=t0: e.tensor_tensor(
                        out=tmp1[:], in0=src[:, srcoff:srcoff + TC],
                        in1=cosT[:, t0:t0 + TC], op=ALU.mult),
                    lambda e, t0=t0: e.tensor_tensor(out=tmp2[:], in0=ps_bc[0][:],
                                                     in1=sinT[:, t0:t0 + TC], op=ALU.mult),
                    lambda e: e.tensor_tensor(out=tmp1[:], in0=tmp1[:], in1=tmp2[:], op=ALU.add),
                ], waits=[("wt", t_rot)], inc="wv")
                s_sq = p.step("scalar", lambda e: e.activation(out=tmp2[:], in_=tmp1[:], func=AF.Square),
                              waits=[("wv", v_rope)], inc="ws")
                t_ss = p.step("tensor", lambda e: e.matmul(ps_bc[0][0:1, :], onesc[:], tmp2[:],
                                                           start=True, stop=True),
                              waits=[("ws", s_sq)], inc="wt")
                s_rstd = p.step("scalar", [
                    lambda e: e.activation(out=rows[0:1, 0:TC], in_=ps_bc[0][0:1, :],
                                           func=AF.Ln, scale=fconst[0:1, 1:2], bias=fconst[0:1, 0:1]),
                    lambda e: e.activation(out=rows[0:1, 0:TC], in_=rows[0:1, 0:TC],
                                           func=AF.Exp, scale=fconst[0:1, 2:3]),
                ], waits=[("wt", t_ss)], inc="ws")
                t_bc = p.step("tensor", lambda e: e.matmul(ps_bc[0][:], ones1[:], rows[0:1, 0:TC],
                                                           start=True, stop=True),
                              waits=[("ws", s_rstd)], inc="wt")
                if rt < 16:
                    dst = qn[:, rt * TC:(rt + 1) * TC]
                    scl = SCL
                else:
                    dst = kn_ring[:, (slot * HKV + (rt - 16)) * TC:(slot * HKV + (rt - 16) + 1) * TC]
                    scl = 1.0
                v_qn = p.step("vector", lambda e, dst=dst, scl=scl: e.scalar_tensor_tensor(
                    out=dst, in0=tmp1[:], scalar=scl, in1=ps_bc[0][:], op0=ALU.mult, op1=ALU.mult),
                    waits=[("wt", t_bc)], inc="wv")
            else:
                ev = p.step("vector", lambda e, bank=bank, rt=rt: e.tensor_copy(
                    out=vbf[:, (rt - 20) * TC:(rt - 19) * TC], in_=ps_mm[bank][:]),
                    waits=[("wt", mm_tgt)], inc="wv")
                bank_free[bank] = ev

        v_stageQ = p.ctr["wv"]

        tr_evac = None
        for g in range(HKV):
            for kb in range(4):
                w = [("wv", v_stageQ)] if (g == 0 and kb == 0) else [("wv", tr_evac)]
                t_tr = p.step("tensor", lambda e, g=g, kb=kb: e.transpose(
                    ps_tr[:], vbf[:, g * TC + kb * 128:g * TC + (kb + 1) * 128], ident[:]),
                    waits=w, inc="wt")
                tr_evac = p.step("vector", lambda e, g=g, kb=kb, slot=slot: e.tensor_copy(
                    out=vt_ring[:, ((slot * HKV + g) * 4 + kb) * 128:((slot * HKV + g) * 4 + kb + 1) * 128],
                    in_=ps_tr[:]), waits=[("wt", t_tr)], inc="wv")
        ring_ok.append(p.ctr["wv"])

        kb_list = [kb for kb in range(4 * c - 4, 4 * c + 4) if kb >= 0]

        def kb_addr(kb):
            cc = kb // 4
            sl = cc % 2
            return sl, kb % 4

        for g in range(HKV):
            for qh in range(4 * g, 4 * g + 4):
                av_done = None
                first = True
                for j, kb in enumerate(kb_list):
                    sl, kbl = kb_addr(kb)
                    scb = j % 2
                    eb = j % 2
                    w = [("wv", ring_ok[c])]
                    if sc_free[scb] is not None:
                        w.append(("ws", sc_free[scb]))
                    kslice = kn_ring[:, (sl * HKV + g) * TC + kbl * 128:(sl * HKV + g) * TC + (kbl + 1) * 128]
                    t_sc = p.step("tensor", lambda e, scb=scb, kslice=kslice, qh=qh: e.matmul(
                        ps_sc[scb][:], kslice, qn[:, qh * TC:(qh + 1) * TC], start=True, stop=True),
                        waits=w, inc="wt")
                    mask_fns = []
                    inval = []
                    for qb in range(4):
                        wdist = 4 * c + qb - kb
                        if wdist == 0:
                            mask_fns.append(lambda e, scb=scb, qb=qb: e.tensor_tensor(
                                out=ps_sc[scb][:, qb * 128:(qb + 1) * 128],
                                in0=ps_sc[scb][:, qb * 128:(qb + 1) * 128], in1=mlow[:], op=ALU.add))
                        elif wdist == 4:
                            mask_fns.append(lambda e, scb=scb, qb=qb: e.tensor_tensor(
                                out=ps_sc[scb][:, qb * 128:(qb + 1) * 128],
                                in0=ps_sc[scb][:, qb * 128:(qb + 1) * 128], in1=mup[:], op=ALU.add))
                        elif wdist < 0 or wdist > 4:
                            inval.append(qb)
                    v_mask = None
                    if mask_fns:
                        v_mask = p.step("vector", mask_fns, waits=[("wt", t_sc)], inc="wv")
                    w = [("wv", v_mask)] if v_mask is not None else [("wt", t_sc)]
                    if ebuf_free[eb] is not None:
                        w.append(("wt", ebuf_free[eb]))
                    s_exp = p.step("scalar", lambda e, scb=scb, eb=eb: e.activation(
                        out=Ebuf[:, eb * TC:(eb + 1) * TC], in_=ps_sc[scb][:], func=AF.Exp),
                        waits=w, inc="ws")
                    sc_free[scb] = s_exp
                    v_zero = None
                    if inval:
                        v_zero = p.step("vector", [
                            lambda e, eb=eb, qb=qb: e.memset(
                                Ebuf[:, eb * TC + qb * 128:eb * TC + (qb + 1) * 128], 0.0)
                            for qb in inval], waits=[("ws", s_exp)], inc="wv")
                    w = [("wv", v_zero)] if v_zero is not None else [("ws", s_exp)]
                    if first and bc1_free is not None:
                        w.append(("wv", bc1_free))
                    vslice = vt_ring[:, ((sl * HKV + g) * 4 + kbl) * 128:((sl * HKV + g) * 4 + kbl + 1) * 128]
                    last = (j == len(kb_list) - 1)
                    av_done = p.step("tensor", [
                        lambda e, eb=eb, first=first, last=last: e.matmul(
                            ps_bc[1][0:1, :], onescb[:], Ebuf[:, eb * TC:(eb + 1) * TC],
                            start=first, stop=last),
                        lambda e, eb=eb, vslice=vslice, first=first, last=last: e.matmul(
                            ps_y[:], vslice, Ebuf[:, eb * TC:(eb + 1) * TC],
                            start=first, stop=last),
                    ], waits=w, inc="wt")
                    ebuf_free[eb] = av_done
                    first = False
                v_rec = p.step("vector", [
                    lambda e: e.reciprocal(out=rows[0:1, TC:2 * TC], in_=ps_bc[1][0:1, :]),
                    lambda e: e.tensor_copy(out=ytmp[:], in_=ps_y[:]),
                ], waits=[("wt", av_done)], inc="wv")
                t_dbc = p.step("tensor", lambda e: e.matmul(
                    ps_bc[1][:], ones1[:], rows[0:1, TC:2 * TC], start=True, stop=True),
                    waits=[("wv", v_rec)], inc="wt")
                bc1_free = p.step("vector", lambda e, qh=qh: e.tensor_tensor(
                    out=ybuf[:, qh * TC:(qh + 1) * TC], in0=ytmp[:], in1=ps_bc[1][:], op=ALU.mult),
                    waits=[("wt", t_dbc)], inc="wv")

        v_attn = p.ctr["wv"]

        res_done = None
        for ot in range(NFT):
            buf = ot % 2
            w = [("wt", wbuf_free[buf])] if wbuf_free[buf] is not None else []
            lw_tgt = p.dma("gpsimd", wgrp[buf][:], wproj_img[ot], "lw", waits=w)
            bank = ot % 2
            w = [("lw", lw_tgt), ("wv", v_attn)]
            if bank_free[bank] is not None:
                w.append(("wv", bank_free[bank]))
            mms = [lambda e, dt=dt, buf=buf, bank=bank: e.matmul(
                ps_mm[bank][:], wgrp[buf][:, dt * 128:(dt + 1) * 128],
                ybuf[:, dt * TC:(dt + 1) * TC], start=(dt == 0), stop=(dt == NFT - 1))
                for dt in range(NFT)]
            mm_tgt = p.step("tensor", mms, waits=w, inc="wt")
            wbuf_free[buf] = mm_tgt
            res_done = p.step("vector", lambda e, ot=ot, bank=bank: e.tensor_tensor(
                out=xf[:, ot * TC:(ot + 1) * TC], in0=xf[:, ot * TC:(ot + 1) * TC],
                in1=ps_mm[bank][:], op=ALU.add), waits=[("wt", mm_tgt)], inc="wv")
            bank_free[bank] = res_done

        def out_cb(ft, buf, v_tgt, probe_wait):
            if probe_wait:
                d = out_dma[ft % 2]
                return [("st", d)] if d is not None else []
            out_dma[ft % 2] = p.dma("sync", x1[ft * 128:(ft + 1) * 128, t0:t0 + TC],
                                    buf[:], "st", waits=[("wv", v_tgt)])
            return None

        xf_free = emit_ln_chunk(p, xf, None, ps_bc[0], ps_bc[1], rows, tmp1, tmp2,
                                out_cb, onesc, ones1, fconst,
                                wt_after=[("wv", res_done)], tag=f"ln{c}")

    return


DH_FFN = 8192
NHT = DH_FFN // 128


def emit_phase2(p):
    nc = p.nc
    xT = p.io["x_in"]
    x2 = p.io["x_out"]
    w1_img = p.dram("w1_img", [NHT, 128, DM], F32, "ExternalInput")
    w3_img = p.dram("w3_img", [NHT, 128, DM], F32, "ExternalInput")
    w2_img = p.dram("w2_img", [NFT, 128, DH_FFN], F32, "ExternalInput")
    h128_d = p.dram("h128_bf", [128, 128], BF16, "ExternalInput")
    ones_st = p.dram("ones_st", [1, 128], F32, "ExternalInput")
    ones_col = p.dram("ones_col", [128, 1], F32, "ExternalInput")
    fconst_d = p.dram("fconst", [1, 8], F32, "ExternalInput")
    w2h = p.dram("w2h", [NFT, 128, DH_FFN], BF16, "Internal")

    for s in ("ld", "lw", "wv", "ws", "wt", "st"):
        p.sem(s)

    h128 = p.sb("h128", [128, 128], BF16)
    ones1 = p.sb("ones1", [1, 128], F32)
    onesc = p.sb("onesc", [128, 1], F32)
    fconst = p.sb("fconst_sb", [1, 8], F32)

    xf = p.sb("xf", [128, NFT * TC], F32)
    xb = p.sb("xb", [128, NFT * TC], BF16)
    hidH = p.sb("hidH", [128, NHT * TC], BF16)
    hidt = p.sb("hidt", [128, TC], BF16)
    tmpS = p.sb("tmpS", [128, TC], F32)
    tmp1 = p.sb("tmp1", [128, TC], F32)
    tmp2 = p.sb("tmp2", [128, TC], F32)
    rows = p.sb("rows", [1, 4 * TC], F32)
    w1g = [p.sb(f"w1g{i}", [128, NFT * 128], BF16) for i in range(2)]
    w3g = [p.sb(f"w3g{i}", [128, NFT * 128], BF16) for i in range(2)]
    w2g = p.sb("w2g", [128, NHT * 128], BF16)
    wbig = [p.sb(f"wbig{i}", [128, NHT * 128], BF16) for i in range(2)]

    ps_g = [p.ps(f"ps_g{i}", [128, TC]) for i in range(2)]
    ps_l = [p.ps(f"ps_l{i}", [128, TC]) for i in range(2)]
    ps_h = p.ps("ps_h", [128, TC])
    ps_bc = [p.ps(f"ps_bc{i}", [128, TC]) for i in range(2)]

    for dst, src in [(h128[:], h128_d), (ones1[:], ones_st), (onesc[:], ones_col),
                     (fconst[:], fconst_d)]:
        p.dma("sync", dst, src, "ld")
    const_ld = p.ctr["ld"]

    IS = 1.0 / math.sqrt(DH_FFN)
    st_prev = None
    v_last_bfly = None
    for ob in range(NFT):
        w = [] if v_last_bfly is None else [("wv", v_last_bfly)]
        if st_prev is not None:
            w.append(("st", st_prev))
        lw = p.dma("gpsimd", wbig[0][:], w2_img[ob], "lw", waits=w)
        for s in range(6):
            src = wbig[s % 2]
            dstb = wbig[(s + 1) % 2]
            d = 1 << s
            fns = []
            for i in range(NHT):
                j = i ^ d
                lo = min(i, j)
                op = ALU.add if i == lo else ALU.subtract
                a = lo * 128
                b = (lo ^ d) * 128
                fns.append(lambda e, i=i, a=a, b=b, op=op, src=src, dstb=dstb: e.tensor_tensor(
                    out=dstb[:, i * 128:(i + 1) * 128],
                    in0=src[:, a:a + 128], in1=src[:, b:b + 128], op=op))
            w = [("lw", lw)] if s == 0 else []
            v_last_bfly = p.step("vector", fns, waits=w, inc="wv")
        st_prev = p.dma("sync", w2h[ob], wbig[0][:], "st", waits=[("wv", v_last_bfly)])
    w2h_ready = st_prev

    w1buf_free = {0: None, 1: None}
    w3buf_free = {0: None, 1: None}
    w2buf_free = None
    g_free = {0: None, 1: None}
    l_free = {0: None, 1: None}
    h_free = None
    hidt_free = None
    tmpS_free = None
    out_dma = {0: None, 1: None}
    xf_free = None

    for c in range(NCH):
        t0 = c * TC

        if c == 0:
            w = [("ld", const_ld)]
            if p.ctr["st"] > 0:
                w.append(("st", p.ctr["st"]))
        else:
            w = [("wv", xf_free)]
        ldx = None
        for ft in range(NFT):
            ldx = p.dma("sync", xf[:, ft * TC:(ft + 1) * TC],
                        xT[ft * 128:(ft + 1) * 128, t0:t0 + TC], "ld",
                        waits=(w if ft == 0 else ()))
        cast_done = None
        for ft in range(NFT):
            cast_done = p.step("vector", lambda e, ft=ft: e.tensor_copy(
                out=xb[:, ft * TC:(ft + 1) * TC], in_=xf[:, ft * TC:(ft + 1) * TC]),
                waits=([("ld", ldx)] if ft == 0 else ()), inc="wv")

        ev_hid = None
        for ht in range(NHT):
            buf = ht % 2
            bank = ht % 2
            w = [] if w1buf_free[buf] is None else [("wt", w1buf_free[buf])]
            lw1 = p.dma("gpsimd", w1g[buf][:], w1_img[ht], "lw", waits=w)
            w = [] if w3buf_free[buf] is None else [("wt", w3buf_free[buf])]
            lw3 = p.dma("gpsimd", w3g[buf][:], w3_img[ht], "lw", waits=w)
            w = [("lw", lw1), ("wv", cast_done)]
            if g_free[bank] is not None:
                w.append(("ws", g_free[bank]))
            mmG = p.step("tensor", [lambda e, ci=ci, buf=buf, bank=bank: e.matmul(
                ps_g[bank][:], w1g[buf][:, ci * 128:(ci + 1) * 128],
                xb[:, ci * TC:(ci + 1) * TC], start=(ci == 0), stop=(ci == NFT - 1))
                for ci in range(NFT)], waits=w, inc="wt")
            w1buf_free[buf] = mmG
            w = [("lw", lw3)]
            if l_free[bank] is not None:
                w.append(("wv", l_free[bank]))
            mmL = p.step("tensor", [lambda e, ci=ci, buf=buf, bank=bank: e.matmul(
                ps_l[bank][:], w3g[buf][:, ci * 128:(ci + 1) * 128],
                xb[:, ci * TC:(ci + 1) * TC], start=(ci == 0), stop=(ci == NFT - 1))
                for ci in range(NFT)], waits=w, inc="wt")
            w3buf_free[buf] = mmL
            w = [("wt", mmG)]
            if tmpS_free is not None:
                w.append(("wv", tmpS_free))
            s_silu = p.step("scalar", lambda e, bank=bank: e.activation(
                out=tmpS[:], in_=ps_g[bank][:], func=AF.Silu), waits=w, inc="ws")
            g_free[bank] = s_silu
            w = [("ws", s_silu), ("wt", mmL)]
            if hidt_free is not None:
                w.append(("wt", hidt_free))
            v_mul = p.step("vector", lambda e, bank=bank: e.tensor_tensor(
                out=hidt[:], in0=tmpS[:], in1=ps_l[bank][:], op=ALU.mult),
                waits=w, inc="wv")
            tmpS_free = v_mul
            l_free[bank] = v_mul
            w = [("wv", v_mul)]
            if h_free is not None:
                w.append(("wv", h_free))
            mmH = p.step("tensor", lambda e: e.matmul(
                ps_h[:], h128[:], hidt[:], start=True, stop=True), waits=w, inc="wt")
            hidt_free = mmH
            ev_hid = p.step("vector", lambda e, ht=ht: e.tensor_scalar(
                out=hidH[:, ht * TC:(ht + 1) * TC], in0=ps_h[:], scalar1=IS,
                scalar2=None, op0=ALU.mult), waits=[("wt", mmH)], inc="wv")
            h_free = ev_hid

        res_done = None
        for ot in range(NFT):
            w = [("st", w2h_ready)] if (c == 0 and ot == 0) else []
            if w2buf_free is not None:
                w.append(("wt", w2buf_free))
            lw2a = p.dma("gpsimd", w2g[:, 0:4096], w2h[ot][:, 0:4096], "lw", waits=w)
            w = [("wt", w2buf_free)] if w2buf_free is not None else []
            lw2b = p.dma("gpsimd", w2g[:, 4096:8192], w2h[ot][:, 4096:8192], "lw", waits=w)
            bank = ot % 2
            w = [("lw", lw2a), ("wv", ev_hid)]
            if g_free[bank] is not None:
                w.append(("ws", g_free[bank]))
            if l_free[bank] is not None:
                w.append(("wv", l_free[bank]))
            mm_a = p.step("tensor", [lambda e, kt=kt, bank=bank: e.matmul(
                ps_g[bank][:], w2g[:, kt * 128:(kt + 1) * 128],
                hidH[:, kt * TC:(kt + 1) * TC], start=(kt == 0), stop=False)
                for kt in range(NHT // 2)], waits=w, inc="wt")
            mm = p.step("tensor", [lambda e, kt=kt, bank=bank: e.matmul(
                ps_g[bank][:], w2g[:, kt * 128:(kt + 1) * 128],
                hidH[:, kt * TC:(kt + 1) * TC], start=False, stop=(kt == NHT - 1))
                for kt in range(NHT // 2, NHT)], waits=[("lw", lw2b)], inc="wt")
            w2buf_free = mm
            res_done = p.step("vector", lambda e, ot=ot, bank=bank: e.tensor_tensor(
                out=xf[:, ot * TC:(ot + 1) * TC], in0=xf[:, ot * TC:(ot + 1) * TC],
                in1=ps_g[bank][:], op=ALU.add), waits=[("wt", mm)], inc="wv")
            g_free[bank] = None
            l_free[bank] = None
            l_free[bank] = res_done

        if DEBUG_DUMP and c == 0:
            dbg_xf = nc.dram_tensor("dbg_xf", [128, NFT * TC], F32, kind="ExternalOutput").ap()
            p.dma("sync", dbg_xf, xf[:], "st", waits=[("wv", res_done)])

        def out_cb(ft, buf, v_tgt, probe_wait):
            if probe_wait:
                d = out_dma[ft % 2]
                return [("st", d)] if d is not None else []
            out_dma[ft % 2] = p.dma("sync", x2[ft * 128:(ft + 1) * 128, t0:t0 + TC],
                                    buf[:], "st", waits=[("wv", v_tgt)])
            return None

        xf_free = emit_ln_chunk(p, xf, None, ps_bc[0], ps_bc[1], rows, tmp1, tmp2,
                                out_cb, onesc, ones1, fconst,
                                wt_after=[("wv", res_done)], tag=f"l2ln{c}")

    return


def run_launch2(x1T_list, inputs, n_cores=8):
    c = host_consts()
    h = np.zeros((128, 128), np.float64)
    for i in range(128):
        for j in range(128):
            h[i, j] = 1.0 if bin(i & j).count("1") % 2 == 0 else -1.0
    base = {
        "w1_img": tile_img(inputs["L0_ffn_W1"].T.astype(np.float32)),
        "w3_img": tile_img(inputs["L0_ffn_W3"].T.astype(np.float32)),
        "w2_img": tile_img(inputs["L0_ffn_W2"].T.astype(np.float32)),
        "h128_bf": h.astype(BF),
        "ones_st": c["ones_st"], "ones_col": c["ones_col"], "fconst": c["fconst"],
    }
    nc = build_single(2, "x2")
    in_maps = []
    for xT in x1T_list:
        m = dict(base)
        m["xT"] = xT
        in_maps.append(m)
    res = run_bass_kernel_spmd(nc, in_maps, core_ids=list(range(n_cores)))
    if DEBUG_DUMP:
        return [np.asarray(r["x2"], dtype=np.float32) for r in res.results], res.results
    return [np.asarray(r["x2"], dtype=np.float32) for r in res.results]


DIN = 4096
NDT = DIN // 128
DST = 16


def emit_phase3(p):
    nc = p.nc
    xT = p.io["x_in"]
    x3 = p.io["x_out"]
    win_img = p.dram("win_img", [64, 128, DM], F32, "ExternalInput")
    outw_img = p.dram("outw_img", [NFT, 128, DIN], F32, "ExternalInput")
    conv_cols_d = p.dram("conv_cols", [128, NDT * 4], F32, "ExternalInput")
    convb_d = p.dram("convb_cols", [128, NDT], F32, "ExternalInput")
    wxb_d = p.dram("wxb_cols", [128, NDT * DST], F32, "ExternalInput")
    wxc_d = p.dram("wxc_cols", [128, NDT * DST], F32, "ExternalInput")
    wxdt_d = p.dram("wxdt_cols", [128, NDT], F32, "ExternalInput")
    a2_d = p.dram("a2_cols", [128, NDT * DST], F32, "ExternalInput")
    dtb_d = p.dram("dtb_cols", [128, NDT], F32, "ExternalInput")
    dcols_d = p.dram("d_cols", [128, NDT], F32, "ExternalInput")
    ones_st = p.dram("ones_st", [1, 128], F32, "ExternalInput")
    ones_stb_d = p.dram("ones_st_bf", [1, 128], BF16, "ExternalInput")
    ones_col = p.dram("ones_col", [128, 1], F32, "ExternalInput")
    fconst_d = p.dram("fconst", [1, 8], F32, "ExternalInput")
    sel16_d = p.dram("sel16", [DST, DST * 128], BF16, "ExternalInput")
    zscr = p.dram("z_scr", [DIN, T], BF16, "Internal")

    for s in ("ld", "lw", "wv", "ws", "wt", "st"):
        p.sem(s)

    ones1 = p.sb("ones1", [1, 128], F32)
    ones1b = p.sb("ones1b", [1, 128], BF16)
    sel16 = p.sb("sel16_sb", [DST, DST * 128], BF16)
    onesc = p.sb("onesc", [128, 1], F32)
    fconst = p.sb("fconst_sb", [1, 8], F32)
    conv_cols = p.sb("conv_cols_sb", [128, NDT * 4], F32)
    convb = p.sb("convb_sb", [128, NDT], F32)
    wxb = p.sb("wxb", [128, NDT * DST], BF16)
    wxc = p.sb("wxc", [128, NDT * DST], BF16)
    wxdt = p.sb("wxdt", [128, NDT], BF16)
    a2 = p.sb("a2", [128, NDT * DST], F32)
    dtb = p.sb("dtb", [128, NDT], F32)
    dcols = p.sb("dcols", [128, NDT], F32)
    mb = p.sb("mb", [1, 1], F32)

    xf = p.sb("xf", [128, NFT * TC], F32)
    xb = p.sb("xb", [128, NFT * TC], BF16)
    xcb = p.sb("xcb", [128, NDT * TC], BF16)
    dtBrep = p.sb("dtBrep", [128, DST * TC], BF16)
    crepA = p.sb("crepA", [128, DST * TC], BF16)
    zst = [p.sb(f"zst{i}", [128, TC], BF16) for i in range(2)]
    hist = p.sb("hist", [128, NDT * 3], F32)
    xs_tmp = p.sb("xs_tmp", [128, TC + 3], F32)
    convacc = p.sb("convacc", [128, TC], F32)
    B_sb = p.sb("B_sb", [DST, TC], BF16)
    C_sb = p.sb("C_sb", [DST, TC], BF16)
    dtB_sb = p.sb("dtB_sb", [DST, TC], BF16)
    rowm = p.sb("rowm", [1, 3 * TC], F32)
    carry = p.sb("carry", [128, 2 * NDT * DST], F32)
    abar = [p.sb(f"abar{i}", [128, TC], F32) for i in range(2)]
    u_sb = [p.sb(f"u_sb{i}", [128, TC], BF16) for i in range(2)]
    hbf = [p.sb(f"hbf{i}", [128, TC], BF16) for i in range(2)]
    wbf = [p.sb(f"wbf{i}", [128, TC], BF16) for i in range(2)]
    h_tmp = p.sb("h_tmp", [128, TC], F32)
    w_tmp = p.sb("w_tmp", [128, TC], F32)
    yacc = p.sb("yacc", [128, TC], F32)
    ytmp2 = p.sb("ytmp2", [128, TC], F32)
    wgi = [p.sb("wgi0", [128, DM], BF16)] * 2
    wgo = [p.sb("wgo0", [128, DIN], BF16)] * 2

    ps_mm = [p.ps(f"ps_mm{i}", [128, TC]) for i in range(2)]
    ps_B = p.ps("ps_B", [DST, TC])
    ps_C = p.ps("ps_C", [DST, TC])
    ps_dt = p.ps("ps_dt", [1, TC])
    ps_bc = [p.ps(f"ps_bc{i}", [128, TC]) for i in range(3)]

    for dst, src in [(ones1[:], ones_st), (ones1b[:], ones_stb_d), (onesc[:], ones_col),
                     (fconst[:], fconst_d),
                     (conv_cols[:], conv_cols_d), (convb[:], convb_d),
                     (a2[:], a2_d), (dtb[:], dtb_d), (dcols[:], dcols_d),
                     (sel16[:], sel16_d)]:
        p.dma("sync", dst, src, "ld")
    for dst, src in [(wxb[:], wxb_d), (wxc[:], wxc_d), (wxdt[:], wxdt_d)]:
        p.dma("gpsimd", dst, src, "ld")
    const_ld = p.ctr["ld"]

    v0 = p.step("vector", [
        lambda e: e.memset(carry[:], 0.0),
        lambda e: e.memset(hist[:], 0.0),
        lambda e: e.reduce_sum(out=h_tmp[:, 0:1], in_=dtb[:], axis=AX.X),
    ], waits=[("ld", const_ld)], inc="wv")
    s0 = p.step("scalar", lambda e: e.activation(out=a2[:], in_=a2[:], func=AF.Exp),
                waits=[("ld", const_ld)], inc="ws")
    v0b = p.step("vector", lambda e: e.tensor_scalar(out=a2[:], in0=a2[:], scalar1=-1.0,
                                                     scalar2=None, op0=ALU.mult),
                 waits=[("ws", s0)], inc="wv")
    t0_ = p.step("tensor", lambda e: e.matmul(ps_dt[0:1, 0:1], onesc[:], h_tmp[:, 0:1],
                                              start=True, stop=True),
                 waits=[("wv", v0)], inc="wt")
    v0c = p.step("vector", lambda e: e.tensor_scalar(out=mb[:], in0=ps_dt[0:1, 0:1],
                                                     scalar1=1.0 / DIN, scalar2=None,
                                                     op0=ALU.mult),
                 waits=[("wt", t0_)], inc="wv")

    wibuf_free = {0: None, 1: None}
    wobuf_free = {0: None, 1: None}
    zst_free = {0: None, 1: None}
    zld_free = {0: None, 1: None}
    bank_busy = {0: None, 1: None}
    convacc_free = None
    bc0_sfree = None
    bc1_free = None
    bc2_free = None
    out_dma = {0: None, 1: None}
    xf_free = None

    for c in range(NCH):
        t0 = c * TC
        if c == 0:
            w = [("wv", v0c), ("ld", const_ld)]
            if p.ctr["st"] > 0:
                w.append(("st", p.ctr["st"]))
        else:
            w = [("wv", xf_free)]
        ldx = None
        for ft in range(NFT):
            ldx = p.dma("sync", xf[:, ft * TC:(ft + 1) * TC],
                        xT[ft * 128:(ft + 1) * 128, t0:t0 + TC], "ld",
                        waits=(w if ft == 0 else ()))
        cast_done = None
        for ft in range(NFT):
            cast_done = p.step("vector", lambda e, ft=ft: e.tensor_copy(
                out=xb[:, ft * TC:(ft + 1) * TC], in_=xf[:, ft * TC:(ft + 1) * TC]),
                waits=([("ld", ldx)] if ft == 0 else ()), inc="wv")

        silu_last = None
        for rt in range(64):
            buf = 0
            bank = rt % 2
            w = [] if wibuf_free[buf] is None else [("wt", wibuf_free[buf])]
            lw_t = p.dma("gpsimd", wgi[buf][:], win_img[rt], "lw", waits=w)
            w = [("lw", lw_t), ("wv", cast_done)]
            if bank_busy[bank] is not None:
                w.append(bank_busy[bank])
            mms = [lambda e, ci=ci, buf=buf, bank=bank: e.matmul(
                ps_mm[bank][:], wgi[buf][:, ci * 128:(ci + 1) * 128],
                xb[:, ci * TC:(ci + 1) * TC], start=(ci == 0), stop=(ci == NFT - 1))
                for ci in range(NFT)]
            mm = p.step("tensor", mms, waits=w, inc="wt")
            wibuf_free[buf] = mm
            if rt < NDT:
                dh = rt
                w = [("wt", mm)]
                if convacc_free is not None:
                    w.append(("ws", convacc_free))
                v_conv = p.step("vector", [
                    lambda e, dh=dh: e.tensor_copy(out=xs_tmp[:, 0:3], in_=hist[:, dh * 3:dh * 3 + 3]),
                    lambda e, bank=bank: e.tensor_copy(out=xs_tmp[:, 3:TC + 3], in_=ps_mm[bank][:]),
                    lambda e, dh=dh, bank=bank: e.tensor_copy(out=hist[:, dh * 3:dh * 3 + 3],
                                                              in_=ps_mm[bank][:, TC - 3:TC]),
                    lambda e, dh=dh: e.tensor_scalar(
                        out=convacc[:], in0=xs_tmp[:, 0:TC],
                        scalar1=conv_cols[:, dh * 4:dh * 4 + 1],
                        scalar2=convb[:, dh:dh + 1], op0=ALU.mult, op1=ALU.add),
                    lambda e, dh=dh: e.scalar_tensor_tensor(
                        out=convacc[:], in0=xs_tmp[:, 1:TC + 1],
                        scalar=conv_cols[:, dh * 4 + 1:dh * 4 + 2], in1=convacc[:],
                        op0=ALU.mult, op1=ALU.add),
                    lambda e, dh=dh: e.scalar_tensor_tensor(
                        out=convacc[:], in0=xs_tmp[:, 2:TC + 2],
                        scalar=conv_cols[:, dh * 4 + 2:dh * 4 + 3], in1=convacc[:],
                        op0=ALU.mult, op1=ALU.add),
                    lambda e, dh=dh: e.scalar_tensor_tensor(
                        out=convacc[:], in0=xs_tmp[:, 3:TC + 3],
                        scalar=conv_cols[:, dh * 4 + 3:dh * 4 + 4], in1=convacc[:],
                        op0=ALU.mult, op1=ALU.add),
                ], waits=w, inc="wv")
                bank_busy[bank] = ("wv", v_conv)
                silu_last = p.step("scalar", lambda e, dh=dh: e.activation(
                    out=xcb[:, dh * TC:(dh + 1) * TC], in_=convacc[:], func=AF.Silu),
                    waits=[("wv", v_conv)], inc="ws")
                convacc_free = silu_last
            else:
                dh = rt - NDT
                zb = dh % 2
                w = [("wt", mm)]
                if zst_free[zb] is not None:
                    w.append(("st", zst_free[zb]))
                silu_last = p.step("scalar", lambda e, zb=zb, bank=bank: e.activation(
                    out=zst[zb][:], in_=ps_mm[bank][:], func=AF.Silu),
                    waits=w, inc="ws")
                bank_busy[bank] = ("ws", silu_last)
                zst_free[zb] = p.dma("sync", zscr[dh * 128:(dh + 1) * 128, t0:t0 + TC],
                                     zst[zb][:], "st", waits=[("ws", silu_last)])

        w = [("ws", silu_last)]
        mmB = p.step("tensor", [lambda e, d=d: e.matmul(
            ps_B[:], wxb[:, d * DST:(d + 1) * DST], xcb[:, d * TC:(d + 1) * TC],
            start=(d == 0), stop=(d == NDT - 1)) for d in range(NDT)],
            waits=w, inc="wt")
        mmC = p.step("tensor", [lambda e, d=d: e.matmul(
            ps_C[:], wxc[:, d * DST:(d + 1) * DST], xcb[:, d * TC:(d + 1) * TC],
            start=(d == 0), stop=(d == NDT - 1)) for d in range(NDT)],
            inc="wt")
        mmD = p.step("tensor", [lambda e, d=d: e.matmul(
            ps_dt[:], wxdt[:, d:d + 1], xcb[:, d * TC:(d + 1) * TC],
            start=(d == 0), stop=(d == NDT - 1)) for d in range(NDT)],
            inc="wt")
        v_bc = p.step("vector", [
            lambda e: e.tensor_copy(out=B_sb[:], in_=ps_B[:]),
            lambda e: e.tensor_copy(out=C_sb[:], in_=ps_C[:]),
        ], waits=[("wt", mmB), ("wt", mmC), ("wt", mmD)], inc="wv")
        abs_r = rowm[0:1, 0:TC]
        ln_r = rowm[0:1, TC:2 * TC]
        relu_r = rowm[0:1, 0:TC]
        dt_row = rowm[0:1, 2 * TC:3 * TC]
        s_dt = p.step("scalar", [
            lambda e: e.activation(out=abs_r, in_=ps_dt[0:1, :], func=AF.Abs, bias=mb[0:1, 0:1]),
            lambda e: e.activation(out=abs_r, in_=abs_r, func=AF.Exp, scale=fconst[0:1, 4:5]),
            lambda e: e.activation(out=ln_r, in_=abs_r, func=AF.Ln, bias=1.0),
            lambda e: e.activation(out=relu_r, in_=ps_dt[0:1, :], func=AF.Relu, bias=mb[0:1, 0:1]),
        ], waits=[("wt", mmD)], inc="ws")
        v_dt = p.step("vector", lambda e: e.tensor_tensor(out=dt_row, in0=relu_r, in1=ln_r,
                                                          op=ALU.add),
                      waits=[("ws", s_dt)], inc="wv")
        w = [("wv", v_dt)]
        if bc1_free is not None:
            w.append(("wv", bc1_free))
        t_d16 = p.step("tensor", lambda e: e.matmul(ps_bc[1][0:DST, :], ones1[0:1, 0:DST],
                                                    dt_row, start=True, stop=True),
                       waits=w, inc="wt")
        v_dtb = p.step("vector", lambda e: e.tensor_tensor(
            out=dtB_sb[:], in0=B_sb[:], in1=ps_bc[1][0:DST, :], op=ALU.mult),
            waits=[("wt", t_d16), ("wv", v_bc)], inc="wv")
        bc1_free = v_dtb
        w = [("wv", v_dt)]
        if bc0_sfree is not None:
            w.append(("ws", bc0_sfree))
        t_dtrep = p.step("tensor", lambda e: e.matmul(ps_bc[0][:], ones1[:], dt_row,
                                                      start=True, stop=True),
                         waits=w, inc="wt")
        cache_done = None
        for si in range(DST):
            if si == 0:
                w = [("wv", v_dtb)]
                if bc1_free is not None:
                    w.append(("wv", bc1_free))
            else:
                w = [("wv", cache_done)]
            tb1 = p.step("tensor", lambda e, si=si: e.matmul(
                ps_bc[1][:], sel16[:, si * 128:(si + 1) * 128], dtB_sb[:],
                start=True, stop=True), waits=w, inc="wt")
            w = [("wv", bc2_free)] if (bc2_free is not None and si == 0) else []
            tb2 = p.step("tensor", lambda e, si=si: e.matmul(
                ps_bc[2][:], sel16[:, si * 128:(si + 1) * 128], C_sb[:],
                start=True, stop=True), waits=w, inc="wt")
            cache_done = p.step("vector", [
                lambda e, si=si: e.tensor_copy(out=dtBrep[:, si * TC:(si + 1) * TC],
                                               in_=ps_bc[1][:]),
                lambda e, si=si: e.tensor_copy(out=crepA[:, si * TC:(si + 1) * TC],
                                               in_=ps_bc[2][:]),
            ], waits=[("wt", tb1), ("wt", tb2)], inc="wv")
            bc1_free = cache_done
            bc2_free = cache_done

        p.sem("gp")
        y2_last = None
        abar_free = [None, None]
        u_free = [None, None]
        h_free = [None, None]
        wbf_free = {0: None, 1: None}
        pi = 0
        for dh in range(NDT):
            zb = dh % 2
            w = []
            if zld_free[zb] is not None:
                w.append(("wv", zld_free[zb]))
            if dh < 2:
                w.append(("st", zst_free[0] if zst_free[0] else 0) if False else ("st", p.ctr["st"]))
            zld = p.dma("sync", zst[zb][:], zscr[dh * 128:(dh + 1) * 128, t0:t0 + TC],
                        "lw", waits=w)
            for si in range(DST):
                pi = (dh * DST + si) % 2
                idx = dh * DST + si
                w = [("wt", t_dtrep)]
                if abar_free[pi] is not None:
                    w.append(("wv", abar_free[pi]))
                s_ab = p.step("scalar", lambda e, dh=dh, si=si, pi=pi: e.activation(
                    out=abar[pi][:], in_=ps_bc[0][:], func=AF.Exp,
                    scale=a2[:, dh * DST + si:dh * DST + si + 1]), waits=w, inc="ws")
                bc0_sfree = s_ab
                w = [("wv", cache_done)] if idx == 0 else []
                if idx == 0 and p.ctr["st"] > 0:
                    w.append(("st", p.ctr["st"]))
                if u_free[pi] is not None:
                    w.append(("wv", u_free[pi]))
                v_u = p.step("vector", lambda e, dh=dh, si=si, pi=pi: e.tensor_tensor(
                    out=u_sb[pi][:], in0=dtBrep[:, si * TC:(si + 1) * TC],
                    in1=xcb[:, dh * TC:(dh + 1) * TC], op=ALU.mult), waits=w, inc="wv")
                w = [("ws", s_ab)]
                if h_free[pi] is not None:
                    w.append(h_free[pi])
                v_scan = p.step("vector", lambda e, idx=idx, pi=pi: e.tensor_tensor_scan(
                    out=hbf[pi][:], data0=abar[pi][:], data1=u_sb[pi][:],
                    initial=carry[:, 2 * idx + 1:2 * idx + 2], op0=ALU.mult, op1=ALU.add),
                    waits=w, inc="wv")
                abar_free[pi] = v_scan
                u_free[pi] = v_scan
                wb_i = si % 2
                w = []
                if wbf_free[wb_i] is not None:
                    w.append(("gp", wbf_free[wb_i]))
                v_w = p.step("vector", lambda e, si=si, pi=pi, wb_i=wb_i: e.tensor_tensor(
                    out=wbf[wb_i][:], in0=hbf[pi][:], in1=crepA[:, si * TC:(si + 1) * TC],
                    op=ALU.mult), waits=w, inc="wv")
                s_cy = p.step("vector", lambda e, idx=idx, pi=pi: e.tensor_copy(
                    out=carry[:, 2 * idx:2 * idx + 2], in_=hbf[pi][:, TC - 2:TC]),
                    inc="wv")
                if si == 0:
                    g_acc = p.step("gpsimd", lambda e, wb_i=wb_i: e.tensor_copy(
                        out=yacc[:], in_=wbf[wb_i][:]), waits=[("wv", v_w)], inc="gp")
                else:
                    g_acc = p.step("gpsimd", lambda e, wb_i=wb_i: e.tensor_tensor(
                        out=yacc[:], in0=yacc[:], in1=wbf[wb_i][:], op=ALU.add),
                        waits=[("wv", v_w)], inc="gp")
                wbf_free[wb_i] = g_acc
                h_free[pi] = ("wv", s_cy)
            y2_last = p.step("vector", [
                lambda e, dh=dh: e.scalar_tensor_tensor(
                    out=ytmp2[:], in0=xcb[:, dh * TC:(dh + 1) * TC],
                    scalar=dcols[:, dh:dh + 1], in1=yacc[:], op0=ALU.mult, op1=ALU.add),
                lambda e, dh=dh, zb=zb: e.tensor_tensor(
                    out=xcb[:, dh * TC:(dh + 1) * TC], in0=ytmp2[:],
                    in1=zst[zb][:], op=ALU.mult),
            ], waits=[("lw", zld), ("gp", g_acc)], inc="wv")
            zld_free[zb] = y2_last

        res_done = None
        for ot in range(NFT):
            buf = 0
            bank = ot % 2
            w = [] if wobuf_free[buf] is None else [("wt", wobuf_free[buf])]
            lw_t = p.dma("gpsimd", wgo[buf][:], outw_img[ot], "lw", waits=w)
            w = [("lw", lw_t), ("wv", y2_last)]
            if bank_busy[bank] is not None:
                w.append(bank_busy[bank])
            mms = [lambda e, d=d, buf=buf, bank=bank: e.matmul(
                ps_mm[bank][:], wgo[buf][:, d * 128:(d + 1) * 128],
                xcb[:, d * TC:(d + 1) * TC], start=(d == 0), stop=(d == NDT - 1))
                for d in range(NDT)]
            mm = p.step("tensor", mms, waits=w, inc="wt")
            wobuf_free[buf] = mm
            res_done = p.step("vector", lambda e, ot=ot, bank=bank: e.tensor_tensor(
                out=xf[:, ot * TC:(ot + 1) * TC], in0=xf[:, ot * TC:(ot + 1) * TC],
                in1=ps_mm[bank][:], op=ALU.add), waits=[("wt", mm)], inc="wv")
            bank_busy[bank] = ("wv", res_done)

        def out_cb(ft, buf, v_tgt, probe_wait):
            if probe_wait:
                d = out_dma[ft % 2]
                return [("st", d)] if d is not None else []
            out_dma[ft % 2] = p.dma("sync", x3[ft * 128:(ft + 1) * 128, t0:t0 + TC],
                                    buf[:], "st", waits=[("wv", v_tgt)])
            return None

        xf_free = emit_ln_chunk(p, xf, None, ps_bc[1], ps_bc[2], rowm, h_tmp, w_tmp,
                                out_cb, onesc, ones1, fconst,
                                wt_after=[("wv", res_done)], tag=f"l3ln{c}")
        bc1_free = xf_free
        bc2_free = xf_free

    return


def run_launch3(x2T_list, inputs, n_cores=8):
    c = host_consts()
    conv_w = inputs["L1_attn_conv_w"].astype(np.float32)
    cc = conv_w[:, 0, :].reshape(NDT, 128, 4).transpose(1, 0, 2).reshape(128, NDT * 4)
    cb = inputs["L1_attn_conv_b"].astype(np.float32).reshape(NDT, 128).T
    W_x = inputs["L1_attn_W_x"].astype(np.float32)
    wxb = W_x[0:DST].T.reshape(NDT, 128, DST).transpose(1, 0, 2).reshape(128, NDT * DST)
    wxc = W_x[DST:2 * DST].T.reshape(NDT, 128, DST).transpose(1, 0, 2).reshape(128, NDT * DST)
    wxdt = W_x[2 * DST].reshape(NDT, 128).T
    a2 = inputs["L1_attn_A_log"].astype(np.float32).reshape(NDT, 128, DST).transpose(1, 0, 2).reshape(128, NDT * DST)
    dtb = inputs["L1_attn_dt_bias"].astype(np.float32).reshape(NDT, 128).T
    dc = inputs["L1_attn_D"].astype(np.float32).reshape(NDT, 128).T
    base = {
        "win_img": tile_img(inputs["L1_attn_W_in"].T.astype(np.float32)),
        "outw_img": tile_img(inputs["L1_attn_out_W"].T.astype(np.float32)),
        "conv_cols": np.ascontiguousarray(cc),
        "convb_cols": np.ascontiguousarray(cb),
        "wxb_cols": np.ascontiguousarray(wxb),
        "wxc_cols": np.ascontiguousarray(wxc),
        "wxdt_cols": np.ascontiguousarray(wxdt),
        "a2_cols": np.ascontiguousarray(a2),
        "dtb_cols": np.ascontiguousarray(dtb),
        "d_cols": np.ascontiguousarray(dc),
        "ones_st": c["ones_st"], "ones_st_bf": c["ones_st_bf"], "ones_col": c["ones_col"],
        "fconst": c["fconst"], "sel16": c["sel16"],
    }
    nc = build_single(3, "x3")
    in_maps = []
    for xT in x2T_list:
        m = dict(base)
        m["xT"] = xT
        in_maps.append(m)
    res = run_bass_kernel_spmd(nc, in_maps, core_ids=list(range(n_cores)))
    if DEBUG_DUMP:
        return [np.asarray(r["x3"], dtype=np.float32) for r in res.results], res.results
    return [np.asarray(r["x3"], dtype=np.float32) for r in res.results]


NE = 8
TQ = 512
NQ = T // TQ
NSEG = TQ // TC


def emit_phase4(p):
    nc = p.nc
    st0 = p.ctr.get("st", 0)
    xT = p.io["x_in"]
    x4 = p.io["x_out"]
    rw_img = p.dram("rw_img", [NE, NFT, 128, DM], F32, "ExternalInput")
    ew_img = p.dram("ew_img", [NE, NFT, 128, DM], F32, "ExternalInput")
    rb_cols_d = p.dram("rb_cols", [128, NE * NFT], F32, "ExternalInput")
    ebT_d = p.dram("ebT", [NE, DM], F32, "ExternalInput")
    ones_st = p.dram("ones_st", [1, 128], F32, "ExternalInput")
    ones_stb_d = p.dram("ones_st_bf", [1, 128], BF16, "ExternalInput")
    ones_col = p.dram("ones_col", [128, 1], F32, "ExternalInput")
    ones_colb_d = p.dram("ones_col_bf", [128, 1], BF16, "ExternalInput")
    fconst_d = p.dram("fconst", [1, 8], F32, "ExternalInput")
    cbounce = p.dram("c_bounce", [1, NE * TQ], BF16, "Internal")

    for s in ("ld", "lw", "wv", "ws", "wt", "st"):
        p.sem(s)

    ones1 = p.sb("ones1", [1, 128], F32)
    ones1b = p.sb("ones1b", [1, 128], BF16)
    onesc = p.sb("onesc", [128, 1], F32)
    onescb = p.sb("onescb", [128, 1], BF16)
    fconst = p.sb("fconst_sb", [1, 8], F32)
    rb_cols = p.sb("rb_cols_sb", [128, NE * NFT], F32)
    ebT = p.sb("ebT_sb", [NE, DM], BF16)

    x3b = p.sb("x3b", [128, NFT * TQ], BF16)
    finalq = p.sb("finalq", [128, NFT * TQ], F32)
    planes = p.sb("planes", [128, 2 * NE * NSEG * TC], BF16)
    crep = p.sb("crep", [128, NE * NSEG * TC], BF16)
    c_sb = p.sb("c_sb", [1, NE * TQ], BF16)
    c_tile = p.sb("c_tile", [NE, TQ], BF16)
    xf = p.sb("xf", [128, NFT * TC], F32)
    m1 = p.sb("m1", [128, TC], F32)
    m2 = p.sb("m2", [128, TC], F32)
    s12 = p.sb("s12", [128, TC], F32)
    tmpd = p.sb("tmpd", [128, TC], F32)
    tmps = p.sb("tmps", [128, TC], F32)
    tmp1 = p.sb("tmp1", [128, TC], F32)
    tmp2 = p.sb("tmp2", [128, TC], F32)
    rows = p.sb("rows", [1, 3 * TC], F32)
    wg = [p.sb(f"wg{i}", [128, DM], BF16) for i in range(2)]

    ps_mm = [p.ps(f"ps_mm{i}", [128, TC]) for i in range(2)]
    ps_row = p.ps("ps_row", [1, TC])
    ps_bc = [p.ps(f"ps_bc{i}", [128, TC]) for i in range(2)]

    for dst, src in [(ones1[:], ones_st), (ones1b[:], ones_stb_d), (onesc[:], ones_col),
                     (onescb[:], ones_colb_d),
                     (fconst[:], fconst_d), (rb_cols[:], rb_cols_d)]:
        p.dma("sync", dst, src, "ld")
    p.dma("gpsimd", ebT[:], ebT_d, "ld")
    const_ld = p.ctr["ld"]

    wbuf_free = {0: None, 1: None}
    bank_free = {0: None, 1: None}
    row_free = None
    x3b_free = None
    out_dma = {0: None, 1: None}
    xf_free = None
    bc_free = [None, None]

    for q in range(NQ):
        q0 = q * TQ
        if q == 0:
            w = [("ld", const_ld)]
            if p.ctr["st"] > 0:
                w.append(("st", p.ctr["st"]))
        else:
            w = [("wt", x3b_free)]
        ldq = None
        for ft in range(NFT):
            ldq = p.dma("gpsimd", x3b[:, ft * TQ:(ft + 1) * TQ],
                        xT[ft * 128:(ft + 1) * 128, q0:q0 + TQ], "lw",
                        waits=(w if ft == 0 else ()))
        gv_last = None
        pending = None

        def emit_top2(ot, pbuf, q=q):
            nonlocal gv_last

            def pl(ei, pbuf=pbuf):
                return planes[:, (pbuf * NE + ei) * TC:(pbuf * NE + ei + 1) * TC]
            fns = [
                lambda e, pl=pl: e.tensor_tensor(out=m1[:], in0=pl(0), in1=pl(1), op=ALU.max),
                lambda e, pl=pl: e.tensor_tensor(out=tmpd[:], in0=pl(2), in1=pl(3), op=ALU.max),
                lambda e: e.tensor_tensor(out=m1[:], in0=m1[:], in1=tmpd[:], op=ALU.max),
                lambda e, pl=pl: e.tensor_tensor(out=tmpd[:], in0=pl(4), in1=pl(5), op=ALU.max),
                lambda e: e.tensor_tensor(out=m1[:], in0=m1[:], in1=tmpd[:], op=ALU.max),
                lambda e, pl=pl: e.tensor_tensor(out=tmpd[:], in0=pl(6), in1=pl(7), op=ALU.max),
                lambda e: e.tensor_tensor(out=m1[:], in0=m1[:], in1=tmpd[:], op=ALU.max),
            ]
            for ei in range(NE):
                fns.append(lambda e, ei=ei, pl=pl: e.tensor_tensor(
                    out=tmpd[:], in0=pl(ei), in1=m1[:], op=ALU.is_ge))
                fns.append(lambda e, ei=ei, pl=pl: e.scalar_tensor_tensor(
                    out=tmpd[:], in0=tmpd[:], scalar=-1e30, in1=pl(ei),
                    op0=ALU.mult, op1=ALU.add))
                if ei == 0:
                    fns.append(lambda e: e.tensor_copy(out=m2[:], in_=tmpd[:]))
                else:
                    fns.append(lambda e: e.tensor_tensor(out=m2[:], in0=m2[:],
                                                         in1=tmpd[:], op=ALU.max))
            fns.append(lambda e: e.tensor_tensor(out=s12[:], in0=m1[:], in1=m2[:], op=ALU.add))
            fns.append(lambda e: e.tensor_tensor(out=m2[:], in0=m2[:], in1=m1[:], op=ALU.subtract))
            for ei in range(NE):
                fns.append(lambda e, ei=ei, pl=pl: e.scalar_tensor_tensor(
                    out=pl(ei), in0=pl(ei), scalar=2.0, in1=s12[:],
                    op0=ALU.mult, op1=ALU.subtract))
                fns.append(lambda e, ei=ei, pl=pl: e.tensor_tensor(
                    out=tmpd[:], in0=pl(ei), in1=m2[:], op=ALU.is_lt))
                fns.append(lambda e, ei=ei, pl=pl: e.scalar_tensor_tensor(
                    out=pl(ei), in0=tmpd[:], scalar=-1e30, in1=pl(ei),
                    op0=ALU.mult, op1=ALU.add))
            v_pre = p.step("vector", fns, inc="wv")
            s_g = p.step("scalar", lambda e, pbuf=pbuf: e.activation(
                out=planes[:, pbuf * NE * TC:(pbuf + 1) * NE * TC],
                in_=planes[:, pbuf * NE * TC:(pbuf + 1) * NE * TC], func=AF.Sigmoid),
                waits=[("wv", v_pre)], inc="ws")
            afns = []
            for ei in range(NE):
                gslice = crep[:, ei * TC:(ei + 1) * TC]
                if ot == 0:
                    afns.append(lambda e, ei=ei, pl=pl, gslice=gslice: e.tensor_copy(
                        out=gslice, in_=pl(ei)))
                else:
                    afns.append(lambda e, ei=ei, pl=pl, gslice=gslice: e.tensor_tensor(
                        out=gslice, in0=gslice, in1=pl(ei), op=ALU.add))
            w = [("ws", s_g)]
            if ot == 0 and bc_free[0] is not None:
                w.append(bc_free[0])
            gv_last = p.step("vector", afns, waits=w, inc="wv")

        plane_evac = {0: None, 1: None}
        for ot in range(NFT):
            pbuf = ot % 2
            for ei in range(NE):
                buf = (ot * NE + ei) % 2
                w = [] if wbuf_free[buf] is None else [("wt", wbuf_free[buf])]
                lw_t = p.dma("gpsimd", wg[buf][:], rw_img[ei, ot], "lw", waits=w)
                for ts_ in range(NSEG):
                    bank = ei % 2
                    w = [("lw", max(lw_t, ldq))]
                    if bank_free[bank] is not None:
                        w.append(bank_free[bank])
                    mm = p.step("tensor", [lambda e, ci=ci, buf=buf, bank=bank, ts_=ts_: e.matmul(
                        ps_mm[bank][:], wg[buf][:, ci * 128:(ci + 1) * 128],
                        x3b[:, ci * TQ + ts_ * TC:ci * TQ + (ts_ + 1) * TC],
                        start=(ci == 0), stop=(ci == NFT - 1))
                        for ci in range(NFT)], waits=w, inc="wt")
                    if ts_ == NSEG - 1:
                        wbuf_free[buf] = mm
                    w = [("wt", mm)]
                    if plane_evac[pbuf] is not None and ei == 0:
                        w.append(("wv", plane_evac[pbuf]))
                    ev = p.step("vector", lambda e, ei=ei, pbuf=pbuf, bank=bank, ot=ot: e.tensor_scalar(
                        out=planes[:, (pbuf * NE + ei) * TC:(pbuf * NE + ei + 1) * TC],
                        in0=ps_mm[bank][:], scalar1=rb_cols[:, ei * NFT + ot:ei * NFT + ot + 1],
                        scalar2=None, op0=ALU.add), waits=w, inc="wv")
                    bank_free[bank] = ("wv", ev)
            if pending is not None:
                pending()
            ot_c, pb_c = ot, pbuf
            pending = (lambda ot_c=ot_c, pb_c=pb_c: (emit_top2(ot_c, pb_c),
                                                     plane_evac.__setitem__(pb_c, gv_last)))
        if pending is not None:
            pending()

        c_add = None
        for ei in range(NE):
            for ts_ in range(NSEG):
                w = [("wv", gv_last)] if (ei == 0 and ts_ == 0) else []
                if row_free is not None:
                    w.append(("wv", row_free))
                t_cs = p.step("tensor", lambda e, ei=ei, ts_=ts_: e.matmul(
                    ps_row[:], onescb[:],
                    crep[:, (ei * NSEG + ts_) * TC:(ei * NSEG + ts_ + 1) * TC],
                    start=True, stop=True), waits=w, inc="wt")
                c_add = p.step("vector", lambda e, ei=ei, ts_=ts_: e.tensor_scalar(
                    out=c_sb[0:1, ei * TQ + ts_ * TC:ei * TQ + (ts_ + 1) * TC],
                    in0=ps_row[:], scalar1=1.0, scalar2=None, op0=ALU.mult),
                    waits=[("wt", t_cs)], inc="wv")
                row_free = c_add
        st_c = p.dma("sync", cbounce, c_sb[0:1, :], "lw", waits=[("wv", c_add)])
        ld_ct = p.dma("gpsimd", c_tile[:], cbounce.rearrange("c (a b) -> (c a) b", b=TQ),
                      "lw", waits=[("lw", st_c)])
        crep_done = None
        for ei in range(NE):
            for ts_ in range(NSEG):
                w = [("wv", c_add), ("wt", p.ctr["wt"])] if (ei == 0 and ts_ == 0) else []
                if bc_free[0] is not None:
                    w.append(bc_free[0])
                t_b = p.step("tensor", lambda e, ei=ei, ts_=ts_: e.matmul(
                    ps_bc[0][:], ones1b[:],
                    c_sb[0:1, ei * TQ + ts_ * TC:ei * TQ + (ts_ + 1) * TC],
                    start=True, stop=True), waits=w, inc="wt")
                crep_done = p.step("vector", lambda e, ei=ei, ts_=ts_: e.tensor_copy(
                    out=crep[:, (ei * NSEG + ts_) * TC:(ei * NSEG + ts_ + 1) * TC],
                    in_=ps_bc[0][:]), waits=[("wt", t_b)], inc="wv")
                bc_free[0] = ("wv", crep_done)

        ex_last_mm = None
        fin_last = None
        for ot in range(NFT):
            for ei in range(NE):
                buf = (ot * NE + ei) % 2
                w = [] if wbuf_free[buf] is None else [("wt", wbuf_free[buf])]
                lw_t = p.dma("gpsimd", wg[buf][:], ew_img[ei, ot], "lw", waits=w)
                for ts_ in range(NSEG):
                    bank = ei % 2
                    w = [("lw", lw_t), ("wv", crep_done), ("wv", gv_last)]
                    if bank_free[bank] is not None:
                        w.append(bank_free[bank])
                    mm = p.step("tensor", [lambda e, ci=ci, buf=buf, bank=bank, ts_=ts_: e.matmul(
                        ps_mm[bank][:], wg[buf][:, ci * 128:(ci + 1) * 128],
                        x3b[:, ci * TQ + ts_ * TC:ci * TQ + (ts_ + 1) * TC],
                        start=(ci == 0), stop=(ci == NFT - 1))
                        for ci in range(NFT)], waits=w, inc="wt")
                    if ts_ == NSEG - 1:
                        wbuf_free[buf] = mm
                    ex_last_mm = mm
                    fslice = finalq[:, ot * TQ + ts_ * TC:ot * TQ + (ts_ + 1) * TC]
                    cslice = crep[:, (ei * NSEG + ts_) * TC:(ei * NSEG + ts_ + 1) * TC]
                    if ei == 0:
                        fin = p.step("vector", lambda e, fslice=fslice, cslice=cslice, bank=bank:
                                     e.tensor_tensor(out=fslice, in0=ps_mm[bank][:], in1=cslice,
                                                     op=ALU.mult), waits=[("wt", mm)], inc="wv")
                    else:
                        fin = p.step("vector", [
                            lambda e, cslice=cslice, bank=bank: e.tensor_tensor(
                                out=tmp2[:], in0=ps_mm[bank][:], in1=cslice, op=ALU.mult),
                            lambda e, fslice=fslice: e.tensor_tensor(
                                out=fslice, in0=fslice, in1=tmp2[:], op=ALU.add),
                        ], waits=[("wt", mm)], inc="wv")
                    bank_free[bank] = ("wv", fin)
                    fin_last = fin
            for ts_ in range(NSEG):
                w = [("lw", ld_ct)]
                if bc_free[1] is not None:
                    w.append(bc_free[1])
                t_bias = p.step("tensor", lambda e, ot=ot, ts_=ts_: e.matmul(
                    ps_bc[1][:], ebT[:, ot * 128:(ot + 1) * 128],
                    c_tile[:, ts_ * TC:(ts_ + 1) * TC], start=True, stop=True),
                    waits=w, inc="wt")
                fin_last = p.step("vector", lambda e, ot=ot, ts_=ts_: e.tensor_tensor(
                    out=finalq[:, ot * TQ + ts_ * TC:ot * TQ + (ts_ + 1) * TC],
                    in0=finalq[:, ot * TQ + ts_ * TC:ot * TQ + (ts_ + 1) * TC],
                    in1=ps_bc[1][:], op=ALU.add), waits=[("wt", t_bias)], inc="wv")
                bc_free[1] = ("wv", fin_last)
        x3b_free = ex_last_mm

        for ts_ in range(NSEG):
            t0 = q0 + ts_ * TC
            if xf_free is not None:
                w = [("wv", xf_free)]
            else:
                w = [("st", st0)] if st0 > 0 else []
            ldx = None
            for ft in range(NFT):
                ldx = p.dma("sync", xf[:, ft * TC:(ft + 1) * TC],
                            xT[ft * 128:(ft + 1) * 128, t0:t0 + TC], "ld",
                            waits=(w if ft == 0 else ()))
            res_done = None
            for ft in range(NFT):
                res_done = p.step("vector", lambda e, ft=ft, ts_=ts_: e.tensor_tensor(
                    out=xf[:, ft * TC:(ft + 1) * TC], in0=xf[:, ft * TC:(ft + 1) * TC],
                    in1=finalq[:, ft * TQ + ts_ * TC:ft * TQ + (ts_ + 1) * TC], op=ALU.add),
                    waits=([("ld", ldx), ("wv", fin_last)] if ft == 0 else ()), inc="wv")

            def out_cb(ft, buf, v_tgt, probe_wait, t0=t0):
                if probe_wait:
                    d = out_dma[ft % 2]
                    return [("st", d)] if d is not None else []
                out_dma[ft % 2] = p.dma("sync", x4[ft * 128:(ft + 1) * 128, t0:t0 + TC],
                                        buf[:], "st", waits=[("wv", v_tgt)])
                return None

            xf_free = emit_ln_chunk(p, xf, None, ps_bc[0], ps_bc[1], rows, tmp1, tmp2,
                                    out_cb, onesc, ones1, fconst,
                                    wt_after=[("wv", res_done)], tag=f"l4ln{q}_{ts_}")
            bc_free = [("wv", xf_free), ("wv", xf_free)]

    return


def run_launch4(x3T_list, inputs, n_cores=8):
    c = host_consts()
    rw = inputs["L1_ffn_router_w"].astype(np.float32)
    rw_em = rw.reshape(DM, NE, DM).transpose(1, 0, 2)
    rw_img = np.stack([tile_img(np.ascontiguousarray(rw_em[e].T)) for e in range(NE)])
    ew = inputs["L1_ffn_expert_w"].astype(np.float32)
    ew_img = np.stack([tile_img(np.ascontiguousarray(ew[e].T)) for e in range(NE)])
    rb = inputs["L1_ffn_router_b"].astype(np.float32)
    rb_cols = rb.reshape(NFT, 128, NE).transpose(1, 2, 0).reshape(128, NE * NFT)
    base = {
        "rw_img": rw_img, "ew_img": ew_img,
        "rb_cols": np.ascontiguousarray(rb_cols),
        "ebT": inputs["L1_ffn_expert_b"].astype(np.float32),
        "ones_st": c["ones_st"], "ones_st_bf": c["ones_st_bf"], "ones_col": c["ones_col"],
        "ones_col_bf": c["ones_col_bf"], "fconst": c["fconst"],
    }
    nc = build_single(4, "x4")
    in_maps = []
    for xT in x3T_list:
        m = dict(base)
        m["xT"] = xT
        in_maps.append(m)
    res = run_bass_kernel_spmd(nc, in_maps, core_ids=list(range(n_cores)))
    return [np.asarray(r["x4"], dtype=np.float32) for r in res.results]


def _prep_l1(inputs, consts):
    base = dict(consts_sub(consts, ["rope_cos", "rope_sin", "rot_st", "m_low", "m_up",
                                    "ident_bf", "ones_st", "ones_col", "ones_col_bf", "fconst"]))
    base["wqkv_img"] = tile_img(inputs["L0_attn_w_qkv"].T.astype(np.float32))
    base["wproj_img"] = tile_img(inputs["L0_attn_w_proj"].T.astype(np.float32))
    return base


def consts_sub(c, names):
    m = {"ident_bf": "ident_bf", "m_low": "m_low", "m_up": "m_up"}
    out = {}
    for n in names:
        out[n] = c[n]
    return out


PHASE_EMITTERS = {1: None, 2: None, 3: None, 4: None}


def _mk_prog():
    nc = bass.Bass("TRN2", target_bir_lowering=False)
    ctx = contextlib.ExitStack()
    return Prog(nc, ctx)


def _finish(p):
    p.step("gpsimd", lambda e: e.nop(), waits=[("st", p.ctr["st"])])
    p.run()
    return p.nc


def build_single(phase_idx, out_name):
    p = _mk_prog()
    p.io["x_in"] = p.dram("xT", [DM, T], F32, "ExternalInput")
    p.io["x_out"] = p.dram(out_name, [DM, T], F32, "ExternalOutput")
    PHASE_EMITTERS[phase_idx](p)
    return _finish(p)


def fused_aliases(p):
    a = p.alias
    xf = p.ctx.enter_context(p.nc.sbuf_tensor("A_xf", [128, 8192], F32))
    xb = p.ctx.enter_context(p.nc.sbuf_tensor("A_xb", [128, 8192], BF16))
    bfA = p.ctx.enter_context(p.nc.sbuf_tensor("A_bfA", [128, 32768], BF16))
    fC = p.ctx.enter_context(p.nc.sbuf_tensor("A_fC", [128, 4096], F32))
    bfB = p.ctx.enter_context(p.nc.sbuf_tensor("A_bfB", [128, 4096], BF16))
    f32big = p.ctx.enter_context(p.nc.sbuf_tensor("A_f32big", [128, 8192], F32))
    wstr = p.ctx.enter_context(p.nc.sbuf_tensor("A_wstr", [128, 16384], BF16))
    rows = p.ctx.enter_context(p.nc.sbuf_tensor("A_rows", [1, 1536], F32))
    K = [p.ctx.enter_context(p.nc.psum_tensor(f"K{i}", [128, 512], F32)) for i in range(7)]
    a["rows"] = rows[:]
    a["rowm"] = rows[:]
    a["xf"] = xf[:]
    a["xb"] = xb[:]
    a["cosT"] = f32big[:, 0:4096]
    a["sinT"] = f32big[:, 4096:8192]
    a["qn"] = bfA[:, 0:8192]
    a["ybuf"] = bfA[:, 8192:16384]
    a["kn_ring"] = bfA[:, 16384:20480]
    a["vt_ring"] = bfA[:, 20480:24576]
    a["Ebuf"] = bfA[:, 24576:25600]
    a["vbf"] = bfA[:, 25600:27648]
    a["kf"] = fC[:, 0:2048]
    a["qtile"] = fC[:, 2048:2560]
    a["tmp1"] = fC[:, 2560:3072]
    a["tmp2"] = fC[:, 3072:3584]
    a["ytmp"] = fC[:, 3584:4096]
    a["wgrp0"] = wstr[:, 0:2048]
    a["wgrp1"] = wstr[:, 2048:4096]
    a["ps_mm0"] = K[0][:]
    a["ps_mm1"] = K[1][:]
    a["ps_sc0"] = K[2][:]
    a["ps_sc1"] = K[3][:]
    a["ps_y"] = K[4][:]
    a["ps_bc0"] = K[5][:]
    a["ps_bc1"] = K[6][:]
    a["hidH"] = bfA[:, 0:32768]
    a["hidt"] = bfB[:, 0:512]
    a["tmpS"] = fC[:, 0:512]
    a["w1g0"] = wstr[:, 0:2048]
    a["w1g1"] = wstr[:, 2048:4096]
    a["w3g0"] = wstr[:, 4096:6144]
    a["w3g1"] = wstr[:, 6144:8192]
    a["w2g"] = wstr[:, 8192:16384]
    a["wbig0"] = bfA[:, 0:8192]
    a["wbig1"] = bfA[:, 8192:16384]
    a["ps_g0"] = K[0][:]
    a["ps_g1"] = K[1][:]
    a["ps_l0"] = K[2][:]
    a["ps_l1"] = K[3][:]
    a["ps_h"] = K[4][:]
    a["xcb"] = bfA[:, 0:16384]
    a["dtBrep"] = bfA[:, 16384:24576]
    a["crepA"] = bfA[:, 24576:32768]
    a["a2"] = f32big[:, 0:512]
    a["carry"] = f32big[:, 512:1536]
    a["conv_cols_sb"] = f32big[:, 1536:1664]
    a["convb_sb"] = f32big[:, 1664:1696]
    a["dtb"] = f32big[:, 1696:1728]
    a["dcols"] = f32big[:, 1728:1760]
    a["mb"] = f32big[0:1, 1760:1761]
    a["abar0"] = fC[:, 2048:2560]
    a["abar1"] = f32big[:, 1792:2304]
    a["u_sb0"] = bfB[:, 0:512]
    a["u_sb1"] = bfB[:, 512:1024]
    a["hbf0"] = bfB[:, 1024:1536]
    a["hbf1"] = bfB[:, 1536:2048]
    a["wbf0"] = bfB[:, 2048:2560]
    a["zst0"] = bfB[:, 2560:3072]
    a["zst1"] = bfB[:, 3072:3584]
    a["wbf1"] = bfB[:, 3584:4096]
    a["hist"] = fC[:, 3600:3696]
    a["wxb"] = wstr[:, 3584:4096]
    a["wxc"] = wstr[:, 10752:11264]
    a["wxdt"] = wstr[:, 11264:11296]
    a["ident"] = wstr[:, 11296:11424]
    a["ones1b"] = wstr[0:1, 11424:11552]
    a["onescb"] = wstr[0:128, 11552:11553]
    a["B_sb"] = wstr[0:16, 2048:2560]
    a["C_sb"] = wstr[0:16, 2560:3072]
    a["dtB_sb"] = wstr[0:16, 3072:3584]
    a["sel16_sb"] = wstr[0:16, 8192:10240]
    a["h_tmp"] = fC[:, 0:512]
    a["w_tmp"] = fC[:, 512:1024]
    a["yacc"] = fC[:, 1024:1536]
    a["ytmp2"] = fC[:, 1536:2048]
    a["convacc"] = fC[:, 2560:3072]
    a["xs_tmp"] = fC[:, 3072:3588]
    a["wgi0"] = wstr[:, 0:2048]
    a["wgo0"] = wstr[:, 4096:8192]
    a["ps_B"] = K[2][0:16, :]
    a["ps_C"] = K[2][32:48, :]
    a["ps_dt"] = K[2][64:65, :]
    a["ps_bc2"] = K[4][:]
    a["x3b"] = xb[:]
    a["finalq"] = f32big[:, 0:8192]
    a["planes"] = bfA[:, 0:8192]
    a["crep"] = bfA[:, 8192:12288]
    a["m1"] = fC[:, 0:512]
    a["m2"] = fC[:, 512:1024]
    a["s12"] = fC[:, 1024:1536]
    a["tmpd"] = fC[:, 1536:2048]
    a["tmps"] = fC[:, 2048:2560]
    a["wg0"] = wstr[:, 0:2048]
    a["wg1"] = wstr[:, 2048:4096]
    a["ebT_sb"] = wstr[0:8, 4096:6144]
    a["rb_cols_sb"] = fC[:, 3696:3824]
    a["c_sb"] = wstr[0:1, 6144:10240]
    a["c_tile"] = wstr[0:8, 10240:10752]
    a["ps_row"] = K[2][0:1, :]


def build_fused():
    p = _mk_prog()
    fused_aliases(p)
    xin = p.dram("xT", [DM, T], F32, "ExternalInput")
    x1s = p.dram("x1s", [DM, T], F32, "Internal")
    x2s = p.dram("x2s", [DM, T], F32, "Internal")
    x3s = p.dram("x3s", [DM, T], F32, "Internal")
    x4o = p.dram("x4", [DM, T], F32, "ExternalOutput")
    chain = [(xin, x1s), (x1s, x2s), (x2s, x3s), (x3s, x4o)]
    for i, (src, dst) in enumerate(chain):
        p.io["x_in"] = src
        p.io["x_out"] = dst
        PHASE_EMITTERS[i + 1](p)
        if i < 3:
            p.barrier()
    return _finish(p)


def run_launch1(x_full, inputs, n_cores=8):
    outs = run_launch1_fm(x_full, inputs, n_cores)
    if DEBUG_DUMP:
        outs, results = outs
        return np.stack([o.T for o in outs], axis=0), results
    return np.stack([o.T for o in outs], axis=0)


def run_launch1_fm(x_full, inputs, n_cores=8):
    c = host_consts()
    base = _prep_l1(inputs, c)
    nc = build_single(1, "x1")
    in_maps = []
    for b in range(x_full.shape[0]):
        m = dict(base)
        m["xT"] = np.ascontiguousarray(x_full[b].T).astype(np.float32)
        in_maps.append(m)
    res = run_bass_kernel_spmd(nc, in_maps, core_ids=list(range(n_cores)))
    outs = [np.asarray(r["x1"], dtype=np.float32) for r in res.results]
    if DEBUG_DUMP:
        return outs, res.results
    return outs


PHASE_EMITTERS[1] = emit_phase1
PHASE_EMITTERS[2] = emit_phase2
PHASE_EMITTERS[3] = emit_phase3
PHASE_EMITTERS[4] = emit_phase4


def host_all_inputs(inputs):
    c = host_consts()
    base = _prep_l1(inputs, c)
    h = np.zeros((128, 128), np.float64)
    for i in range(128):
        for j in range(128):
            h[i, j] = 1.0 if bin(i & j).count("1") % 2 == 0 else -1.0
    base.update({
        "w1_img": tile_img(inputs["L0_ffn_W1"].T.astype(np.float32)),
        "w3_img": tile_img(inputs["L0_ffn_W3"].T.astype(np.float32)),
        "w2_img": tile_img(inputs["L0_ffn_W2"].T.astype(np.float32)),
        "h128_bf": h.astype(BF),
    })
    conv_w = inputs["L1_attn_conv_w"].astype(np.float32)
    cc = conv_w[:, 0, :].reshape(NDT, 128, 4).transpose(1, 0, 2).reshape(128, NDT * 4)
    cb = inputs["L1_attn_conv_b"].astype(np.float32).reshape(NDT, 128).T
    W_x = inputs["L1_attn_W_x"].astype(np.float32)
    base.update({
        "win_img": tile_img(inputs["L1_attn_W_in"].T.astype(np.float32)),
        "outw_img": tile_img(inputs["L1_attn_out_W"].T.astype(np.float32)),
        "conv_cols": np.ascontiguousarray(cc),
        "convb_cols": np.ascontiguousarray(cb),
        "wxb_cols": np.ascontiguousarray(W_x[0:DST].T.reshape(NDT, 128, DST).transpose(1, 0, 2).reshape(128, NDT * DST)),
        "wxc_cols": np.ascontiguousarray(W_x[DST:2 * DST].T.reshape(NDT, 128, DST).transpose(1, 0, 2).reshape(128, NDT * DST)),
        "wxdt_cols": np.ascontiguousarray(W_x[2 * DST].reshape(NDT, 128).T),
        "a2_cols": np.ascontiguousarray(inputs["L1_attn_A_log"].astype(np.float32).reshape(NDT, 128, DST).transpose(1, 0, 2).reshape(128, NDT * DST)),
        "dtb_cols": np.ascontiguousarray(inputs["L1_attn_dt_bias"].astype(np.float32).reshape(NDT, 128).T),
        "d_cols": np.ascontiguousarray(inputs["L1_attn_D"].astype(np.float32).reshape(NDT, 128).T),
        "ones_st_bf": c["ones_st_bf"],
        "sel16": c["sel16"],
    })
    rw = inputs["L1_ffn_router_w"].astype(np.float32)
    rw_em = rw.reshape(DM, NE, DM).transpose(1, 0, 2)
    rb = inputs["L1_ffn_router_b"].astype(np.float32)
    base.update({
        "rw_img": np.stack([tile_img(np.ascontiguousarray(rw_em[e].T)) for e in range(NE)]),
        "ew_img": np.stack([tile_img(np.ascontiguousarray(inputs["L1_ffn_expert_w"].astype(np.float32)[e].T)) for e in range(NE)]),
        "rb_cols": np.ascontiguousarray(rb.reshape(NFT, 128, NE).transpose(1, 2, 0).reshape(128, NE * NFT)),
        "ebT": inputs["L1_ffn_expert_b"].astype(np.float32),
    })
    return base


def kernel(**inputs):
    x = np.asarray(inputs["L0_attn_x"], dtype=np.float32)
    n_cores = x.shape[0]
    base = host_all_inputs(inputs)
    nc = build_fused()
    in_maps = []
    for b in range(n_cores):
        m = dict(base)
        m["xT"] = np.ascontiguousarray(x[b].T).astype(np.float32)
        in_maps.append(m)
    res = run_bass_kernel_spmd(nc, in_maps, core_ids=list(range(n_cores)))
    return np.stack([np.asarray(r["x4"], dtype=np.float32).T for r in res.results], axis=0)


# version 15 (final)
# speedup vs baseline: 1.0322x; 1.0108x over previous
import contextlib
import math

import numpy as np
import ml_dtypes

import concourse.bass as bass
import concourse.mybir as mybir
from concourse.bass_utils import run_bass_kernel_spmd

F32 = mybir.dt.float32
BF16 = mybir.dt.bfloat16
AF = mybir.ActivationFunctionType
ALU = mybir.AluOpType
AX = mybir.AxisListType

BF = ml_dtypes.bfloat16

T = 4096
DM = 2048
NFT = DM // 128
TC = 512
NCH = T // TC
LN_EPS = 1e-5


class Prog:

    ENGINES = ("sync", "scalar", "vector", "tensor", "gpsimd")

    def __init__(self, nc, ctx):
        self.nc = nc
        self.ctx = ctx
        self.ops = {e: [] for e in self.ENGINES}
        self.sems = {}
        self.ctr = {}
        self.alias = {}
        self.dram_map = {}
        self.io = {}

    def sem(self, name):
        if name in self.sems:
            return self.sems[name]
        self.sems[name] = self.ctx.enter_context(self.nc.semaphore(name))
        self.ctr[name] = 0
        return self.sems[name]

    def sb(self, name, shape, dtype):
        if name in self.alias:
            return self.alias[name]
        if not hasattr(self, "_sbufs"):
            self._sbufs = {}
        if name not in self._sbufs:
            self._sbufs[name] = self.ctx.enter_context(self.nc.sbuf_tensor(name, shape, dtype))
        return self._sbufs[name]

    def ps(self, name, shape, dtype=F32):
        if name in self.alias:
            return self.alias[name]
        if not hasattr(self, "_psums"):
            self._psums = {}
        if name not in self._psums:
            self._psums[name] = self.ctx.enter_context(self.nc.psum_tensor(name, shape, dtype))
        return self._psums[name]

    def dram(self, name, shape, dtype, kind):
        if name in self.dram_map:
            return self.dram_map[name]
        ap = self.nc.dram_tensor(name, shape, dtype, kind=kind).ap()
        self.dram_map[name] = ap
        return ap

    def barrier(self, tag=""):
        self.sem("bar")
        for eng in self.ENGINES:
            self.step(eng, lambda e: e.nop(), inc="bar")
        tgt = self.ctr["bar"]
        for eng in self.ENGINES:
            sh = self.sems["bar"]
            self.ops[eng].append(lambda e, sh=sh, v=tgt: e.wait_ge(sh, v))

    def step(self, eng, fns, waits=(), inc=None, inc_dma=False):
        if not isinstance(fns, (list, tuple)):
            fns = [fns]
        for s, v in waits:
            sh = self.sems[s]
            self.ops[eng].append(lambda e, sh=sh, v=v: e.wait_ge(sh, v))
        tgt = None
        if inc is not None:
            sh = self.sems[inc]
            self.ctr[inc] += 16 if inc_dma else 1
            tgt = self.ctr[inc]
        for i, f in enumerate(fns):
            if inc is not None and i == len(fns) - 1:
                self.ops[eng].append(
                    lambda e, f=f, sh=sh, n=(16 if inc_dma else 1): f(e).then_inc(sh, n))
            else:
                self.ops[eng].append(f)
        return tgt

    def dma(self, eng, out, in_, sem="ld", waits=()):
        return self.step(eng, lambda e, o=out, i=in_: e.dma_start(out=o, in_=i),
                         waits=waits, inc=sem, inc_dma=True)

    def run(self):
        block = self.ctx.enter_context(self.nc.Block())
        for eng_name in self.ENGINES:
            opslist = self.ops[eng_name]
            if not opslist:
                continue

            def body(e, opslist=opslist):
                for fn in opslist:
                    fn(e)

            getattr(block, eng_name)(body)
        self.ctx.close()


def tile_img(WT):
    C, R = WT.shape
    a = C // 128
    rb = R // 128
    return np.ascontiguousarray(
        WT.reshape(a, 128, rb, 128).transpose(2, 1, 0, 3).reshape(rb, 128, C))


def host_consts():
    c = {}
    c["ones_st"] = np.ones((1, 128), np.float32)
    c["ones_st_bf"] = np.ones((1, 128), BF)
    c["ones_col"] = np.ones((128, 1), np.float32)
    c["ones_col_bf"] = np.ones((128, 1), BF)
    dh = 128
    inv = 1.0 / (10000.0 ** (np.arange(0, dh, 2, dtype=np.float64) / dh))
    freqs = np.arange(T, dtype=np.float64)[:, None] * inv[None, :]
    c["rope_cos"] = np.ascontiguousarray(np.repeat(np.cos(freqs), 2, axis=1).T).astype(np.float32)
    c["rope_sin"] = np.ascontiguousarray(np.repeat(np.sin(freqs), 2, axis=1).T).astype(np.float32)
    S = np.zeros((128, 128), np.float32)
    for i in range(64):
        S[2 * i + 1, 2 * i] = -1.0
        S[2 * i, 2 * i + 1] = 1.0
    c["rot_st"] = S
    NEG = -1e9
    ka = np.arange(128)[:, None]
    ro = np.arange(128)[None, :]
    c["m_low"] = np.where(ka <= ro, 0.0, NEG).astype(BF)
    c["m_up"] = np.where(ka > ro, 0.0, NEG).astype(BF)
    c["ident_bf"] = np.eye(128).astype(BF)
    sel = np.zeros((16, 16 * 128), np.float32)
    for si in range(16):
        sel[si, si * 128:(si + 1) * 128] = 1.0
    c["sel16"] = sel.astype(BF)
    c["fconst"] = np.array([[1.1920929e-07, 1.0 / 128, -0.5, 1e-5, -1.0, 1.0, 1.0 / 2048, 0.0]],
                           np.float32)
    return c


DEBUG_DUMP = False
HQ, HKV, DH, WIN = 16, 4, 128, 512
RMS_EPS = 1.1920929e-07


def emit_ln_chunk(p, xf, src_rows_ps, ps_bc0, ps_bc1, rowbuf, tmp1, tmp2,
                  out_cb, onesc, ones1, fconst, wt_after=(), tag=""):
    mms = []
    for ft in range(NFT):
        mms.append(lambda e, ft=ft: e.matmul(ps_bc0[0:1, :], onesc[:],
                                             xf[:, ft * TC:(ft + 1) * TC],
                                             start=(ft == 0), stop=(ft == NFT - 1)))
    t_mean = p.step("tensor", mms, waits=wt_after, inc="wt")
    sq_done = [None, None]
    for ft in range(NFT):
        buf = tmp1 if ft % 2 == 0 else tmp2
        w = list(wt_after) if ft == 0 else []
        if ft == 0 and p.ctr["st"] > 0:
            w.append(("st", p.ctr["st"]))
        if sq_done[ft % 2] is not None:
            w.append(("wt", sq_done[ft % 2]))
        s_tgt = p.step("scalar", lambda e, ft=ft, buf=buf: e.activation(
            out=buf[:], in_=xf[:, ft * TC:(ft + 1) * TC], func=AF.Square),
            waits=w, inc="ws")
        sq_done[ft % 2] = p.step("tensor", lambda e, buf=buf, ft=ft: e.matmul(
            ps_bc1[0:1, :], onesc[:], buf[:],
            start=(ft == 0), stop=(ft == NFT - 1)),
            waits=[("ws", s_tgt)], inc="wt")
    t_sq = sq_done[(NFT - 1) % 2]
    R = TC
    mean_r = rowbuf[0:1, 0:R]
    var_r = rowbuf[0:1, R:2 * R]
    rstd_r = rowbuf[0:1, 2 * R:3 * R]
    v1 = p.step("vector", [
        lambda e: e.tensor_scalar(out=mean_r, in0=ps_bc0[0:1, :], scalar1=1.0 / DM,
                                  scalar2=None, op0=ALU.mult),
        lambda e: e.tensor_tensor(out=var_r, in0=mean_r, in1=mean_r, op=ALU.mult),
        lambda e: e.scalar_tensor_tensor(out=var_r, in0=ps_bc1[0:1, :], scalar=1.0 / DM,
                                         in1=var_r, op0=ALU.mult, op1=ALU.subtract),
    ], waits=[("wt", t_mean), ("wt", t_sq)], inc="wv")
    s1 = p.step("scalar", [
        lambda e: e.activation(out=rstd_r, in_=var_r, func=AF.Ln, bias=fconst[0:1, 3:4]),
        lambda e: e.activation(out=rstd_r, in_=rstd_r, func=AF.Exp, scale=fconst[0:1, 2:3]),
    ], waits=[("wv", v1)], inc="ws")
    t_bc = p.step("tensor", [
        lambda e: e.matmul(ps_bc0[:], ones1[:], mean_r, start=True, stop=True),
        lambda e: e.matmul(ps_bc1[:], ones1[:], rstd_r, start=True, stop=True),
    ], waits=[("ws", s1), ("wv", v1)], inc="wt")
    last_v = None
    for ft in range(NFT):
        buf = tmp1 if ft % 2 == 0 else tmp2
        w = [("wt", t_bc)] if ft < 2 else []
        w2 = out_cb(ft, None, None, probe_wait=True)
        if w2:
            w += w2
        last_v = p.step("vector", [
            lambda e, ft=ft, buf=buf: e.tensor_tensor(
                out=buf[:], in0=xf[:, ft * TC:(ft + 1) * TC], in1=ps_bc0[:], op=ALU.subtract),
            lambda e, buf=buf: e.tensor_tensor(out=buf[:], in0=buf[:], in1=ps_bc1[:], op=ALU.mult),
        ], waits=w, inc="wv")
        out_cb(ft, buf, last_v, probe_wait=False)
    return last_v


def emit_phase1(p):
    nc = p.nc
    xT = p.io["x_in"]
    x1 = p.io["x_out"]
    wqkv_img = p.dram("wqkv_img", [24, 128, DM], F32, "ExternalInput")
    wproj_img = p.dram("wproj_img", [NFT, 128, DM], F32, "ExternalInput")
    rope_cos = p.dram("rope_cos", [128, T], F32, "ExternalInput")
    rope_sin = p.dram("rope_sin", [128, T], F32, "ExternalInput")
    rot_st = p.dram("rot_st", [128, 128], F32, "ExternalInput")
    m_low_d = p.dram("m_low", [128, 128], BF16, "ExternalInput")
    m_up_d = p.dram("m_up", [128, 128], BF16, "ExternalInput")
    ident_d = p.dram("ident_bf", [128, 128], BF16, "ExternalInput")
    ones_st = p.dram("ones_st", [1, 128], F32, "ExternalInput")
    ones_col = p.dram("ones_col", [128, 1], F32, "ExternalInput")
    ones_col_bf = p.dram("ones_col_bf", [128, 1], BF16, "ExternalInput")
    fconst_d = p.dram("fconst", [1, 8], F32, "ExternalInput")

    for s in ("ld", "lw", "wv", "ws", "wt", "st"):
        p.sem(s)

    cosT = p.sb("cosT", [128, T], F32)
    sinT = p.sb("sinT", [128, T], F32)
    rotS = p.sb("rotS", [128, 128], F32)
    mlow = p.sb("mlow", [128, 128], BF16)
    mup = p.sb("mup", [128, 128], BF16)
    ident = p.sb("ident", [128, 128], BF16)
    ones1 = p.sb("ones1", [1, 128], F32)
    onesc = p.sb("onesc", [128, 1], F32)
    onescb = p.sb("onescb", [128, 1], BF16)
    fconst = p.sb("fconst_sb", [1, 8], F32)

    xf = p.sb("xf", [128, NFT * TC], F32)
    xb = p.sb("xb", [128, NFT * TC], BF16)
    kf = p.sb("kf", [128, HKV * TC], F32)
    vbf = p.sb("vbf", [128, HKV * TC], BF16)
    qn = p.sb("qn", [128, HQ * TC], BF16)
    kn_ring = p.sb("kn_ring", [128, 2 * HKV * TC], BF16)
    vt_ring = p.sb("vt_ring", [128, 2 * HKV * 4 * 128], BF16)
    Ebuf = p.sb("Ebuf", [128, 2 * TC], BF16)
    ybuf = p.sb("ybuf", [128, HQ * TC], BF16)
    qtile = p.sb("qtile", [128, TC], F32)
    tmp1 = p.sb("tmp1", [128, TC], F32)
    tmp2 = p.sb("tmp2", [128, TC], F32)
    ytmp = p.sb("ytmp", [128, TC], F32)
    rows = p.sb("rows", [1, 3 * TC], F32)
    wgrp = [p.sb(f"wgrp{i}", [128, NFT * 128], BF16) for i in range(2)]

    ps_mm = [p.ps(f"ps_mm{i}", [128, TC]) for i in range(2)]
    ps_sc = [p.ps(f"ps_sc{i}", [128, TC]) for i in range(2)]
    ps_y = p.ps("ps_y", [128, TC])
    ps_bc = [p.ps(f"ps_bc{i}", [128, TC]) for i in range(2)]
    ps_tr = p.ps("ps_tr", [128, 128], BF16)

    for dst, src in [(cosT[:], rope_cos), (sinT[:], rope_sin), (rotS[:], rot_st),
                     (mlow[:], m_low_d), (mup[:], m_up_d), (ident[:], ident_d),
                     (ones1[:], ones_st), (onesc[:], ones_col), (onescb[:], ones_col_bf),
                     (fconst[:], fconst_d)]:
        p.dma("sync", dst, src, "ld")
    const_ld = p.ctr["ld"]

    SCL = 1.0 / math.sqrt(DH)

    wbuf_free = {0: None, 1: None}
    bank_free = {0: None, 1: None}
    sc_free = {0: None, 1: None}
    ebuf_free = {0: None, 1: None}
    bc1_free = None
    out_dma = {0: None, 1: None}
    xf_free = None
    ring_ok = []
    prev_attn_done = None

    for c in range(NCH):
        t0 = c * TC
        slot = c % 2

        if c == 0:
            w = [("ld", const_ld)]
            if p.ctr["st"] > 0:
                w.append(("st", p.ctr["st"]))
        else:
            w = [("wv", xf_free)]
        ldx = None
        for ft in range(NFT):
            ldx = p.dma("sync", xf[:, ft * TC:(ft + 1) * TC],
                        xT[ft * 128:(ft + 1) * 128, t0:t0 + TC], "ld",
                        waits=(w if ft == 0 else ()))
        cast_done = None
        for ft in range(NFT):
            cast_done = p.step("vector", lambda e, ft=ft: e.tensor_copy(
                out=xb[:, ft * TC:(ft + 1) * TC], in_=xf[:, ft * TC:(ft + 1) * TC]),
                waits=([("ld", ldx)] if ft == 0 else ()), inc="wv")

        for rt in range(24):
            buf = rt % 2
            w = [] if wbuf_free[buf] is None else [("wt", wbuf_free[buf])]
            lw_tgt = p.dma("gpsimd", wgrp[buf][:], wqkv_img[rt], "lw", waits=w)
            bank = rt % 2
            w = [("lw", lw_tgt), ("wv", cast_done)]
            if bank_free[bank] is not None:
                w.append(("wv", bank_free[bank]))
            mms = [lambda e, ci=ci, buf=buf, bank=bank: e.matmul(
                ps_mm[bank][:], wgrp[buf][:, ci * 128:(ci + 1) * 128],
                xb[:, ci * TC:(ci + 1) * TC], start=(ci == 0), stop=(ci == NFT - 1))
                for ci in range(NFT)]
            mm_tgt = p.step("tensor", mms, waits=w, inc="wt")
            wbuf_free[buf] = mm_tgt
            if rt < 20:
                if rt < 16:
                    ev = p.step("vector", lambda e, bank=bank: e.tensor_copy(
                        out=qtile[:], in_=ps_mm[bank][:]), waits=[("wt", mm_tgt)], inc="wv")
                    src = qtile
                    srcoff = 0
                else:
                    ev = p.step("vector", lambda e, bank=bank, rt=rt: e.tensor_copy(
                        out=kf[:, (rt - 16) * TC:(rt - 15) * TC], in_=ps_mm[bank][:]),
                        waits=[("wt", mm_tgt)], inc="wv")
                    src = kf
                    srcoff = (rt - 16) * TC
                bank_free[bank] = ev
                t_rot = p.step("tensor", lambda e, src=src, srcoff=srcoff: e.matmul(
                    ps_bc[0][:], rotS[:], src[:, srcoff:srcoff + TC], start=True, stop=True),
                    waits=[("wv", ev)], inc="wt")
                v_rope = p.step("vector", [
                    lambda e, src=src, srcoff=srcoff, t0=t0: e.tensor_tensor(
                        out=tmp1[:], in0=src[:, srcoff:srcoff + TC],
                        in1=cosT[:, t0:t0 + TC], op=ALU.mult),
                    lambda e, t0=t0: e.tensor_tensor(out=tmp2[:], in0=ps_bc[0][:],
                                                     in1=sinT[:, t0:t0 + TC], op=ALU.mult),
                    lambda e: e.tensor_tensor(out=tmp1[:], in0=tmp1[:], in1=tmp2[:], op=ALU.add),
                ], waits=[("wt", t_rot)], inc="wv")
                s_sq = p.step("scalar", lambda e: e.activation(out=tmp2[:], in_=tmp1[:], func=AF.Square),
                              waits=[("wv", v_rope)], inc="ws")
                t_ss = p.step("tensor", lambda e: e.matmul(ps_bc[0][0:1, :], onesc[:], tmp2[:],
                                                           start=True, stop=True),
                              waits=[("ws", s_sq)], inc="wt")
                s_rstd = p.step("scalar", [
                    lambda e: e.activation(out=rows[0:1, 0:TC], in_=ps_bc[0][0:1, :],
                                           func=AF.Ln, scale=fconst[0:1, 1:2], bias=fconst[0:1, 0:1]),
                    lambda e: e.activation(out=rows[0:1, 0:TC], in_=rows[0:1, 0:TC],
                                           func=AF.Exp, scale=fconst[0:1, 2:3]),
                ], waits=[("wt", t_ss)], inc="ws")
                t_bc = p.step("tensor", lambda e: e.matmul(ps_bc[0][:], ones1[:], rows[0:1, 0:TC],
                                                           start=True, stop=True),
                              waits=[("ws", s_rstd)], inc="wt")
                if rt < 16:
                    dst = qn[:, rt * TC:(rt + 1) * TC]
                    scl = SCL
                else:
                    dst = kn_ring[:, (slot * HKV + (rt - 16)) * TC:(slot * HKV + (rt - 16) + 1) * TC]
                    scl = 1.0
                v_qn = p.step("vector", lambda e, dst=dst, scl=scl: e.scalar_tensor_tensor(
                    out=dst, in0=tmp1[:], scalar=scl, in1=ps_bc[0][:], op0=ALU.mult, op1=ALU.mult),
                    waits=[("wt", t_bc)], inc="wv")
            else:
                ev = p.step("vector", lambda e, bank=bank, rt=rt: e.tensor_copy(
                    out=vbf[:, (rt - 20) * TC:(rt - 19) * TC], in_=ps_mm[bank][:]),
                    waits=[("wt", mm_tgt)], inc="wv")
                bank_free[bank] = ev

        v_stageQ = p.ctr["wv"]

        tr_evac = None
        for g in range(HKV):
            for kb in range(4):
                w = [("wv", v_stageQ)] if (g == 0 and kb == 0) else [("wv", tr_evac)]
                t_tr = p.step("tensor", lambda e, g=g, kb=kb: e.transpose(
                    ps_tr[:], vbf[:, g * TC + kb * 128:g * TC + (kb + 1) * 128], ident[:]),
                    waits=w, inc="wt")
                tr_evac = p.step("vector", lambda e, g=g, kb=kb, slot=slot: e.tensor_copy(
                    out=vt_ring[:, ((slot * HKV + g) * 4 + kb) * 128:((slot * HKV + g) * 4 + kb + 1) * 128],
                    in_=ps_tr[:]), waits=[("wt", t_tr)], inc="wv")
        ring_ok.append(p.ctr["wv"])

        kb_list = [kb for kb in range(4 * c - 4, 4 * c + 4) if kb >= 0]

        def kb_addr(kb):
            cc = kb // 4
            sl = cc % 2
            return sl, kb % 4

        for g in range(HKV):
            for qh in range(4 * g, 4 * g + 4):
                av_done = None
                first = True
                for j, kb in enumerate(kb_list):
                    sl, kbl = kb_addr(kb)
                    scb = j % 2
                    eb = j % 2
                    w = [("wv", ring_ok[c])]
                    if sc_free[scb] is not None:
                        w.append(("ws", sc_free[scb]))
                    kslice = kn_ring[:, (sl * HKV + g) * TC + kbl * 128:(sl * HKV + g) * TC + (kbl + 1) * 128]
                    t_sc = p.step("tensor", lambda e, scb=scb, kslice=kslice, qh=qh: e.matmul(
                        ps_sc[scb][:], kslice, qn[:, qh * TC:(qh + 1) * TC], start=True, stop=True),
                        waits=w, inc="wt")
                    mask_fns = []
                    inval = []
                    for qb in range(4):
                        wdist = 4 * c + qb - kb
                        if wdist == 0:
                            mask_fns.append(lambda e, scb=scb, qb=qb: e.tensor_tensor(
                                out=ps_sc[scb][:, qb * 128:(qb + 1) * 128],
                                in0=ps_sc[scb][:, qb * 128:(qb + 1) * 128], in1=mlow[:], op=ALU.add))
                        elif wdist == 4:
                            mask_fns.append(lambda e, scb=scb, qb=qb: e.tensor_tensor(
                                out=ps_sc[scb][:, qb * 128:(qb + 1) * 128],
                                in0=ps_sc[scb][:, qb * 128:(qb + 1) * 128], in1=mup[:], op=ALU.add))
                        elif wdist < 0 or wdist > 4:
                            inval.append(qb)
                    v_mask = None
                    if mask_fns:
                        v_mask = p.step("vector", mask_fns, waits=[("wt", t_sc)], inc="wv")
                    w = [("wv", v_mask)] if v_mask is not None else [("wt", t_sc)]
                    if ebuf_free[eb] is not None:
                        w.append(("wt", ebuf_free[eb]))
                    s_exp = p.step("scalar", lambda e, scb=scb, eb=eb: e.activation(
                        out=Ebuf[:, eb * TC:(eb + 1) * TC], in_=ps_sc[scb][:], func=AF.Exp),
                        waits=w, inc="ws")
                    sc_free[scb] = s_exp
                    v_zero = None
                    if inval:
                        v_zero = p.step("vector", [
                            lambda e, eb=eb, qb=qb: e.memset(
                                Ebuf[:, eb * TC + qb * 128:eb * TC + (qb + 1) * 128], 0.0)
                            for qb in inval], waits=[("ws", s_exp)], inc="wv")
                    w = [("wv", v_zero)] if v_zero is not None else [("ws", s_exp)]
                    if first and bc1_free is not None:
                        w.append(("wv", bc1_free))
                    vslice = vt_ring[:, ((sl * HKV + g) * 4 + kbl) * 128:((sl * HKV + g) * 4 + kbl + 1) * 128]
                    last = (j == len(kb_list) - 1)
                    av_done = p.step("tensor", [
                        lambda e, eb=eb, first=first, last=last: e.matmul(
                            ps_bc[1][0:1, :], onescb[:], Ebuf[:, eb * TC:(eb + 1) * TC],
                            start=first, stop=last),
                        lambda e, eb=eb, vslice=vslice, first=first, last=last: e.matmul(
                            ps_y[:], vslice, Ebuf[:, eb * TC:(eb + 1) * TC],
                            start=first, stop=last),
                    ], waits=w, inc="wt")
                    ebuf_free[eb] = av_done
                    first = False
                v_rec = p.step("vector", [
                    lambda e: e.reciprocal(out=rows[0:1, TC:2 * TC], in_=ps_bc[1][0:1, :]),
                    lambda e: e.tensor_copy(out=ytmp[:], in_=ps_y[:]),
                ], waits=[("wt", av_done)], inc="wv")
                t_dbc = p.step("tensor", lambda e: e.matmul(
                    ps_bc[1][:], ones1[:], rows[0:1, TC:2 * TC], start=True, stop=True),
                    waits=[("wv", v_rec)], inc="wt")
                bc1_free = p.step("vector", lambda e, qh=qh: e.tensor_tensor(
                    out=ybuf[:, qh * TC:(qh + 1) * TC], in0=ytmp[:], in1=ps_bc[1][:], op=ALU.mult),
                    waits=[("wt", t_dbc)], inc="wv")

        v_attn = p.ctr["wv"]

        res_done = None
        for ot in range(NFT):
            buf = ot % 2
            w = [("wt", wbuf_free[buf])] if wbuf_free[buf] is not None else []
            lw_tgt = p.dma("gpsimd", wgrp[buf][:], wproj_img[ot], "lw", waits=w)
            bank = ot % 2
            w = [("lw", lw_tgt), ("wv", v_attn)]
            if bank_free[bank] is not None:
                w.append(("wv", bank_free[bank]))
            mms = [lambda e, dt=dt, buf=buf, bank=bank: e.matmul(
                ps_mm[bank][:], wgrp[buf][:, dt * 128:(dt + 1) * 128],
                ybuf[:, dt * TC:(dt + 1) * TC], start=(dt == 0), stop=(dt == NFT - 1))
                for dt in range(NFT)]
            mm_tgt = p.step("tensor", mms, waits=w, inc="wt")
            wbuf_free[buf] = mm_tgt
            res_done = p.step("vector", lambda e, ot=ot, bank=bank: e.tensor_tensor(
                out=xf[:, ot * TC:(ot + 1) * TC], in0=xf[:, ot * TC:(ot + 1) * TC],
                in1=ps_mm[bank][:], op=ALU.add), waits=[("wt", mm_tgt)], inc="wv")
            bank_free[bank] = res_done

        def out_cb(ft, buf, v_tgt, probe_wait):
            if probe_wait:
                d = out_dma[ft % 2]
                return [("st", d)] if d is not None else []
            out_dma[ft % 2] = p.dma("sync", x1[ft * 128:(ft + 1) * 128, t0:t0 + TC],
                                    buf[:], "st", waits=[("wv", v_tgt)])
            return None

        xf_free = emit_ln_chunk(p, xf, None, ps_bc[0], ps_bc[1], rows, tmp1, tmp2,
                                out_cb, onesc, ones1, fconst,
                                wt_after=[("wv", res_done)], tag=f"ln{c}")

    return


DH_FFN = 8192
NHT = DH_FFN // 128


def emit_phase2(p):
    nc = p.nc
    xT = p.io["x_in"]
    x2 = p.io["x_out"]
    w1_img = p.dram("w1_img", [NHT, 128, DM], F32, "ExternalInput")
    w3_img = p.dram("w3_img", [NHT, 128, DM], F32, "ExternalInput")
    w2_img = p.dram("w2_img", [NFT, 128, DH_FFN], F32, "ExternalInput")
    h128_d = p.dram("h128_bf", [128, 128], BF16, "ExternalInput")
    ones_st = p.dram("ones_st", [1, 128], F32, "ExternalInput")
    ones_col = p.dram("ones_col", [128, 1], F32, "ExternalInput")
    fconst_d = p.dram("fconst", [1, 8], F32, "ExternalInput")
    w2h = p.dram("w2h", [NFT, 128, DH_FFN], BF16, "Internal")

    for s in ("ld", "lw", "wv", "ws", "wt", "st"):
        p.sem(s)

    h128 = p.sb("h128", [128, 128], BF16)
    ones1 = p.sb("ones1", [1, 128], F32)
    onesc = p.sb("onesc", [128, 1], F32)
    fconst = p.sb("fconst_sb", [1, 8], F32)

    xf = p.sb("xf", [128, NFT * TC], F32)
    xb = p.sb("xb", [128, NFT * TC], BF16)
    hidH = p.sb("hidH", [128, NHT * TC], BF16)
    hidt = p.sb("hidt", [128, TC], BF16)
    tmpS = p.sb("tmpS", [128, TC], F32)
    tmp1 = p.sb("tmp1", [128, TC], F32)
    tmp2 = p.sb("tmp2", [128, TC], F32)
    rows = p.sb("rows", [1, 4 * TC], F32)
    w1g = [p.sb(f"w1g{i}", [128, NFT * 128], BF16) for i in range(2)]
    w3g = [p.sb(f"w3g{i}", [128, NFT * 128], BF16) for i in range(2)]
    w2g = p.sb("w2g", [128, NHT * 128], BF16)
    wbig = [p.sb(f"wbig{i}", [128, NHT * 128], BF16) for i in range(2)]

    ps_g = [p.ps(f"ps_g{i}", [128, TC]) for i in range(2)]
    ps_l = [p.ps(f"ps_l{i}", [128, TC]) for i in range(2)]
    ps_h = p.ps("ps_h", [128, TC])
    ps_bc = [p.ps(f"ps_bc{i}", [128, TC]) for i in range(2)]

    for dst, src in [(h128[:], h128_d), (ones1[:], ones_st), (onesc[:], ones_col),
                     (fconst[:], fconst_d)]:
        p.dma("sync", dst, src, "ld")
    const_ld = p.ctr["ld"]

    IS = 1.0 / math.sqrt(DH_FFN)
    st_prev = None
    v_last_bfly = None
    for ob in range(NFT):
        w = [] if v_last_bfly is None else [("wv", v_last_bfly)]
        if st_prev is not None:
            w.append(("st", st_prev))
        lw = p.dma("gpsimd", wbig[0][:], w2_img[ob], "lw", waits=w)
        for s in range(6):
            src = wbig[s % 2]
            dstb = wbig[(s + 1) % 2]
            d = 1 << s
            fns = []
            for i in range(NHT):
                j = i ^ d
                lo = min(i, j)
                op = ALU.add if i == lo else ALU.subtract
                a = lo * 128
                b = (lo ^ d) * 128
                fns.append(lambda e, i=i, a=a, b=b, op=op, src=src, dstb=dstb: e.tensor_tensor(
                    out=dstb[:, i * 128:(i + 1) * 128],
                    in0=src[:, a:a + 128], in1=src[:, b:b + 128], op=op))
            w = [("lw", lw)] if s == 0 else []
            v_last_bfly = p.step("vector", fns, waits=w, inc="wv")
        st_prev = p.dma("sync", w2h[ob], wbig[0][:], "st", waits=[("wv", v_last_bfly)])
    w2h_ready = st_prev

    w1buf_free = {0: None, 1: None}
    w3buf_free = {0: None, 1: None}
    w2buf_free = None
    g_free = {0: None, 1: None}
    l_free = {0: None, 1: None}
    h_free = None
    hidt_free = None
    tmpS_free = None
    out_dma = {0: None, 1: None}
    xf_free = None

    for c in range(NCH):
        t0 = c * TC

        if c == 0:
            w = [("ld", const_ld)]
            if p.ctr["st"] > 0:
                w.append(("st", p.ctr["st"]))
        else:
            w = [("wv", xf_free)]
        ldx = None
        for ft in range(NFT):
            ldx = p.dma("sync", xf[:, ft * TC:(ft + 1) * TC],
                        xT[ft * 128:(ft + 1) * 128, t0:t0 + TC], "ld",
                        waits=(w if ft == 0 else ()))
        cast_done = None
        for ft in range(NFT):
            cast_done = p.step("vector", lambda e, ft=ft: e.tensor_copy(
                out=xb[:, ft * TC:(ft + 1) * TC], in_=xf[:, ft * TC:(ft + 1) * TC]),
                waits=([("ld", ldx)] if ft == 0 else ()), inc="wv")

        ev_hid = None
        for ht in range(NHT):
            buf = ht % 2
            bank = ht % 2
            w = [] if w1buf_free[buf] is None else [("wt", w1buf_free[buf])]
            lw1 = p.dma("gpsimd", w1g[buf][:], w1_img[ht], "lw", waits=w)
            w = [] if w3buf_free[buf] is None else [("wt", w3buf_free[buf])]
            lw3 = p.dma("gpsimd", w3g[buf][:], w3_img[ht], "lw", waits=w)
            w = [("lw", lw1), ("wv", cast_done)]
            if g_free[bank] is not None:
                w.append(("ws", g_free[bank]))
            mmG = p.step("tensor", [lambda e, ci=ci, buf=buf, bank=bank: e.matmul(
                ps_g[bank][:], w1g[buf][:, ci * 128:(ci + 1) * 128],
                xb[:, ci * TC:(ci + 1) * TC], start=(ci == 0), stop=(ci == NFT - 1))
                for ci in range(NFT)], waits=w, inc="wt")
            w1buf_free[buf] = mmG
            w = [("lw", lw3)]
            if l_free[bank] is not None:
                w.append(("wv", l_free[bank]))
            mmL = p.step("tensor", [lambda e, ci=ci, buf=buf, bank=bank: e.matmul(
                ps_l[bank][:], w3g[buf][:, ci * 128:(ci + 1) * 128],
                xb[:, ci * TC:(ci + 1) * TC], start=(ci == 0), stop=(ci == NFT - 1))
                for ci in range(NFT)], waits=w, inc="wt")
            w3buf_free[buf] = mmL
            w = [("wt", mmG)]
            if tmpS_free is not None:
                w.append(("wv", tmpS_free))
            s_silu = p.step("scalar", lambda e, bank=bank: e.activation(
                out=tmpS[:], in_=ps_g[bank][:], func=AF.Silu), waits=w, inc="ws")
            g_free[bank] = s_silu
            w = [("ws", s_silu), ("wt", mmL)]
            if hidt_free is not None:
                w.append(("wt", hidt_free))
            v_mul = p.step("vector", lambda e, bank=bank: e.tensor_tensor(
                out=hidt[:], in0=tmpS[:], in1=ps_l[bank][:], op=ALU.mult),
                waits=w, inc="wv")
            tmpS_free = v_mul
            l_free[bank] = v_mul
            w = [("wv", v_mul)]
            if h_free is not None:
                w.append(("wv", h_free))
            mmH = p.step("tensor", lambda e: e.matmul(
                ps_h[:], h128[:], hidt[:], start=True, stop=True), waits=w, inc="wt")
            hidt_free = mmH
            ev_hid = p.step("vector", lambda e, ht=ht: e.tensor_scalar(
                out=hidH[:, ht * TC:(ht + 1) * TC], in0=ps_h[:], scalar1=IS,
                scalar2=None, op0=ALU.mult), waits=[("wt", mmH)], inc="wv")
            h_free = ev_hid

        res_done = None
        for ot in range(NFT):
            w = [("st", w2h_ready)] if (c == 0 and ot == 0) else []
            if w2buf_free is not None:
                w.append(("wt", w2buf_free))
            lw2a = p.dma("gpsimd", w2g[:, 0:4096], w2h[ot][:, 0:4096], "lw", waits=w)
            w = [("wt", w2buf_free)] if w2buf_free is not None else []
            lw2b = p.dma("gpsimd", w2g[:, 4096:8192], w2h[ot][:, 4096:8192], "lw", waits=w)
            bank = ot % 2
            w = [("lw", lw2a), ("wv", ev_hid)]
            if g_free[bank] is not None:
                w.append(("ws", g_free[bank]))
            if l_free[bank] is not None:
                w.append(("wv", l_free[bank]))
            mm_a = p.step("tensor", [lambda e, kt=kt, bank=bank: e.matmul(
                ps_g[bank][:], w2g[:, kt * 128:(kt + 1) * 128],
                hidH[:, kt * TC:(kt + 1) * TC], start=(kt == 0), stop=False)
                for kt in range(NHT // 2)], waits=w, inc="wt")
            mm = p.step("tensor", [lambda e, kt=kt, bank=bank: e.matmul(
                ps_g[bank][:], w2g[:, kt * 128:(kt + 1) * 128],
                hidH[:, kt * TC:(kt + 1) * TC], start=False, stop=(kt == NHT - 1))
                for kt in range(NHT // 2, NHT)], waits=[("lw", lw2b)], inc="wt")
            w2buf_free = mm
            res_done = p.step("vector", lambda e, ot=ot, bank=bank: e.tensor_tensor(
                out=xf[:, ot * TC:(ot + 1) * TC], in0=xf[:, ot * TC:(ot + 1) * TC],
                in1=ps_g[bank][:], op=ALU.add), waits=[("wt", mm)], inc="wv")
            g_free[bank] = None
            l_free[bank] = None
            l_free[bank] = res_done

        if DEBUG_DUMP and c == 0:
            dbg_xf = nc.dram_tensor("dbg_xf", [128, NFT * TC], F32, kind="ExternalOutput").ap()
            p.dma("sync", dbg_xf, xf[:], "st", waits=[("wv", res_done)])

        def out_cb(ft, buf, v_tgt, probe_wait):
            if probe_wait:
                d = out_dma[ft % 2]
                return [("st", d)] if d is not None else []
            out_dma[ft % 2] = p.dma("sync", x2[ft * 128:(ft + 1) * 128, t0:t0 + TC],
                                    buf[:], "st", waits=[("wv", v_tgt)])
            return None

        xf_free = emit_ln_chunk(p, xf, None, ps_bc[0], ps_bc[1], rows, tmp1, tmp2,
                                out_cb, onesc, ones1, fconst,
                                wt_after=[("wv", res_done)], tag=f"l2ln{c}")

    return


def run_launch2(x1T_list, inputs, n_cores=8):
    c = host_consts()
    h = np.zeros((128, 128), np.float64)
    for i in range(128):
        for j in range(128):
            h[i, j] = 1.0 if bin(i & j).count("1") % 2 == 0 else -1.0
    base = {
        "w1_img": tile_img(inputs["L0_ffn_W1"].T.astype(np.float32)),
        "w3_img": tile_img(inputs["L0_ffn_W3"].T.astype(np.float32)),
        "w2_img": tile_img(inputs["L0_ffn_W2"].T.astype(np.float32)),
        "h128_bf": h.astype(BF),
        "ones_st": c["ones_st"], "ones_col": c["ones_col"], "fconst": c["fconst"],
    }
    nc = build_single(2, "x2")
    in_maps = []
    for xT in x1T_list:
        m = dict(base)
        m["xT"] = xT
        in_maps.append(m)
    res = run_bass_kernel_spmd(nc, in_maps, core_ids=list(range(n_cores)))
    if DEBUG_DUMP:
        return [np.asarray(r["x2"], dtype=np.float32) for r in res.results], res.results
    return [np.asarray(r["x2"], dtype=np.float32) for r in res.results]


DIN = 4096
NDT = DIN // 128
DST = 16


def emit_phase3(p):
    nc = p.nc
    xT = p.io["x_in"]
    x3 = p.io["x_out"]
    win_img = p.dram("win_img", [64, 128, DM], F32, "ExternalInput")
    outw_img = p.dram("outw_img", [NFT, 128, DIN], F32, "ExternalInput")
    conv_cols_d = p.dram("conv_cols", [128, NDT * 4], F32, "ExternalInput")
    convb_d = p.dram("convb_cols", [128, NDT], F32, "ExternalInput")
    wxb_d = p.dram("wxb_cols", [128, NDT * DST], F32, "ExternalInput")
    wxc_d = p.dram("wxc_cols", [128, NDT * DST], F32, "ExternalInput")
    wxdt_d = p.dram("wxdt_cols", [128, NDT], F32, "ExternalInput")
    a2_d = p.dram("a2_cols", [128, NDT * DST], F32, "ExternalInput")
    dtb_d = p.dram("dtb_cols", [128, NDT], F32, "ExternalInput")
    dcols_d = p.dram("d_cols", [128, NDT], F32, "ExternalInput")
    ones_st = p.dram("ones_st", [1, 128], F32, "ExternalInput")
    ones_stb_d = p.dram("ones_st_bf", [1, 128], BF16, "ExternalInput")
    ones_col = p.dram("ones_col", [128, 1], F32, "ExternalInput")
    fconst_d = p.dram("fconst", [1, 8], F32, "ExternalInput")
    sel16_d = p.dram("sel16", [DST, DST * 128], BF16, "ExternalInput")
    zscr = p.dram("z_scr", [DIN, T], BF16, "Internal")

    for s in ("ld", "lw", "wv", "ws", "wt", "st"):
        p.sem(s)

    ones1 = p.sb("ones1", [1, 128], F32)
    ones1b = p.sb("ones1b", [1, 128], BF16)
    sel16 = p.sb("sel16_sb", [DST, DST * 128], BF16)
    onesc = p.sb("onesc", [128, 1], F32)
    fconst = p.sb("fconst_sb", [1, 8], F32)
    conv_cols = p.sb("conv_cols_sb", [128, NDT * 4], F32)
    convb = p.sb("convb_sb", [128, NDT], F32)
    wxb = p.sb("wxb", [128, NDT * DST], BF16)
    wxc = p.sb("wxc", [128, NDT * DST], BF16)
    wxdt = p.sb("wxdt", [128, NDT], BF16)
    a2 = p.sb("a2", [128, NDT * DST], F32)
    dtb = p.sb("dtb", [128, NDT], F32)
    dcols = p.sb("dcols", [128, NDT], F32)
    mb = p.sb("mb", [1, 1], F32)

    xf = p.sb("xf", [128, NFT * TC], F32)
    xb = p.sb("xb", [128, NFT * TC], BF16)
    xcb = p.sb("xcb", [128, NDT * TC], BF16)
    dtBrep = p.sb("dtBrep", [128, DST * TC], BF16)
    crepA = p.sb("crepA", [128, DST * TC], BF16)
    zst = [p.sb(f"zst{i}", [128, TC], BF16) for i in range(2)]
    hist = p.sb("hist", [128, NDT * 3], F32)
    xs_tmp = p.sb("xs_tmp", [128, TC + 3], F32)
    convacc = p.sb("convacc", [128, TC], F32)
    B_sb = p.sb("B_sb", [DST, TC], BF16)
    C_sb = p.sb("C_sb", [DST, TC], BF16)
    dtB_sb = p.sb("dtB_sb", [DST, TC], BF16)
    rowm = p.sb("rowm", [1, 3 * TC], F32)
    carry = p.sb("carry", [128, 2 * NDT * DST], F32)
    abar = [p.sb(f"abar{i}", [128, TC], F32) for i in range(2)]
    u_sb = [p.sb(f"u_sb{i}", [128, TC], BF16) for i in range(2)]
    hbf = [p.sb(f"hbf{i}", [128, TC], BF16) for i in range(2)]
    wbf = [p.sb(f"wbf{i}", [128, TC], BF16) for i in range(2)]
    h_tmp = p.sb("h_tmp", [128, TC], F32)
    w_tmp = p.sb("w_tmp", [128, TC], F32)
    yacc = p.sb("yacc", [128, TC], F32)
    ytmp2 = p.sb("ytmp2", [128, TC], F32)
    wgi = [p.sb("wgi0", [128, DM], BF16)] * 2
    wgo = [p.sb("wgo0", [128, DIN], BF16)] * 2

    ps_mm = [p.ps(f"ps_mm{i}", [128, TC]) for i in range(2)]
    ps_B = p.ps("ps_B", [DST, TC])
    ps_C = p.ps("ps_C", [DST, TC])
    ps_dt = p.ps("ps_dt", [1, TC])
    ps_bc = [p.ps(f"ps_bc{i}", [128, TC]) for i in range(3)]

    for dst, src in [(ones1[:], ones_st), (ones1b[:], ones_stb_d), (onesc[:], ones_col),
                     (fconst[:], fconst_d),
                     (conv_cols[:], conv_cols_d), (convb[:], convb_d),
                     (a2[:], a2_d), (dtb[:], dtb_d), (dcols[:], dcols_d),
                     (sel16[:], sel16_d)]:
        p.dma("sync", dst, src, "ld")
    for dst, src in [(wxb[:], wxb_d), (wxc[:], wxc_d), (wxdt[:], wxdt_d)]:
        p.dma("gpsimd", dst, src, "ld")
    const_ld = p.ctr["ld"]

    v0 = p.step("vector", [
        lambda e: e.memset(carry[:], 0.0),
        lambda e: e.memset(hist[:], 0.0),
        lambda e: e.reduce_sum(out=h_tmp[:, 0:1], in_=dtb[:], axis=AX.X),
    ], waits=[("ld", const_ld)], inc="wv")
    s0 = p.step("scalar", lambda e: e.activation(out=a2[:], in_=a2[:], func=AF.Exp),
                waits=[("ld", const_ld)], inc="ws")
    v0b = p.step("vector", lambda e: e.tensor_scalar(out=a2[:], in0=a2[:], scalar1=-1.0,
                                                     scalar2=None, op0=ALU.mult),
                 waits=[("ws", s0)], inc="wv")
    t0_ = p.step("tensor", lambda e: e.matmul(ps_dt[0:1, 0:1], onesc[:], h_tmp[:, 0:1],
                                              start=True, stop=True),
                 waits=[("wv", v0)], inc="wt")
    v0c = p.step("vector", lambda e: e.tensor_scalar(out=mb[:], in0=ps_dt[0:1, 0:1],
                                                     scalar1=1.0 / DIN, scalar2=None,
                                                     op0=ALU.mult),
                 waits=[("wt", t0_)], inc="wv")

    wibuf_free = {0: None, 1: None}
    wobuf_free = {0: None, 1: None}
    zst_free = {0: None, 1: None}
    zld_free = {0: None, 1: None}
    bank_busy = {0: None, 1: None}
    convacc_free = None
    bc0_sfree = None
    bc1_free = None
    bc2_free = None
    out_dma = {0: None, 1: None}
    xf_free = None

    for c in range(NCH):
        t0 = c * TC
        if c == 0:
            w = [("wv", v0c), ("ld", const_ld)]
            if p.ctr["st"] > 0:
                w.append(("st", p.ctr["st"]))
        else:
            w = [("wv", xf_free)]
        ldx = None
        for ft in range(NFT):
            ldx = p.dma("sync", xf[:, ft * TC:(ft + 1) * TC],
                        xT[ft * 128:(ft + 1) * 128, t0:t0 + TC], "ld",
                        waits=(w if ft == 0 else ()))
        cast_done = None
        for ft in range(NFT):
            cast_done = p.step("vector", lambda e, ft=ft: e.tensor_copy(
                out=xb[:, ft * TC:(ft + 1) * TC], in_=xf[:, ft * TC:(ft + 1) * TC]),
                waits=([("ld", ldx)] if ft == 0 else ()), inc="wv")

        silu_last = None
        for rt in range(64):
            buf = 0
            bank = rt % 2
            w = [] if wibuf_free[buf] is None else [("wt", wibuf_free[buf])]
            lw_t = p.dma("gpsimd", wgi[buf][:], win_img[rt], "lw", waits=w)
            w = [("lw", lw_t), ("wv", cast_done)]
            if bank_busy[bank] is not None:
                w.append(bank_busy[bank])
            mms = [lambda e, ci=ci, buf=buf, bank=bank: e.matmul(
                ps_mm[bank][:], wgi[buf][:, ci * 128:(ci + 1) * 128],
                xb[:, ci * TC:(ci + 1) * TC], start=(ci == 0), stop=(ci == NFT - 1))
                for ci in range(NFT)]
            mm = p.step("tensor", mms, waits=w, inc="wt")
            wibuf_free[buf] = mm
            if rt < NDT:
                dh = rt
                w = [("wt", mm)]
                if convacc_free is not None:
                    w.append(("ws", convacc_free))
                v_conv = p.step("vector", [
                    lambda e, dh=dh: e.tensor_copy(out=xs_tmp[:, 0:3], in_=hist[:, dh * 3:dh * 3 + 3]),
                    lambda e, bank=bank: e.tensor_copy(out=xs_tmp[:, 3:TC + 3], in_=ps_mm[bank][:]),
                    lambda e, dh=dh, bank=bank: e.tensor_copy(out=hist[:, dh * 3:dh * 3 + 3],
                                                              in_=ps_mm[bank][:, TC - 3:TC]),
                    lambda e, dh=dh: e.tensor_scalar(
                        out=convacc[:], in0=xs_tmp[:, 0:TC],
                        scalar1=conv_cols[:, dh * 4:dh * 4 + 1],
                        scalar2=convb[:, dh:dh + 1], op0=ALU.mult, op1=ALU.add),
                    lambda e, dh=dh: e.scalar_tensor_tensor(
                        out=convacc[:], in0=xs_tmp[:, 1:TC + 1],
                        scalar=conv_cols[:, dh * 4 + 1:dh * 4 + 2], in1=convacc[:],
                        op0=ALU.mult, op1=ALU.add),
                    lambda e, dh=dh: e.scalar_tensor_tensor(
                        out=convacc[:], in0=xs_tmp[:, 2:TC + 2],
                        scalar=conv_cols[:, dh * 4 + 2:dh * 4 + 3], in1=convacc[:],
                        op0=ALU.mult, op1=ALU.add),
                    lambda e, dh=dh: e.scalar_tensor_tensor(
                        out=convacc[:], in0=xs_tmp[:, 3:TC + 3],
                        scalar=conv_cols[:, dh * 4 + 3:dh * 4 + 4], in1=convacc[:],
                        op0=ALU.mult, op1=ALU.add),
                ], waits=w, inc="wv")
                bank_busy[bank] = ("wv", v_conv)
                silu_last = p.step("scalar", lambda e, dh=dh: e.activation(
                    out=xcb[:, dh * TC:(dh + 1) * TC], in_=convacc[:], func=AF.Silu),
                    waits=[("wv", v_conv)], inc="ws")
                convacc_free = silu_last
            else:
                dh = rt - NDT
                zb = dh % 2
                w = [("wt", mm)]
                if zst_free[zb] is not None:
                    w.append(("st", zst_free[zb]))
                silu_last = p.step("scalar", lambda e, zb=zb, bank=bank: e.activation(
                    out=zst[zb][:], in_=ps_mm[bank][:], func=AF.Silu),
                    waits=w, inc="ws")
                bank_busy[bank] = ("ws", silu_last)
                zst_free[zb] = p.dma("sync", zscr[dh * 128:(dh + 1) * 128, t0:t0 + TC],
                                     zst[zb][:], "st", waits=[("ws", silu_last)])

        w = [("ws", silu_last)]
        mmB = p.step("tensor", [lambda e, d=d: e.matmul(
            ps_B[:], wxb[:, d * DST:(d + 1) * DST], xcb[:, d * TC:(d + 1) * TC],
            start=(d == 0), stop=(d == NDT - 1)) for d in range(NDT)],
            waits=w, inc="wt")
        mmC = p.step("tensor", [lambda e, d=d: e.matmul(
            ps_C[:], wxc[:, d * DST:(d + 1) * DST], xcb[:, d * TC:(d + 1) * TC],
            start=(d == 0), stop=(d == NDT - 1)) for d in range(NDT)],
            inc="wt")
        mmD = p.step("tensor", [lambda e, d=d: e.matmul(
            ps_dt[:], wxdt[:, d:d + 1], xcb[:, d * TC:(d + 1) * TC],
            start=(d == 0), stop=(d == NDT - 1)) for d in range(NDT)],
            inc="wt")
        v_bc = p.step("vector", [
            lambda e: e.tensor_copy(out=B_sb[:], in_=ps_B[:]),
            lambda e: e.tensor_copy(out=C_sb[:], in_=ps_C[:]),
        ], waits=[("wt", mmB), ("wt", mmC), ("wt", mmD)], inc="wv")
        abs_r = rowm[0:1, 0:TC]
        ln_r = rowm[0:1, TC:2 * TC]
        relu_r = rowm[0:1, 0:TC]
        dt_row = rowm[0:1, 2 * TC:3 * TC]
        s_dt = p.step("scalar", [
            lambda e: e.activation(out=abs_r, in_=ps_dt[0:1, :], func=AF.Abs, bias=mb[0:1, 0:1]),
            lambda e: e.activation(out=abs_r, in_=abs_r, func=AF.Exp, scale=fconst[0:1, 4:5]),
            lambda e: e.activation(out=ln_r, in_=abs_r, func=AF.Ln, bias=1.0),
            lambda e: e.activation(out=relu_r, in_=ps_dt[0:1, :], func=AF.Relu, bias=mb[0:1, 0:1]),
        ], waits=[("wt", mmD)], inc="ws")
        v_dt = p.step("vector", lambda e: e.tensor_tensor(out=dt_row, in0=relu_r, in1=ln_r,
                                                          op=ALU.add),
                      waits=[("ws", s_dt)], inc="wv")
        w = [("wv", v_dt)]
        if bc1_free is not None:
            w.append(("wv", bc1_free))
        t_d16 = p.step("tensor", lambda e: e.matmul(ps_bc[1][0:DST, :], ones1[0:1, 0:DST],
                                                    dt_row, start=True, stop=True),
                       waits=w, inc="wt")
        v_dtb = p.step("vector", lambda e: e.tensor_tensor(
            out=dtB_sb[:], in0=B_sb[:], in1=ps_bc[1][0:DST, :], op=ALU.mult),
            waits=[("wt", t_d16), ("wv", v_bc)], inc="wv")
        bc1_free = v_dtb
        w = [("wv", v_dt)]
        if bc0_sfree is not None:
            w.append(("ws", bc0_sfree))
        t_dtrep = p.step("tensor", lambda e: e.matmul(ps_bc[0][:], ones1[:], dt_row,
                                                      start=True, stop=True),
                         waits=w, inc="wt")
        cache_done = None
        for si in range(DST):
            if si == 0:
                w = [("wv", v_dtb)]
                if bc1_free is not None:
                    w.append(("wv", bc1_free))
            else:
                w = [("wv", cache_done)]
            tb1 = p.step("tensor", lambda e, si=si: e.matmul(
                ps_bc[1][:], sel16[:, si * 128:(si + 1) * 128], dtB_sb[:],
                start=True, stop=True), waits=w, inc="wt")
            w = [("wv", bc2_free)] if (bc2_free is not None and si == 0) else []
            tb2 = p.step("tensor", lambda e, si=si: e.matmul(
                ps_bc[2][:], sel16[:, si * 128:(si + 1) * 128], C_sb[:],
                start=True, stop=True), waits=w, inc="wt")
            cache_done = p.step("vector", [
                lambda e, si=si: e.tensor_copy(out=dtBrep[:, si * TC:(si + 1) * TC],
                                               in_=ps_bc[1][:]),
                lambda e, si=si: e.tensor_copy(out=crepA[:, si * TC:(si + 1) * TC],
                                               in_=ps_bc[2][:]),
            ], waits=[("wt", tb1), ("wt", tb2)], inc="wv")
            bc1_free = cache_done
            bc2_free = cache_done

        p.sem("gp")
        y2_last = None
        abar_free = [None, None]
        u_free = [None, None]
        h_free = [None, None]
        wbf_free = {0: None, 1: None}
        pi = 0
        for dh in range(NDT):
            zb = dh % 2
            w = []
            if zld_free[zb] is not None:
                w.append(("wv", zld_free[zb]))
            if dh < 2:
                w.append(("st", zst_free[0] if zst_free[0] else 0) if False else ("st", p.ctr["st"]))
            zld = p.dma("sync", zst[zb][:], zscr[dh * 128:(dh + 1) * 128, t0:t0 + TC],
                        "lw", waits=w)
            for si in range(DST):
                pi = (dh * DST + si) % 2
                idx = dh * DST + si
                w = [("wt", t_dtrep)]
                if abar_free[pi] is not None:
                    w.append(("wv", abar_free[pi]))
                s_ab = p.step("scalar", lambda e, dh=dh, si=si, pi=pi: e.activation(
                    out=abar[pi][:], in_=ps_bc[0][:], func=AF.Exp,
                    scale=a2[:, dh * DST + si:dh * DST + si + 1]), waits=w, inc="ws")
                bc0_sfree = s_ab
                w = [("wv", cache_done)] if idx == 0 else []
                if idx == 0 and p.ctr["st"] > 0:
                    w.append(("st", p.ctr["st"]))
                if u_free[pi] is not None:
                    w.append(("wv", u_free[pi]))
                v_u = p.step("vector", lambda e, dh=dh, si=si, pi=pi: e.tensor_tensor(
                    out=u_sb[pi][:], in0=dtBrep[:, si * TC:(si + 1) * TC],
                    in1=xcb[:, dh * TC:(dh + 1) * TC], op=ALU.mult), waits=w, inc="wv")
                w = [("ws", s_ab)]
                if h_free[pi] is not None:
                    w.append(h_free[pi])
                v_scan = p.step("vector", lambda e, idx=idx, pi=pi: e.tensor_tensor_scan(
                    out=hbf[pi][:], data0=abar[pi][:], data1=u_sb[pi][:],
                    initial=carry[:, 2 * idx + 1:2 * idx + 2], op0=ALU.mult, op1=ALU.add),
                    waits=w, inc="wv")
                abar_free[pi] = v_scan
                u_free[pi] = v_scan
                wb_i = si % 2
                w = []
                if wbf_free[wb_i] is not None:
                    w.append(("gp", wbf_free[wb_i]))
                v_w = p.step("vector", lambda e, si=si, pi=pi, wb_i=wb_i: e.tensor_tensor(
                    out=wbf[wb_i][:], in0=hbf[pi][:], in1=crepA[:, si * TC:(si + 1) * TC],
                    op=ALU.mult), waits=w, inc="wv")
                s_cy = p.step("vector", lambda e, idx=idx, pi=pi: e.tensor_copy(
                    out=carry[:, 2 * idx:2 * idx + 2], in_=hbf[pi][:, TC - 2:TC]),
                    inc="wv")
                if si == 0:
                    g_acc = p.step("gpsimd", lambda e, wb_i=wb_i: e.tensor_copy(
                        out=yacc[:], in_=wbf[wb_i][:]), waits=[("wv", v_w)], inc="gp")
                else:
                    g_acc = p.step("gpsimd", lambda e, wb_i=wb_i: e.tensor_tensor(
                        out=yacc[:], in0=yacc[:], in1=wbf[wb_i][:], op=ALU.add),
                        waits=[("wv", v_w)], inc="gp")
                wbf_free[wb_i] = g_acc
                h_free[pi] = ("wv", s_cy)
            y2_last = p.step("vector", [
                lambda e, dh=dh: e.scalar_tensor_tensor(
                    out=ytmp2[:], in0=xcb[:, dh * TC:(dh + 1) * TC],
                    scalar=dcols[:, dh:dh + 1], in1=yacc[:], op0=ALU.mult, op1=ALU.add),
                lambda e, dh=dh, zb=zb: e.tensor_tensor(
                    out=xcb[:, dh * TC:(dh + 1) * TC], in0=ytmp2[:],
                    in1=zst[zb][:], op=ALU.mult),
            ], waits=[("lw", zld), ("gp", g_acc)], inc="wv")
            zld_free[zb] = y2_last

        res_done = None
        for ot in range(NFT):
            buf = 0
            bank = ot % 2
            w = [] if wobuf_free[buf] is None else [("wt", wobuf_free[buf])]
            lw_t = p.dma("gpsimd", wgo[buf][:], outw_img[ot], "lw", waits=w)
            w = [("lw", lw_t), ("wv", y2_last)]
            if bank_busy[bank] is not None:
                w.append(bank_busy[bank])
            mms = [lambda e, d=d, buf=buf, bank=bank: e.matmul(
                ps_mm[bank][:], wgo[buf][:, d * 128:(d + 1) * 128],
                xcb[:, d * TC:(d + 1) * TC], start=(d == 0), stop=(d == NDT - 1))
                for d in range(NDT)]
            mm = p.step("tensor", mms, waits=w, inc="wt")
            wobuf_free[buf] = mm
            res_done = p.step("vector", lambda e, ot=ot, bank=bank: e.tensor_tensor(
                out=xf[:, ot * TC:(ot + 1) * TC], in0=xf[:, ot * TC:(ot + 1) * TC],
                in1=ps_mm[bank][:], op=ALU.add), waits=[("wt", mm)], inc="wv")
            bank_busy[bank] = ("wv", res_done)

        def out_cb(ft, buf, v_tgt, probe_wait):
            if probe_wait:
                d = out_dma[ft % 2]
                return [("st", d)] if d is not None else []
            out_dma[ft % 2] = p.dma("sync", x3[ft * 128:(ft + 1) * 128, t0:t0 + TC],
                                    buf[:], "st", waits=[("wv", v_tgt)])
            return None

        xf_free = emit_ln_chunk(p, xf, None, ps_bc[1], ps_bc[2], rowm, h_tmp, w_tmp,
                                out_cb, onesc, ones1, fconst,
                                wt_after=[("wv", res_done)], tag=f"l3ln{c}")
        bc1_free = xf_free
        bc2_free = xf_free

    return


def run_launch3(x2T_list, inputs, n_cores=8):
    c = host_consts()
    conv_w = inputs["L1_attn_conv_w"].astype(np.float32)
    cc = conv_w[:, 0, :].reshape(NDT, 128, 4).transpose(1, 0, 2).reshape(128, NDT * 4)
    cb = inputs["L1_attn_conv_b"].astype(np.float32).reshape(NDT, 128).T
    W_x = inputs["L1_attn_W_x"].astype(np.float32)
    wxb = W_x[0:DST].T.reshape(NDT, 128, DST).transpose(1, 0, 2).reshape(128, NDT * DST)
    wxc = W_x[DST:2 * DST].T.reshape(NDT, 128, DST).transpose(1, 0, 2).reshape(128, NDT * DST)
    wxdt = W_x[2 * DST].reshape(NDT, 128).T
    a2 = inputs["L1_attn_A_log"].astype(np.float32).reshape(NDT, 128, DST).transpose(1, 0, 2).reshape(128, NDT * DST)
    dtb = inputs["L1_attn_dt_bias"].astype(np.float32).reshape(NDT, 128).T
    dc = inputs["L1_attn_D"].astype(np.float32).reshape(NDT, 128).T
    base = {
        "win_img": tile_img(inputs["L1_attn_W_in"].T.astype(np.float32)),
        "outw_img": tile_img(inputs["L1_attn_out_W"].T.astype(np.float32)),
        "conv_cols": np.ascontiguousarray(cc),
        "convb_cols": np.ascontiguousarray(cb),
        "wxb_cols": np.ascontiguousarray(wxb),
        "wxc_cols": np.ascontiguousarray(wxc),
        "wxdt_cols": np.ascontiguousarray(wxdt),
        "a2_cols": np.ascontiguousarray(a2),
        "dtb_cols": np.ascontiguousarray(dtb),
        "d_cols": np.ascontiguousarray(dc),
        "ones_st": c["ones_st"], "ones_st_bf": c["ones_st_bf"], "ones_col": c["ones_col"],
        "fconst": c["fconst"], "sel16": c["sel16"],
    }
    nc = build_single(3, "x3")
    in_maps = []
    for xT in x2T_list:
        m = dict(base)
        m["xT"] = xT
        in_maps.append(m)
    res = run_bass_kernel_spmd(nc, in_maps, core_ids=list(range(n_cores)))
    if DEBUG_DUMP:
        return [np.asarray(r["x3"], dtype=np.float32) for r in res.results], res.results
    return [np.asarray(r["x3"], dtype=np.float32) for r in res.results]


NE = 8
TQ = 512
NQ = T // TQ
NSEG = TQ // TC


def emit_phase4(p):
    nc = p.nc
    st0 = p.ctr.get("st", 0)
    xT = p.io["x_in"]
    x4 = p.io["x_out"]
    rw_img = p.dram("rw_img", [NE, NFT, 128, DM], F32, "ExternalInput")
    ew_img = p.dram("ew_img", [NE, NFT, 128, DM], F32, "ExternalInput")
    rb_cols_d = p.dram("rb_cols", [128, NE * NFT], F32, "ExternalInput")
    ebT_d = p.dram("ebT", [NE, DM], F32, "ExternalInput")
    ones_st = p.dram("ones_st", [1, 128], F32, "ExternalInput")
    ones_stb_d = p.dram("ones_st_bf", [1, 128], BF16, "ExternalInput")
    ones_col = p.dram("ones_col", [128, 1], F32, "ExternalInput")
    ones_colb_d = p.dram("ones_col_bf", [128, 1], BF16, "ExternalInput")
    fconst_d = p.dram("fconst", [1, 8], F32, "ExternalInput")
    cbounce = p.dram("c_bounce", [1, NE * TQ], BF16, "Internal")

    for s in ("ld", "lw", "wv", "ws", "wt", "st"):
        p.sem(s)

    ones1 = p.sb("ones1", [1, 128], F32)
    ones1b = p.sb("ones1b", [1, 128], BF16)
    onesc = p.sb("onesc", [128, 1], F32)
    onescb = p.sb("onescb", [128, 1], BF16)
    fconst = p.sb("fconst_sb", [1, 8], F32)
    rb_cols = p.sb("rb_cols_sb", [128, NE * NFT], F32)
    ebT = p.sb("ebT_sb", [NE, DM], BF16)

    x3b = p.sb("x3b", [128, NFT * TQ], BF16)
    finalq = p.sb("finalq", [128, NFT * TQ], F32)
    planes = p.sb("planes", [128, 2 * NE * NSEG * TC], BF16)
    crep = p.sb("crep", [128, NE * NSEG * TC], BF16)
    c_sb = p.sb("c_sb", [1, NE * TQ], BF16)
    c_tile = p.sb("c_tile", [NE, TQ], BF16)
    xf = p.sb("xf", [128, NFT * TC], F32)
    m1 = p.sb("m1", [128, TC], F32)
    m2 = p.sb("m2", [128, TC], F32)
    s12 = p.sb("s12", [128, TC], F32)
    tmpd = p.sb("tmpd", [128, TC], F32)
    tmps = p.sb("tmps", [128, TC], F32)
    tmp1 = p.sb("tmp1", [128, TC], F32)
    tmp2 = p.sb("tmp2", [128, TC], F32)
    rows = p.sb("rows", [1, 3 * TC], F32)
    wg = [p.sb(f"wg{i}", [128, DM], BF16) for i in range(2)]

    ps_mm = [p.ps(f"ps_mm{i}", [128, TC]) for i in range(2)]
    ps_row = p.ps("ps_row", [1, TC])
    ps_bc = [p.ps(f"ps_bc{i}", [128, TC]) for i in range(2)]

    for dst, src in [(ones1[:], ones_st), (ones1b[:], ones_stb_d), (onesc[:], ones_col),
                     (onescb[:], ones_colb_d),
                     (fconst[:], fconst_d), (rb_cols[:], rb_cols_d)]:
        p.dma("sync", dst, src, "ld")
    p.dma("gpsimd", ebT[:], ebT_d, "ld")
    const_ld = p.ctr["ld"]

    wbuf_free = {0: None, 1: None}
    bank_free = {0: None, 1: None}
    row_free = None
    x3b_free = None
    out_dma = {0: None, 1: None}
    xf_free = None
    bc_free = [None, None]

    for q in range(NQ):
        q0 = q * TQ
        if q == 0:
            w = [("ld", const_ld)]
            if p.ctr["st"] > 0:
                w.append(("st", p.ctr["st"]))
        else:
            w = [("wt", x3b_free)]
        ldq = None
        for ft in range(NFT):
            ldq = p.dma("gpsimd", x3b[:, ft * TQ:(ft + 1) * TQ],
                        xT[ft * 128:(ft + 1) * 128, q0:q0 + TQ], "lw",
                        waits=(w if ft == 0 else ()))
        gv_last = None
        pending = None

        def emit_top2(ot, pbuf, q=q):
            nonlocal gv_last

            def pl(ei, pbuf=pbuf):
                return planes[:, (pbuf * NE + ei) * TC:(pbuf * NE + ei + 1) * TC]
            fns = [
                lambda e, pl=pl: e.tensor_tensor(out=m1[:], in0=pl(0), in1=pl(1), op=ALU.max),
                lambda e, pl=pl: e.tensor_tensor(out=tmpd[:], in0=pl(2), in1=pl(3), op=ALU.max),
                lambda e: e.tensor_tensor(out=m1[:], in0=m1[:], in1=tmpd[:], op=ALU.max),
                lambda e, pl=pl: e.tensor_tensor(out=tmpd[:], in0=pl(4), in1=pl(5), op=ALU.max),
                lambda e: e.tensor_tensor(out=m1[:], in0=m1[:], in1=tmpd[:], op=ALU.max),
                lambda e, pl=pl: e.tensor_tensor(out=tmpd[:], in0=pl(6), in1=pl(7), op=ALU.max),
                lambda e: e.tensor_tensor(out=m1[:], in0=m1[:], in1=tmpd[:], op=ALU.max),
            ]
            for ei in range(NE):
                fns.append(lambda e, ei=ei, pl=pl: e.tensor_tensor(
                    out=tmpd[:], in0=pl(ei), in1=m1[:], op=ALU.is_ge))
                fns.append(lambda e, ei=ei, pl=pl: e.scalar_tensor_tensor(
                    out=tmpd[:], in0=tmpd[:], scalar=-1e30, in1=pl(ei),
                    op0=ALU.mult, op1=ALU.add))
                if ei == 0:
                    fns.append(lambda e: e.tensor_copy(out=m2[:], in_=tmpd[:]))
                else:
                    fns.append(lambda e: e.tensor_tensor(out=m2[:], in0=m2[:],
                                                         in1=tmpd[:], op=ALU.max))
            fns.append(lambda e: e.tensor_tensor(out=s12[:], in0=m1[:], in1=m2[:], op=ALU.add))
            fns.append(lambda e: e.tensor_tensor(out=m2[:], in0=m2[:], in1=m1[:], op=ALU.subtract))
            for ei in range(NE):
                fns.append(lambda e, ei=ei, pl=pl: e.scalar_tensor_tensor(
                    out=pl(ei), in0=pl(ei), scalar=2.0, in1=s12[:],
                    op0=ALU.mult, op1=ALU.subtract))
                fns.append(lambda e, ei=ei, pl=pl: e.tensor_tensor(
                    out=tmpd[:], in0=pl(ei), in1=m2[:], op=ALU.is_lt))
                fns.append(lambda e, ei=ei, pl=pl: e.scalar_tensor_tensor(
                    out=pl(ei), in0=tmpd[:], scalar=-1e30, in1=pl(ei),
                    op0=ALU.mult, op1=ALU.add))
            v_pre = p.step("vector", fns, inc="wv")
            s_g = p.step("scalar", lambda e, pbuf=pbuf: e.activation(
                out=planes[:, pbuf * NE * TC:(pbuf + 1) * NE * TC],
                in_=planes[:, pbuf * NE * TC:(pbuf + 1) * NE * TC], func=AF.Sigmoid),
                waits=[("wv", v_pre)], inc="ws")
            afns = []
            for ei in range(NE):
                gslice = crep[:, ei * TC:(ei + 1) * TC]
                if ot == 0:
                    afns.append(lambda e, ei=ei, pl=pl, gslice=gslice: e.tensor_copy(
                        out=gslice, in_=pl(ei)))
                else:
                    afns.append(lambda e, ei=ei, pl=pl, gslice=gslice: e.tensor_tensor(
                        out=gslice, in0=gslice, in1=pl(ei), op=ALU.add))
            w = [("ws", s_g)]
            if ot == 0 and bc_free[0] is not None:
                w.append(bc_free[0])
            gv_last = p.step("vector", afns, waits=w, inc="wv")

        plane_evac = {0: None, 1: None}
        for ot in range(NFT):
            pbuf = ot % 2
            for ei in range(NE):
                buf = (ot * NE + ei) % 2
                w = [] if wbuf_free[buf] is None else [("wt", wbuf_free[buf])]
                lw_t = p.dma("gpsimd", wg[buf][:], rw_img[ei, ot], "lw", waits=w)
                for ts_ in range(NSEG):
                    bank = ei % 2
                    w = [("lw", max(lw_t, ldq))]
                    if bank_free[bank] is not None:
                        w.append(bank_free[bank])
                    mm = p.step("tensor", [lambda e, ci=ci, buf=buf, bank=bank, ts_=ts_: e.matmul(
                        ps_mm[bank][:], wg[buf][:, ci * 128:(ci + 1) * 128],
                        x3b[:, ci * TQ + ts_ * TC:ci * TQ + (ts_ + 1) * TC],
                        start=(ci == 0), stop=(ci == NFT - 1))
                        for ci in range(NFT)], waits=w, inc="wt")
                    if ts_ == NSEG - 1:
                        wbuf_free[buf] = mm
                    w = [("wt", mm)]
                    if plane_evac[pbuf] is not None and ei == 0:
                        w.append(("wv", plane_evac[pbuf]))
                    ev = p.step("vector", lambda e, ei=ei, pbuf=pbuf, bank=bank, ot=ot: e.tensor_scalar(
                        out=planes[:, (pbuf * NE + ei) * TC:(pbuf * NE + ei + 1) * TC],
                        in0=ps_mm[bank][:], scalar1=rb_cols[:, ei * NFT + ot:ei * NFT + ot + 1],
                        scalar2=None, op0=ALU.add), waits=w, inc="wv")
                    bank_free[bank] = ("wv", ev)
            if pending is not None:
                pending()
            ot_c, pb_c = ot, pbuf
            pending = (lambda ot_c=ot_c, pb_c=pb_c: (emit_top2(ot_c, pb_c),
                                                     plane_evac.__setitem__(pb_c, gv_last)))
        if pending is not None:
            pending()

        c_add = None
        for ei in range(NE):
            for ts_ in range(NSEG):
                w = [("wv", gv_last)] if (ei == 0 and ts_ == 0) else []
                if row_free is not None:
                    w.append(("wv", row_free))
                t_cs = p.step("tensor", lambda e, ei=ei, ts_=ts_: e.matmul(
                    ps_row[:], onescb[:],
                    crep[:, (ei * NSEG + ts_) * TC:(ei * NSEG + ts_ + 1) * TC],
                    start=True, stop=True), waits=w, inc="wt")
                c_add = p.step("vector", lambda e, ei=ei, ts_=ts_: e.tensor_scalar(
                    out=c_sb[0:1, ei * TQ + ts_ * TC:ei * TQ + (ts_ + 1) * TC],
                    in0=ps_row[:], scalar1=1.0, scalar2=None, op0=ALU.mult),
                    waits=[("wt", t_cs)], inc="wv")
                row_free = c_add
        st_c = p.dma("sync", cbounce, c_sb[0:1, :], "lw", waits=[("wv", c_add)])
        ld_ct = p.dma("gpsimd", c_tile[:], cbounce.rearrange("c (a b) -> (c a) b", b=TQ),
                      "lw", waits=[("lw", st_c)])
        crep_done = None
        for ei in range(NE):
            for ts_ in range(NSEG):
                w = [("wv", c_add), ("wt", p.ctr["wt"])] if (ei == 0 and ts_ == 0) else []
                if bc_free[0] is not None:
                    w.append(bc_free[0])
                t_b = p.step("tensor", lambda e, ei=ei, ts_=ts_: e.matmul(
                    ps_bc[0][:], ones1b[:],
                    c_sb[0:1, ei * TQ + ts_ * TC:ei * TQ + (ts_ + 1) * TC],
                    start=True, stop=True), waits=w, inc="wt")
                crep_done = p.step("vector", lambda e, ei=ei, ts_=ts_: e.tensor_copy(
                    out=crep[:, (ei * NSEG + ts_) * TC:(ei * NSEG + ts_ + 1) * TC],
                    in_=ps_bc[0][:]), waits=[("wt", t_b)], inc="wv")
                bc_free[0] = ("wv", crep_done)

        ex_last_mm = None
        fin_last = None
        for ot in range(NFT):
            for ei in range(NE):
                buf = (ot * NE + ei) % 2
                w = [] if wbuf_free[buf] is None else [("wt", wbuf_free[buf])]
                lw_t = p.dma("gpsimd", wg[buf][:], ew_img[ei, ot], "lw", waits=w)
                for ts_ in range(NSEG):
                    bank = ei % 2
                    w = [("lw", lw_t), ("wv", crep_done), ("wv", gv_last)]
                    if bank_free[bank] is not None:
                        w.append(bank_free[bank])
                    mm = p.step("tensor", [lambda e, ci=ci, buf=buf, bank=bank, ts_=ts_: e.matmul(
                        ps_mm[bank][:], wg[buf][:, ci * 128:(ci + 1) * 128],
                        x3b[:, ci * TQ + ts_ * TC:ci * TQ + (ts_ + 1) * TC],
                        start=(ci == 0), stop=(ci == NFT - 1))
                        for ci in range(NFT)], waits=w, inc="wt")
                    if ts_ == NSEG - 1:
                        wbuf_free[buf] = mm
                    ex_last_mm = mm
                    fslice = finalq[:, ot * TQ + ts_ * TC:ot * TQ + (ts_ + 1) * TC]
                    cslice = crep[:, (ei * NSEG + ts_) * TC:(ei * NSEG + ts_ + 1) * TC]
                    if ei == 0:
                        fin = p.step("vector", lambda e, fslice=fslice, cslice=cslice, bank=bank:
                                     e.tensor_tensor(out=fslice, in0=ps_mm[bank][:], in1=cslice,
                                                     op=ALU.mult), waits=[("wt", mm)], inc="wv")
                    else:
                        fin = p.step("vector", [
                            lambda e, cslice=cslice, bank=bank: e.tensor_tensor(
                                out=tmp2[:], in0=ps_mm[bank][:], in1=cslice, op=ALU.mult),
                            lambda e, fslice=fslice: e.tensor_tensor(
                                out=fslice, in0=fslice, in1=tmp2[:], op=ALU.add),
                        ], waits=[("wt", mm)], inc="wv")
                    bank_free[bank] = ("wv", fin)
                    fin_last = fin
            for ts_ in range(NSEG):
                w = [("lw", ld_ct)]
                if bc_free[1] is not None:
                    w.append(bc_free[1])
                t_bias = p.step("tensor", lambda e, ot=ot, ts_=ts_: e.matmul(
                    ps_bc[1][:], ebT[:, ot * 128:(ot + 1) * 128],
                    c_tile[:, ts_ * TC:(ts_ + 1) * TC], start=True, stop=True),
                    waits=w, inc="wt")
                fin_last = p.step("vector", lambda e, ot=ot, ts_=ts_: e.tensor_tensor(
                    out=finalq[:, ot * TQ + ts_ * TC:ot * TQ + (ts_ + 1) * TC],
                    in0=finalq[:, ot * TQ + ts_ * TC:ot * TQ + (ts_ + 1) * TC],
                    in1=ps_bc[1][:], op=ALU.add), waits=[("wt", t_bias)], inc="wv")
                bc_free[1] = ("wv", fin_last)
        x3b_free = ex_last_mm

        for ts_ in range(NSEG):
            t0 = q0 + ts_ * TC
            if xf_free is not None:
                w = [("wv", xf_free)]
            else:
                w = [("st", st0)] if st0 > 0 else []
            ldx = None
            for ft in range(NFT):
                ldx = p.dma("sync", xf[:, ft * TC:(ft + 1) * TC],
                            xT[ft * 128:(ft + 1) * 128, t0:t0 + TC], "ld",
                            waits=(w if ft == 0 else ()))
            res_done = None
            for ft in range(NFT):
                res_done = p.step("vector", lambda e, ft=ft, ts_=ts_: e.tensor_tensor(
                    out=xf[:, ft * TC:(ft + 1) * TC], in0=xf[:, ft * TC:(ft + 1) * TC],
                    in1=finalq[:, ft * TQ + ts_ * TC:ft * TQ + (ts_ + 1) * TC], op=ALU.add),
                    waits=([("ld", ldx), ("wv", fin_last)] if ft == 0 else ()), inc="wv")

            def out_cb(ft, buf, v_tgt, probe_wait, t0=t0):
                if probe_wait:
                    d = out_dma[ft % 2]
                    return [("st", d)] if d is not None else []
                out_dma[ft % 2] = p.dma("sync", x4[ft * 128:(ft + 1) * 128, t0:t0 + TC],
                                        buf[:], "st", waits=[("wv", v_tgt)])
                return None

            xf_free = emit_ln_chunk(p, xf, None, ps_bc[0], ps_bc[1], rows, tmp1, tmp2,
                                    out_cb, onesc, ones1, fconst,
                                    wt_after=[("wv", res_done)], tag=f"l4ln{q}_{ts_}")
            bc_free = [("wv", xf_free), ("wv", xf_free)]

    return


def run_launch4(x3T_list, inputs, n_cores=8):
    c = host_consts()
    rw = inputs["L1_ffn_router_w"].astype(np.float32)
    rw_em = rw.reshape(DM, NE, DM).transpose(1, 0, 2)
    rw_img = np.stack([tile_img(np.ascontiguousarray(rw_em[e].T)) for e in range(NE)])
    ew = inputs["L1_ffn_expert_w"].astype(np.float32)
    ew_img = np.stack([tile_img(np.ascontiguousarray(ew[e].T)) for e in range(NE)])
    rb = inputs["L1_ffn_router_b"].astype(np.float32)
    rb_cols = rb.reshape(NFT, 128, NE).transpose(1, 2, 0).reshape(128, NE * NFT)
    base = {
        "rw_img": rw_img, "ew_img": ew_img,
        "rb_cols": np.ascontiguousarray(rb_cols),
        "ebT": inputs["L1_ffn_expert_b"].astype(np.float32),
        "ones_st": c["ones_st"], "ones_st_bf": c["ones_st_bf"], "ones_col": c["ones_col"],
        "ones_col_bf": c["ones_col_bf"], "fconst": c["fconst"],
    }
    nc = build_single(4, "x4")
    in_maps = []
    for xT in x3T_list:
        m = dict(base)
        m["xT"] = xT
        in_maps.append(m)
    res = run_bass_kernel_spmd(nc, in_maps, core_ids=list(range(n_cores)))
    return [np.asarray(r["x4"], dtype=np.float32) for r in res.results]


def _prep_l1(inputs, consts):
    base = dict(consts_sub(consts, ["rope_cos", "rope_sin", "rot_st", "m_low", "m_up",
                                    "ident_bf", "ones_st", "ones_col", "ones_col_bf", "fconst"]))
    base["wqkv_img"] = tile_img(inputs["L0_attn_w_qkv"].T.astype(np.float32))
    base["wproj_img"] = tile_img(inputs["L0_attn_w_proj"].T.astype(np.float32))
    return base


def consts_sub(c, names):
    m = {"ident_bf": "ident_bf", "m_low": "m_low", "m_up": "m_up"}
    out = {}
    for n in names:
        out[n] = c[n]
    return out


PHASE_EMITTERS = {1: None, 2: None, 3: None, 4: None}


def _mk_prog():
    nc = bass.Bass("TRN2", target_bir_lowering=False)
    ctx = contextlib.ExitStack()
    return Prog(nc, ctx)


def _finish(p):
    p.step("gpsimd", lambda e: e.nop(), waits=[("st", p.ctr["st"])])
    p.run()
    return p.nc


def build_single(phase_idx, out_name):
    p = _mk_prog()
    p.io["x_in"] = p.dram("xT", [DM, T], F32, "ExternalInput")
    p.io["x_out"] = p.dram(out_name, [DM, T], F32, "ExternalOutput")
    PHASE_EMITTERS[phase_idx](p)
    return _finish(p)


def fused_aliases(p):
    a = p.alias
    xf = p.ctx.enter_context(p.nc.sbuf_tensor("A_xf", [128, 8192], F32))
    xb = p.ctx.enter_context(p.nc.sbuf_tensor("A_xb", [128, 8192], BF16))
    bfA = p.ctx.enter_context(p.nc.sbuf_tensor("A_bfA", [128, 32768], BF16))
    fC = p.ctx.enter_context(p.nc.sbuf_tensor("A_fC", [128, 4096], F32))
    bfB = p.ctx.enter_context(p.nc.sbuf_tensor("A_bfB", [128, 4096], BF16))
    f32big = p.ctx.enter_context(p.nc.sbuf_tensor("A_f32big", [128, 8192], F32))
    wstr = p.ctx.enter_context(p.nc.sbuf_tensor("A_wstr", [128, 16384], BF16))
    rows = p.ctx.enter_context(p.nc.sbuf_tensor("A_rows", [1, 1536], F32))
    K = [p.ctx.enter_context(p.nc.psum_tensor(f"K{i}", [128, 512], F32)) for i in range(7)]
    a["rows"] = rows[:]
    a["rowm"] = rows[:]
    a["xf"] = xf[:]
    a["xb"] = xb[:]
    a["cosT"] = f32big[:, 0:4096]
    a["sinT"] = f32big[:, 4096:8192]
    a["qn"] = bfA[:, 0:8192]
    a["ybuf"] = bfA[:, 8192:16384]
    a["kn_ring"] = bfA[:, 16384:20480]
    a["vt_ring"] = bfA[:, 20480:24576]
    a["Ebuf"] = bfA[:, 24576:25600]
    a["vbf"] = bfA[:, 25600:27648]
    a["kf"] = fC[:, 0:2048]
    a["qtile"] = fC[:, 2048:2560]
    a["tmp1"] = fC[:, 2560:3072]
    a["tmp2"] = fC[:, 3072:3584]
    a["ytmp"] = fC[:, 3584:4096]
    a["wgrp0"] = wstr[:, 0:2048]
    a["wgrp1"] = wstr[:, 2048:4096]
    a["ps_mm0"] = K[0][:]
    a["ps_mm1"] = K[1][:]
    a["ps_sc0"] = K[2][:]
    a["ps_sc1"] = K[3][:]
    a["ps_y"] = K[4][:]
    a["ps_bc0"] = K[5][:]
    a["ps_bc1"] = K[6][:]
    a["hidH"] = bfA[:, 0:32768]
    a["hidt"] = bfB[:, 0:512]
    a["tmpS"] = fC[:, 0:512]
    a["w1g0"] = wstr[:, 0:2048]
    a["w1g1"] = wstr[:, 2048:4096]
    a["w3g0"] = wstr[:, 4096:6144]
    a["w3g1"] = wstr[:, 6144:8192]
    a["w2g"] = wstr[:, 8192:16384]
    a["wbig0"] = bfA[:, 0:8192]
    a["wbig1"] = bfA[:, 8192:16384]
    a["ps_g0"] = K[0][:]
    a["ps_g1"] = K[1][:]
    a["ps_l0"] = K[2][:]
    a["ps_l1"] = K[3][:]
    a["ps_h"] = K[4][:]
    a["xcb"] = bfA[:, 0:16384]
    a["dtBrep"] = bfA[:, 16384:24576]
    a["crepA"] = bfA[:, 24576:32768]
    a["a2"] = f32big[:, 0:512]
    a["carry"] = f32big[:, 512:1536]
    a["conv_cols_sb"] = f32big[:, 1536:1664]
    a["convb_sb"] = f32big[:, 1664:1696]
    a["dtb"] = f32big[:, 1696:1728]
    a["dcols"] = f32big[:, 1728:1760]
    a["mb"] = f32big[0:1, 1760:1761]
    a["abar0"] = fC[:, 2048:2560]
    a["abar1"] = f32big[:, 1792:2304]
    a["u_sb0"] = bfB[:, 0:512]
    a["u_sb1"] = bfB[:, 512:1024]
    a["hbf0"] = bfB[:, 1024:1536]
    a["hbf1"] = bfB[:, 1536:2048]
    a["wbf0"] = bfB[:, 2048:2560]
    a["zst0"] = bfB[:, 2560:3072]
    a["zst1"] = bfB[:, 3072:3584]
    a["wbf1"] = bfB[:, 3584:4096]
    a["hist"] = fC[:, 3600:3696]
    a["wxb"] = wstr[:, 3584:4096]
    a["wxc"] = wstr[:, 10752:11264]
    a["wxdt"] = wstr[:, 11264:11296]
    a["ident"] = wstr[:, 11296:11424]
    a["ones1b"] = wstr[0:1, 11424:11552]
    a["onescb"] = wstr[0:128, 11552:11553]
    a["B_sb"] = wstr[0:16, 2048:2560]
    a["C_sb"] = wstr[0:16, 2560:3072]
    a["dtB_sb"] = wstr[0:16, 3072:3584]
    a["sel16_sb"] = wstr[0:16, 8192:10240]
    a["h_tmp"] = fC[:, 0:512]
    a["w_tmp"] = fC[:, 512:1024]
    a["yacc"] = fC[:, 1024:1536]
    a["ytmp2"] = fC[:, 1536:2048]
    a["convacc"] = fC[:, 2560:3072]
    a["xs_tmp"] = fC[:, 3072:3588]
    a["wgi0"] = wstr[:, 0:2048]
    a["wgo0"] = wstr[:, 4096:8192]
    a["ps_B"] = K[2][0:16, :]
    a["ps_C"] = K[2][32:48, :]
    a["ps_dt"] = K[2][64:65, :]
    a["ps_bc2"] = K[4][:]
    a["x3b"] = xb[:]
    a["finalq"] = f32big[:, 0:8192]
    a["planes"] = bfA[:, 0:8192]
    a["crep"] = bfA[:, 8192:12288]
    a["m1"] = fC[:, 0:512]
    a["m2"] = fC[:, 512:1024]
    a["s12"] = fC[:, 1024:1536]
    a["tmpd"] = fC[:, 1536:2048]
    a["tmps"] = fC[:, 2048:2560]
    a["wg0"] = wstr[:, 0:2048]
    a["wg1"] = wstr[:, 2048:4096]
    a["pc_stage0"] = wstr[:, 0:4096]
    a["pc_stage1"] = wstr[:, 4096:8192]
    a["ebT_sb"] = wstr[0:8, 4096:6144]
    a["rb_cols_sb"] = fC[:, 3696:3824]
    a["c_sb"] = wstr[0:1, 6144:10240]
    a["c_tile"] = wstr[0:8, 10240:10752]
    a["ps_row"] = K[2][0:1, :]


def emit_precast(p, specs):
    for sem in ("lw", "st", "wv"):
        p.sem(sem)
    stg = [p.sb("wstr_view0", [128, 2048], BF16) if False else None for _ in range(2)]
    wstr0 = p.alias.get("wgrp0")
    twins = {}
    st_prev = {0: None, 1: None}
    for name, n_tiles, cols in specs:
        src = p.dram_map[name]
        tw = p.dram(name + "_bf", [n_tiles, 128, cols], BF16, "Internal")
        twins[name] = tw
        for i in range(n_tiles):
            b = i % 2
            buf = p.alias["pc_stage%d" % b]
            w = [("st", st_prev[b])] if st_prev[b] is not None else []
            lw = p.dma("gpsimd", buf[:, 0:cols], src[i], "lw", waits=w)
            st_prev[b] = p.dma("sync", tw[i], buf[:, 0:cols], "st", waits=[("lw", lw)])
    return twins


def build_fused():
    p = _mk_prog()
    fused_aliases(p)
    xin = p.dram("xT", [DM, T], F32, "ExternalInput")
    x1s = p.dram("x1s", [DM, T], F32, "Internal")
    x2s = p.dram("x2s", [DM, T], F32, "Internal")
    x3s = p.dram("x3s", [DM, T], F32, "Internal")
    x4o = p.dram("x4", [DM, T], F32, "ExternalOutput")
    p.dram("wqkv_img", [24, 128, DM], F32, "ExternalInput")
    p.dram("wproj_img", [NFT, 128, DM], F32, "ExternalInput")
    p.dram("w1_img", [NHT, 128, DM], F32, "ExternalInput")
    p.dram("w3_img", [NHT, 128, DM], F32, "ExternalInput")
    p.dram("win_img", [64, 128, DM], F32, "ExternalInput")
    p.dram("outw_img", [NFT, 128, DIN], F32, "ExternalInput")
    rw4 = p.dram("rw_img", [NE, NFT, 128, DM], F32, "ExternalInput")
    ew4 = p.dram("ew_img", [NE, NFT, 128, DM], F32, "ExternalInput")
    p.dram_map["rw_flat"] = rw4.rearrange("a b p n -> (a b) p n")
    p.dram_map["ew_flat"] = ew4.rearrange("a b p n -> (a b) p n")
    specs = [("wqkv_img", 24, DM), ("wproj_img", NFT, DM), ("w1_img", NHT, DM),
             ("w3_img", NHT, DM), ("win_img", 64, DM), ("outw_img", NFT, DIN),
             ("rw_flat", NE * NFT, DM), ("ew_flat", NE * NFT, DM)]
    twins = emit_precast(p, specs)
    p.step("gpsimd", lambda e: e.nop(), waits=[("st", p.ctr["st"])])
    p.barrier()
    for name in ("wqkv_img", "wproj_img", "w1_img", "w3_img", "win_img", "outw_img"):
        p.dram_map[name] = twins[name]
    p.dram_map["rw_img"] = twins["rw_flat"].rearrange("(a b) p n -> a b p n", a=NE)
    p.dram_map["ew_img"] = twins["ew_flat"].rearrange("(a b) p n -> a b p n", a=NE)
    chain = [(xin, x1s), (x1s, x2s), (x2s, x3s), (x3s, x4o)]
    for i, (src, dst) in enumerate(chain):
        p.io["x_in"] = src
        p.io["x_out"] = dst
        PHASE_EMITTERS[i + 1](p)
        if i < 3:
            p.barrier()
    return _finish(p)


def run_launch1(x_full, inputs, n_cores=8):
    outs = run_launch1_fm(x_full, inputs, n_cores)
    if DEBUG_DUMP:
        outs, results = outs
        return np.stack([o.T for o in outs], axis=0), results
    return np.stack([o.T for o in outs], axis=0)


def run_launch1_fm(x_full, inputs, n_cores=8):
    c = host_consts()
    base = _prep_l1(inputs, c)
    nc = build_single(1, "x1")
    in_maps = []
    for b in range(x_full.shape[0]):
        m = dict(base)
        m["xT"] = np.ascontiguousarray(x_full[b].T).astype(np.float32)
        in_maps.append(m)
    res = run_bass_kernel_spmd(nc, in_maps, core_ids=list(range(n_cores)))
    outs = [np.asarray(r["x1"], dtype=np.float32) for r in res.results]
    if DEBUG_DUMP:
        return outs, res.results
    return outs


PHASE_EMITTERS[1] = emit_phase1
PHASE_EMITTERS[2] = emit_phase2
PHASE_EMITTERS[3] = emit_phase3
PHASE_EMITTERS[4] = emit_phase4


def host_all_inputs(inputs):
    c = host_consts()
    base = _prep_l1(inputs, c)
    h = np.zeros((128, 128), np.float64)
    for i in range(128):
        for j in range(128):
            h[i, j] = 1.0 if bin(i & j).count("1") % 2 == 0 else -1.0
    base.update({
        "w1_img": tile_img(inputs["L0_ffn_W1"].T.astype(np.float32)),
        "w3_img": tile_img(inputs["L0_ffn_W3"].T.astype(np.float32)),
        "w2_img": tile_img(inputs["L0_ffn_W2"].T.astype(np.float32)),
        "h128_bf": h.astype(BF),
    })
    conv_w = inputs["L1_attn_conv_w"].astype(np.float32)
    cc = conv_w[:, 0, :].reshape(NDT, 128, 4).transpose(1, 0, 2).reshape(128, NDT * 4)
    cb = inputs["L1_attn_conv_b"].astype(np.float32).reshape(NDT, 128).T
    W_x = inputs["L1_attn_W_x"].astype(np.float32)
    base.update({
        "win_img": tile_img(inputs["L1_attn_W_in"].T.astype(np.float32)),
        "outw_img": tile_img(inputs["L1_attn_out_W"].T.astype(np.float32)),
        "conv_cols": np.ascontiguousarray(cc),
        "convb_cols": np.ascontiguousarray(cb),
        "wxb_cols": np.ascontiguousarray(W_x[0:DST].T.reshape(NDT, 128, DST).transpose(1, 0, 2).reshape(128, NDT * DST)),
        "wxc_cols": np.ascontiguousarray(W_x[DST:2 * DST].T.reshape(NDT, 128, DST).transpose(1, 0, 2).reshape(128, NDT * DST)),
        "wxdt_cols": np.ascontiguousarray(W_x[2 * DST].reshape(NDT, 128).T),
        "a2_cols": np.ascontiguousarray(inputs["L1_attn_A_log"].astype(np.float32).reshape(NDT, 128, DST).transpose(1, 0, 2).reshape(128, NDT * DST)),
        "dtb_cols": np.ascontiguousarray(inputs["L1_attn_dt_bias"].astype(np.float32).reshape(NDT, 128).T),
        "d_cols": np.ascontiguousarray(inputs["L1_attn_D"].astype(np.float32).reshape(NDT, 128).T),
        "ones_st_bf": c["ones_st_bf"],
        "sel16": c["sel16"],
    })
    rw = inputs["L1_ffn_router_w"].astype(np.float32)
    rw_em = rw.reshape(DM, NE, DM).transpose(1, 0, 2)
    rb = inputs["L1_ffn_router_b"].astype(np.float32)
    base.update({
        "rw_img": np.stack([tile_img(np.ascontiguousarray(rw_em[e].T)) for e in range(NE)]),
        "ew_img": np.stack([tile_img(np.ascontiguousarray(inputs["L1_ffn_expert_w"].astype(np.float32)[e].T)) for e in range(NE)]),
        "rb_cols": np.ascontiguousarray(rb.reshape(NFT, 128, NE).transpose(1, 2, 0).reshape(128, NE * NFT)),
        "ebT": inputs["L1_ffn_expert_b"].astype(np.float32),
    })
    return base


def kernel(**inputs):
    x = np.asarray(inputs["L0_attn_x"], dtype=np.float32)
    n_cores = x.shape[0]
    base = host_all_inputs(inputs)
    nc = build_fused()
    in_maps = []
    for b in range(n_cores):
        m = dict(base)
        m["xT"] = np.ascontiguousarray(x[b].T).astype(np.float32)
        in_maps.append(m)
    res = run_bass_kernel_spmd(nc, in_maps, core_ids=list(range(n_cores)))
    return np.stack([np.asarray(r["x4"], dtype=np.float32).T for r in res.results], axis=0)
